# version 08
# baseline (speedup 1.0000x reference)
import numpy as np
import ml_dtypes
import concourse.bass as bass
import concourse.mybir as mybir
from concourse.bass_utils import run_bass_kernel_spmd

F32 = mybir.dt.float32
BF16 = mybir.dt.bfloat16
I32 = mybir.dt.int32
AF = mybir.ActivationFunctionType
ALU = mybir.AluOpType
AX = mybir.AxisListType
BF = ml_dtypes.bfloat16

T = 4096
D = 1024
DI = 2048
DS = 64
NE = 8
DF = 4096
KSEL = 1024
NT = T // 128
NCH = 8
CH = T // NCH
CT = CH // 128
QK = 256
HD = 2048
G = 64
EPS = 1e-5
BISECT = 42

ENGS = ("sync", "scalar", "vector", "tensor", "gpsimd")
MOE_LIMIT = NE
MOE_PARTS = "all"

STAGE = 7


def ceil(a, b):
    return -(-a // b)


class Plan:
    def __init__(self, nc):
        self.nc = nc
        self.ops = {e: [] for e in ENGS}
        self.sems = {}
        self.cnt = {}

    def sem(self, name):
        if name not in self.sems:
            self.sems[name] = self.nc.alloc_semaphore(name)
            self.cnt[name] = 0
        return self.sems[name]

    def emit(self, eng, fn):
        self.ops[eng].append(fn)

    def inc(self, name, amt):
        self.sem(name)
        self.cnt[name] += amt
        return self.cnt[name]

    def wait(self, eng, name, val=None):
        h = self.sem(name)
        v = self.cnt[name] if val is None else val
        if v <= 0:
            return
        self.emit(eng, lambda e, h=h, v=v: e.wait_ge(h, v))

    def dma(self, eng, sem_name, out, in_, **kw):
        v = self.inc(sem_name, 16)
        h = self.sems[sem_name]
        self.emit(eng, lambda e, out=out, in_=in_, h=h, kw=dict(kw):
                  e.dma_start(out=out, in_=in_, **kw).then_inc(h, 16))
        return v

    def tdma(self, eng, sem_name, out, in_):
        v = self.inc(sem_name, 16)
        h = self.sems[sem_name]
        self.emit(eng, lambda e, out=out, in_=in_, h=h:
                  e.dma_start(out=out, in_=in_, transpose=True).then_inc(h, 16))
        return v

    def igather(self, sem_name, out, in_full, idx_ap):
        v = self.inc(sem_name, 16)
        h = self.sems[sem_name]
        self.emit("gpsimd", lambda e, out=out, i=in_full, x=idx_ap, h=h:
                  e.indirect_dma_start(out, None, i,
                                       bass.IndirectOffsetOnAxis(x, 0)).then_inc(h, 16))
        return v

    def iscat(self, sem_name, out_full, idx_ap, in_):
        v = self.inc(sem_name, 16)
        h = self.sems[sem_name]
        self.emit("gpsimd", lambda e, o=out_full, x=idx_ap, i=in_, h=h:
                  e.indirect_dma_start(o, bass.IndirectOffsetOnAxis(x, 0),
                                       i, None, compute_op=ALU.add).then_inc(h, 16))
        return v

    def op(self, eng, sem_name, fn):
        prev = self.cnt.get(sem_name, 0) if eng in ("vector", "scalar", "gpsimd") else 0
        v = self.inc(sem_name, 1)
        h = self.sems[sem_name]
        if prev > 0:
            self.emit(eng, lambda e, fn=fn, h=h, prev=prev:
                      fn(e)._wait_ge(h, prev).then_inc(h, 1))
        else:
            self.emit(eng, lambda e, fn=fn, h=h: fn(e).then_inc(h, 1))
        return v

    def opq(self, eng, fn):
        if eng == "vector":
            return self.op(eng, "VE", fn)
        if eng == "scalar":
            return self.op(eng, "SE", fn)
        self.emit(eng, lambda e, fn=fn: fn(e))

    def mm(self, out_ap, pairs, sem="TE", tile_position=None, first_start=True):
        n = len(pairs)
        v = self.inc(sem, 1)
        h = self.sems[sem]
        for j, (l, r) in enumerate(pairs):
            st = (j == 0) and first_start
            if j == n - 1:
                self.emit("tensor", lambda e, l=l, r=r, st=st, h=h, tp=tile_position:
                          e.matmul(out_ap, l, r, start=st, stop=True,
                                   tile_position=tp).then_inc(h, 1))
            else:
                self.emit("tensor", lambda e, l=l, r=r, st=st, tp=tile_position:
                          e.matmul(out_ap, l, r, start=st, stop=False,
                                   tile_position=tp))
        return v

    def tr(self, out_psum, in_ap, ident, sem="TE"):
        v = self.inc(sem, 1)
        h = self.sems[sem]
        self.emit("tensor", lambda e, o=out_psum, i=in_ap, d=ident, h=h:
                  e.transpose(o, i, d).then_inc(h, 1))
        return v


class Arena:

    def __init__(self, nc, name, n_f32):
        self.t = nc.alloc_sbuf_tensor(name, [128, n_f32], F32)
        self.free_list = [(0, n_f32)]
        self.allocs = {}
        self.hwm = 0

    def alloc(self, name, shape, dtype=F32):
        free = 1
        for s in shape[1:]:
            free *= s
        f32n = ceil(free, 2) if dtype == BF16 else free
        f32n = ceil(f32n, 16) * 16
        for i, (off, sz) in enumerate(self.free_list):
            if sz >= f32n:
                if sz == f32n:
                    self.free_list.pop(i)
                else:
                    self.free_list[i] = (off + f32n, sz - f32n)
                self.allocs[name] = (off, f32n)
                self.hwm = max(self.hwm, off + f32n)
                ap = self.t[0:128, off:off + f32n]
                if dtype != F32:
                    ap = ap.bitcast(dtype)
                ap = ap[0:shape[0], 0:free]
                if len(shape) == 3:
                    ap = ap.rearrange("p (a b) -> p a b", b=shape[2])
                elif len(shape) == 4:
                    ap = ap.rearrange("p (a b c) -> p a b c", b=shape[2], c=shape[3])
                return ap
        raise AssertionError(f"arena OOM for {name} ({f32n} f32/partition)")

    def view(self, name, shape, dtype=F32):
        off, f32n = self.allocs[name]
        free = 1
        for s in shape[1:]:
            free *= s
        ap = self.t[0:128, off:off + f32n]
        if dtype != F32:
            ap = ap.bitcast(dtype)
        ap = ap[0:shape[0], 0:free]
        if len(shape) == 3:
            ap = ap.rearrange("p (a b) -> p a b", b=shape[2])
        elif len(shape) == 4:
            ap = ap.rearrange("p (a b c) -> p a b c", b=shape[2], c=shape[3])
        return ap

    def free(self, name):
        off, sz = self.allocs.pop(name)
        self.free_list.append((off, sz))
        self.free_list.sort()
        merged = []
        for o, s in self.free_list:
            if merged and merged[-1][0] + merged[-1][1] == o:
                merged[-1] = (merged[-1][0], merged[-1][1] + s)
            else:
                merged.append((o, s))
        self.free_list = [tuple(x) for x in merged]


def build(debug=False):
    nc = bass.Bass("TRN2", target_bir_lowering=False)
    io = {}

    def din(name, shape, dtype=F32):
        io[name] = nc.dram_tensor(name, list(shape), dtype, kind="ExternalInput").ap()
        return io[name]

    def dint(name, shape, dtype=F32):
        kind = "ExternalOutput" if debug else "Internal"
        io[name] = nc.dram_tensor(name, list(shape), dtype, kind=kind).ap()
        return io[name]

    x_h = din("x", (T, D))
    out_h = nc.dram_tensor("out", [T, D], F32, kind="ExternalOutput").ap()
    io["out"] = out_h

    w_inT = din("w_inT", (D, DI))
    w_xdtT = din("w_xdtT", (DI, 128))
    d_mat = din("d_mat", (DS, DI))
    w_outT = din("w_outT", (DI, D))
    gate_wT = din("gate_wT", (D, NE))
    wgT_all = din("wgT_all", (NE, D, DF))
    wuT_all = din("wuT_all", (NE, D, DF))
    wdT_all = din("wdT_all", (NE, DF, D))
    wqkT = din("wqkT", (D, QK))
    wvT = din("wvT", (D, HD))
    wgT1 = din("wgT1", (D, HD))
    woT = din("woT", (HD, D))
    l1wT = din("l1wT", (D, D))
    cst_small = din("cst_small", (128, 64))
    cst_small2 = din("cst_small2", (128, 64))
    convw_h = din("convw_fm", (128, 64))
    cst64 = din("cst64", (64, 8))
    bc_l0 = din("bc_l0", (128, D))
    bc_ln1w = din("bc_ln1w", (128, D))
    bc_ln1b = din("bc_ln1b", (128, D))
    bc_ln2w = din("bc_ln2w", (128, D))
    bc_ln2b = din("bc_ln2b", (128, D))
    bc_bo = din("bc_bo", (128, D))
    bc_l1b = din("bc_l1b", (128, D))
    bc_bv = din("bc_bv", (128, HD))
    pos_bias_h = din("pos_bias", (128, 64))
    iden_b_h = din("iden_b", (128, 128), BF16)
    ones128_h = din("ones128b", (128, 128), BF16)
    ltones_h = din("ltonesb", (128, 128), BF16)
    cos_h = din("cos_fm", (128, T))
    sin_h = din("sin_fm", (128, T))

    sp_xn = dint("sp_xn", (T, D), BF16)
    sp_g = dint("sp_g", (T, D), F32)
    cmp_h = dint("cmp", (NE * T, 2), I32)
    sp_x1 = dint("sp_x1", (T, D), F32)
    sp_ln = dint("sp_ln", (T, D), BF16)
    sp_ln1 = dint("sp_ln1", (T, D), BF16)
    sp_v = dint("sp_v", (T, HD), BF16)
    sp_lk = dint("sp_lk", (QK, T), BF16)
    sp_qk = dint("sp_qk", (QK, T), BF16)

    p = Plan(nc)

    S = {}

    def sbt(name, shape, dtype=F32):
        S[name] = nc.alloc_sbuf_tensor(name, list(shape), dtype)
        return S[name]

    sbt("sb_iden_b", (128, 128), BF16)
    sbt("sb_cst_small", (128, 64))
    sbt("sb_cst_small2", (128, 64))
    sbt("convw", (128, 64))
    sbt("sb_cst64", (64, 8))
    sbt("stat", (128, 16))
    sbt("onesrow", (1, 128))
    sbt("zero2k", (128, 1024))
    sbt("aff_tok", (128, NT * NE))
    sbt("idxf_tok", (128, NT * NE))
    sbt("idxf_tok_i", (128, NT * NE), I32)
    sbt("sb_pos_bias", (128, 64))
    sbt("sb_ones128b", (128, 128), BF16)
    sbt("sb_ltonesb", (128, 128), BF16)

    AR = Arena(nc, "AR", 50700)

    PS = nc.alloc_psum_tensor("ps_main", [128, 6, 512], F32)
    PT = nc.alloc_psum_tensor("ps_tr", [128, 1024], BF16)
    PA = nc.alloc_psum_tensor("ps_aux", [128, 512], F32)

    b_in_c = lambda c: S["sb_cst_small"][:, c:c + 1]
    convb_c = lambda c: S["sb_cst_small"][:, 16 + c:17 + c]
    bqk_c = lambda c: S["sb_cst_small"][:, 32 + c:33 + c]
    bg_c = lambda c: S["sb_cst_small"][:, 34 + c:35 + c]
    gamma_c = lambda h, c: S["sb_cst_small"][:, 50 + 2 * h + c:51 + 2 * h + c]
    beta_c = lambda h, c: S["sb_cst_small2"][:, 2 * h + c:1 + 2 * h + c]
    alpha_c = S["sb_cst_small2"][:, 8:9]
    ones_c = S["sb_cst_small2"][:, 9:10]
    iota_tok = S["sb_cst_small2"][:, 10:42]
    convw_c = lambda c, k: S["convw"][:, 4 * c + k:4 * c + k + 1]
    b_x_col = S["sb_cst64"][0:64, 0:1]
    b_dt_col = S["sb_cst64"][0:64, 1:2]
    a_col = S["sb_cst64"][0:64, 3:4]
    nra_col = S["sb_cst64"][0:64, 4:5]

    def emit_ln(src_ap, dst_ap, n=D):
        vv = p.op("vector", "VE", lambda e: e.reduce_sum(S["stat"][:, 0:1], src_ap, AX.X))
        p.wait("scalar", "VE", vv)
        sv = p.op("scalar", "SE", lambda e: e.activation(dst_ap, src_ap, AF.Square, accum_out=S["stat"][:, 1:2]))
        p.wait("vector", "SE", sv)
        p.op("vector", "VE", lambda e: e.tensor_scalar(S["stat"][:, 2:3], S["stat"][:, 0:1], 1.0 / n, None, ALU.mult))
        p.op("vector", "VE", lambda e: e.tensor_scalar(S["stat"][:, 3:4], S["stat"][:, 1:2], 1.0 / n, None, ALU.mult))
        p.op("vector", "VE", lambda e: e.tensor_tensor(S["stat"][:, 4:5], S["stat"][:, 2:3], S["stat"][:, 2:3], ALU.mult))
        p.op("vector", "VE", lambda e: e.tensor_tensor(S["stat"][:, 5:6], S["stat"][:, 3:4], S["stat"][:, 4:5], ALU.subtract))
        p.op("vector", "VE", lambda e: e.tensor_scalar(S["stat"][:, 5:6], S["stat"][:, 5:6], EPS, None, ALU.add))
        vv = p.op("vector", "VE", lambda e: e.reciprocal(S["stat"][:, 6:7], S["stat"][:, 5:6]))
        p.wait("scalar", "VE", vv)
        sv = p.op("scalar", "SE", lambda e: e.activation(S["stat"][:, 7:8], S["stat"][:, 6:7], AF.Sqrt))
        p.wait("vector", "SE", sv)
        return p.op("vector", "VE", lambda e: e.tensor_scalar(
            dst_ap, src_ap, S["stat"][:, 2:3], S["stat"][:, 7:8], ALU.subtract, ALU.mult))

    def dbg_finish():
        for nm in list(p.cnt):
            if nm != "STO":
                p.wait("sync", nm)
        for j in range(32):
            p.dma("sync", "STO", out_h[128 * j:128 * (j + 1), :], S["zero2k"][:, :])
        p.wait("sync", "STO")
        emit(nc, p)

    p.dma("sync", "LDC", S["sb_iden_b"][:, :], iden_b_h[:, :])
    p.dma("sync", "LDC", S["sb_ones128b"][:, :], ones128_h[:, :])
    p.dma("sync", "LDC", S["sb_ltonesb"][:, :], ltones_h[:, :])
    p.dma("sync", "LDC", S["sb_cst_small"][:, :], cst_small[:, :])
    p.dma("sync", "LDC", S["sb_cst_small2"][:, :], cst_small2[:, :])
    p.dma("sync", "LDC", S["convw"][:, :], convw_h[:, :])
    p.dma("sync", "LDC", S["sb_pos_bias"][:, :], pos_bias_h[:, :])
    ld0 = p.dma("sync", "LDC", S["sb_cst64"][0:64, :], cst64[:, :])
    p.opq("vector", lambda e: e.memset(S["zero2k"][:, :], 0.0))
    p.opq("vector", lambda e: e.memset(S["onesrow"][0:1, :], 1.0))
    zready = p.op("vector", "VE", lambda e: e.memset(S["stat"][:, :], 0.0))
    p.wait("scalar", "LDC", ld0)
    p.op("scalar", "SE", lambda e: e.activation(S["sb_cst64"][0:64, 2:3], S["sb_cst64"][0:64, 2:3], AF.Exp))
    sv = p.op("scalar", "SE", lambda e: e.mul(a_col, S["sb_cst64"][0:64, 2:3], -1.0))
    p.wait("vector", "SE", sv)
    p.op("vector", "VE", lambda e: e.reciprocal(nra_col, S["sb_cst64"][0:64, 2:3]))

    w_inT_s = AR.alloc("w_inT", [128, 8, DI], BF16)
    w_xdtT_s = AR.alloc("w_xdtT", [128, 16, 128], BF16)
    dmat_s = AR.alloc("dmat", [128, DI], BF16)
    gate_wT_s = AR.alloc("gate_wT", [128, 8, NE], BF16)
    for k in range(8):
        p.dma("gpsimd", "GW1", w_inT_s[:, k, :], w_inT[128 * k:128 * (k + 1), :])
        p.dma("gpsimd", "GW1", gate_wT_s[:, k, :], gate_wT[128 * k:128 * (k + 1), :])
    for k in range(16):
        p.dma("gpsimd", "GW1", w_xdtT_s[:, k, :], w_xdtT[128 * k:128 * (k + 1), :])
    gw_l0 = p.dma("gpsimd", "GW1", dmat_s[0:64, :], d_mat[:, :])

    p.wait("gpsimd", "VE", zready)
    for j in range(32):
        p.dma("gpsimd", "GZ", sp_g[128 * j:128 * (j + 1), :], S["zero2k"][:, :])
    for j in range(4):
        p.dma("gpsimd", "GZ",
              cmp_h[8192 * j:8192 * (j + 1), :].rearrange("(a b) c -> a (b c)", b=64),
              S["zero2k"][:, 0:128])
    gz_done = p.cnt["GZ"]

    x_t = AR.alloc("x_t", [128, 2, D])
    xn_f = AR.alloc("xn_f", [128, D])
    xn_b = AR.alloc("xn_b", [128, 2, D], BF16)

    xn_done_tile = []
    for it in range(NT):
        sl = it % 2
        if it >= 2:
            p.wait("sync", "VE", xn_done_tile[it - 2])
        ldv = p.dma("sync", "LX%d" % sl, x_t[:, sl, :], x_h[128 * it:128 * (it + 1), :])
        p.wait("vector", "LX%d" % sl, ldv)
        emit_ln(x_t[:, sl, :], xn_f[:, :])
        if it >= 2:
            p.wait("vector", "SX%d" % sl)
        vv = p.op("vector", "VE", lambda e, sl=sl: e.tensor_copy(xn_b[:, sl, :], xn_f[:, :]))
        xn_done_tile.append(vv)
        p.wait("sync", "VE", vv)
        p.dma("sync", "SX%d" % sl, sp_xn[128 * it:128 * (it + 1), :], xn_b[:, sl, :])

    AR.free("x_t")
    AR.free("xn_f")
    AR.free("xn_b")
    xn_fm = AR.alloc("xn_fm", [128, 8, T], BF16)
    p.wait("scalar", "SX0")
    p.wait("scalar", "SX1")
    for dk in range(8):
        p.tdma("scalar", "TL1", xn_fm[:, dk, :], sp_xn[:, 128 * dk:128 * (dk + 1)])
    tld_xnfm = p.cnt["TL1"]

    if debug and STAGE == 1:
        dbg_finish()
        return nc, io

    p.wait("tensor", "TL1", tld_xnfm)
    p.wait("tensor", "GW1", gw_l0)
    for it in range(NT):
        p.mm(PA[:, 8 * it:8 * it + 8],
             [(xn_fm[:, k, 128 * it:128 * (it + 1)], gate_wT_s[:, k, :]) for k in range(8)])
    te_logits = p.cnt["TE"]

    aff3 = S["aff_tok"][:, :].rearrange("p (a b) -> p a b", b=NE)
    pa3 = PA[:, 0:256].rearrange("p (a b) -> p a b", b=NE)
    smax = AR.alloc("smax", [128, NT])
    ssum = AR.alloc("ssum", [128, NT])
    p.wait("vector", "TE", te_logits)
    p.op("vector", "VE", lambda e: e.reduce_max(smax[:, :], pa3, AX.X))
    vv = p.op("vector", "VE", lambda e: e.tensor_tensor(
        aff3, pa3, smax[:, :].unsqueeze(2).broadcast_to([128, NT, NE]), ALU.subtract))
    p.wait("scalar", "VE", vv)
    sv = p.op("scalar", "SE", lambda e: e.activation(S["aff_tok"][:, :], S["aff_tok"][:, :], AF.Exp))
    p.wait("vector", "SE", sv)
    p.op("vector", "VE", lambda e: e.reduce_sum(ssum[:, :], aff3, AX.X))
    p.op("vector", "VE", lambda e: e.reciprocal(ssum[:, :], ssum[:, :]))
    aff_done = p.op("vector", "VE", lambda e: e.tensor_tensor(
        aff3, aff3, ssum[:, :].unsqueeze(2).broadcast_to([128, NT, NE]), ALU.mult))

    mask3f = AR.alloc("mask3", [128, NT * NE], BF16)
    pref3 = AR.alloc("pref3", [128, NT * NE])
    u3 = AR.alloc("u3", [128, NT * NE])
    selw = AR.alloc("selw", [128, 80])
    selwb = AR.alloc("selwb", [128, 16], BF16)
    LO = selw[:, 0:8]
    HI = selw[:, 8:16]
    MID = selw[:, 16:24]
    CNTC = selw[:, 24:32]
    SW = selw[:, 32:40]
    TMP8 = selw[:, 40:48]
    ONES32 = selw[:, 48:80]
    PARTB = selwb[:, 0:8]
    TOTSB = selwb[:, 8:16]
    m3 = mask3f[:, :].rearrange("p (a b) -> p a b", b=NE)
    me = lambda e_: mask3f[:, e_::NE]
    pe = lambda e_: pref3[:, e_::NE]
    ue = lambda e_: u3[:, e_::NE]
    mei = mask3f[:, :].rearrange("p (a b) -> p b a", b=NE)
    eoffk = S["sb_cst_small2"][:, 43:51]
    idxf3 = S["idxf_tok"][:, :].rearrange("p (a b) -> p a b", b=NE)
    idxi3 = S["idxf_tok_i"][:, :].rearrange("p (a b) -> p a b", b=NE)

    p.op("vector", "VE", lambda e: e.memset(LO, 0.0))
    p.op("vector", "VE", lambda e: e.memset(HI, 1.0))
    p.op("vector", "VE", lambda e: e.memset(ONES32, 1.0))
    p.wait("vector", "VE", aff_done)
    for i in range(BISECT):
        p.op("vector", "VE", lambda e: e.tensor_tensor(MID, LO, HI, ALU.add))
        p.op("vector", "VE", lambda e: e.tensor_scalar(MID, MID, 0.5, None, ALU.mult))
        p.op("vector", "VE", lambda e: e.tensor_tensor(
            m3, aff3, MID.unsqueeze(1).broadcast_to([128, NT, NE]), ALU.is_ge))
        def _cnt_red(e):
            with nc.allow_low_precision(reason="0/1 count reduce, <=32, exact in bf16"):
                return e.tensor_reduce(PARTB, mei, AX.X, ALU.add)
        vv = p.op("vector", "VE", _cnt_red)
        p.wait("tensor", "VE", vv)
        tv = p.mm(PA[:, 0:8], [(S["sb_ones128b"][:, :], PARTB)])
        p.wait("vector", "TE", tv)
        p.op("vector", "VE", lambda e: e.tensor_copy(CNTC, PA[:, 0:8]))
        p.op("vector", "VE", lambda e: e.tensor_scalar(SW, CNTC, float(KSEL), None, ALU.is_ge))
        p.op("vector", "VE", lambda e: e.tensor_tensor(TMP8, MID, LO, ALU.subtract))
        p.op("vector", "VE", lambda e: e.tensor_tensor(TMP8, TMP8, SW, ALU.mult))
        p.op("vector", "VE", lambda e: e.tensor_tensor(LO, LO, TMP8, ALU.add))
        p.op("vector", "VE", lambda e: e.tensor_tensor(TMP8, HI, MID, ALU.subtract))
        p.op("vector", "VE", lambda e: e.tensor_tensor(TMP8, TMP8, SW, ALU.mult))
        p.op("vector", "VE", lambda e: e.tensor_tensor(HI, MID, TMP8, ALU.add))
    p.op("vector", "VE", lambda e: e.tensor_tensor(
        m3, aff3, LO.unsqueeze(1).broadcast_to([128, NT, NE]), ALU.is_ge))
    p.op("vector", "VE", lambda e: e.tensor_scalar(u3[:, :], mask3f[:, :], -1.0, 1.0, ALU.mult, ALU.add))
    for e_ in range(NE):
        p.op("vector", "VE", lambda e, e_=e_: e.tensor_tensor_scan(
            pe(e_), ONES32, me(e_), 0.0, ALU.mult, ALU.add))
    for e_ in range(NE):
        p.op("vector", "VE", lambda e, e_=e_: e.tensor_tensor_scan(
            ue(e_), ONES32, ue(e_), 0.0, ALU.mult, ALU.add))
    p.op("vector", "VE", lambda e: e.tensor_copy(TOTSB, pref3[:, (NT - 1) * NE:NT * NE]))
    vv = p.op("vector", "VE", lambda e: e.tensor_copy(PARTB, u3[:, (NT - 1) * NE:NT * NE]))
    p.wait("tensor", "VE", vv)
    p.mm(PA[:, 8:16], [(S["sb_ltonesb"][:, :], TOTSB)])
    tv = p.mm(PA[:, 16:24], [(S["sb_ltonesb"][:, :], PARTB)])
    p.wait("vector", "TE", tv)
    p.op("vector", "VE", lambda e: e.tensor_tensor(
        pref3[:, :].rearrange("p (a b) -> p a b", b=NE),
        pref3[:, :].rearrange("p (a b) -> p a b", b=NE),
        PA[:, 8:16].unsqueeze(1).broadcast_to([128, NT, NE]), ALU.add))
    p.op("vector", "VE", lambda e: e.tensor_tensor(
        u3[:, :].rearrange("p (a b) -> p a b", b=NE),
        u3[:, :].rearrange("p (a b) -> p a b", b=NE),
        PA[:, 16:24].unsqueeze(1).broadcast_to([128, NT, NE]), ALU.add))
    p.op("vector", "VE", lambda e: e.tensor_tensor(S["idxf_tok"][:, :], pref3[:, :], u3[:, :], ALU.subtract))
    p.op("vector", "VE", lambda e: e.tensor_scalar(S["idxf_tok"][:, :], S["idxf_tok"][:, :], float(KSEL), None, ALU.subtract))
    p.op("vector", "VE", lambda e: e.tensor_tensor(S["idxf_tok"][:, :], S["idxf_tok"][:, :], mask3f[:, :], ALU.mult))
    p.op("vector", "VE", lambda e: e.tensor_tensor(S["idxf_tok"][:, :], S["idxf_tok"][:, :], u3[:, :], ALU.add))
    p.op("vector", "VE", lambda e: e.tensor_tensor(
        idxf3, idxf3, eoffk.unsqueeze(1).broadcast_to([128, NT, NE]), ALU.add))
    cast_idx = p.op("vector", "VE", lambda e: e.tensor_copy(S["idxf_tok_i"][:, :], S["idxf_tok"][:, :]))

    pairs_i = AR.alloc("pairs_all", [128, NE, NT, 2], I32)
    pairs_f = AR.view("pairs_all", [128, NE, NT, 2], F32)
    stg_i = AR.alloc("stg", [128, NE, 8, 2], I32)
    stg_f = AR.view("stg", [128, NE, 8, 2], F32)
    stg_flat = AR.view("stg", [128, NE * 16], I32)
    xg = AR.alloc("xg", [128, 2, 8, D], BF16)
    for ei in range(NE):
        p.op("vector", "VE", lambda e, ei=ei: e.tensor_copy(pairs_i[:, ei, :, 0], iota_tok))
        p.op("vector", "VE", lambda e, ei=ei: e.tensor_copy(pairs_f[:, ei, :, 1], aff3[:, :, ei]))
    pairs_ve = p.cnt["VE"]
    p.wait("gpsimd", "GZ", gz_done)
    p.wait("gpsimd", "VE", pairs_ve)
    gath_done = {}
    lds_end = {}
    for ei in range(NE):
        for blk in range(NT):
            p.iscat("GSa%d" % ei, cmp_h[:, :], idxi3[:, blk, ei:ei + 1], pairs_i[:, ei, blk, :])
        p.wait("sync", "GSa%d" % ei, p.cnt["GSa%d" % ei])
        sld = p.dma("sync", "LDSa%d" % ei, stg_flat[:, 16 * ei:16 * (ei + 1)],
                    cmp_h[ei * T:ei * T + KSEL, :].rearrange("(p j) c -> p (j c)", p=128))
        lds_end[ei] = sld
        if ei < 2:
            p.wait("gpsimd", "LDSa%d" % ei, sld)
            for j in range(8):
                p.igather("GG%d" % ei, xg[:, ei, j, :], sp_xn[:, :], stg_i[:, ei, j, 0:1])
            gath_done[ei] = ("GG%d" % ei, p.cnt["GG%d" % ei])

    if debug and STAGE == 3:
        p.dma("sync", "SDB", sp_x1[0:128, 0:256], S["aff_tok"][:, :])
        dbg_finish()
        return nc, io

    hpre = AR.alloc("hpre", [128, 16, CH + 3], BF16)
    hc = AR.alloc("hc", [128, 16, CH], BF16)
    t0b = AR.alloc("t0b", [128, 2, CH], BF16)
    zb = AR.alloc("zb", [64, T], BF16)
    scn = AR.alloc("scn", [64, 6, CH])
    zcar = AR.alloc("zcar", [64, 2])

    p.opq("vector", lambda e: e.memset(hpre[:, :, 0:3], 0.0))
    p.opq("vector", lambda e: e.memset(zcar[0:64, 0:2], 0.0))

    evac_hist = {}
    halo_v = {}
    conv_v = {}
    xdt_te = 0
    sc_epi = 0

    def emit_win(c):
        col = slice(CH * c, CH * (c + 1))
        for mi in range(16):
            bank = mi % 2
            key = c * 16 + mi
            if key >= 2:
                p.wait("tensor", "SE", evac_hist[key - 2])
            tv = p.mm(PS[:, bank, :],
                      [(w_inT_s[:, k, 128 * mi:128 * (mi + 1)], xn_fm[:, k, col])
                       for k in range(8)])
            p.wait("scalar", "TE", tv)
            if mi == 0 and c > 0:
                p.wait("scalar", "VE", halo_v[c - 1])
            evac_hist[key] = p.op("scalar", "SE",
                lambda e, mi=mi, bank=bank: e.activation(
                    hpre[:, mi, 3:3 + CH], PS[:, bank, :], AF.Identity, bias=b_in_c(mi)))

    def emit_conv(c):
        p.wait("vector", "SE", evac_hist[c * 16 + 15])
        if c > 0:
            p.wait("vector", "TE", xdt_te)
        for mi in range(16):
            tsl = mi % 2
            p.op("vector", "VE", lambda e, mi=mi, tsl=tsl: e.tensor_scalar(
                t0b[:, tsl, :], hpre[:, mi, 0:CH], convw_c(mi, 0), convb_c(mi), ALU.mult, ALU.add))
            p.op("vector", "VE", lambda e, mi=mi, tsl=tsl: e.scalar_tensor_tensor(
                t0b[:, tsl, :], hpre[:, mi, 1:1 + CH], convw_c(mi, 1), t0b[:, tsl, :], ALU.mult, ALU.add))
            p.op("vector", "VE", lambda e, mi=mi, tsl=tsl: e.scalar_tensor_tensor(
                t0b[:, tsl, :], hpre[:, mi, 2:2 + CH], convw_c(mi, 2), t0b[:, tsl, :], ALU.mult, ALU.add))
            p.op("vector", "VE", lambda e, mi=mi, tsl=tsl: e.scalar_tensor_tensor(
                hc[:, mi, :], hpre[:, mi, 3:3 + CH], convw_c(mi, 3), t0b[:, tsl, :], ALU.mult, ALU.add))
        conv_v[c] = p.cnt["VE"]
        if c + 1 < NCH:
            p.op("vector", "VE", lambda e: e.tensor_copy(hpre[:, :, 0:3], hpre[:, :, CH:CH + 3]))
        halo_v[c] = p.cnt["VE"]

    def emit_xdt(c):
        col = slice(CH * c, CH * (c + 1))
        p.wait("tensor", "VE", conv_v[c])
        if c > 0:
            p.wait("tensor", "SE", sc_epi)
        p.mm(PS[0:64, 2, :], [(w_xdtT_s[:, k, 0:64], hc[:, k, :]) for k in range(16)])
        tv = p.mm(PS[0:64, 3, :], [(w_xdtT_s[:, k, 64:128], hc[:, k, :]) for k in range(16)])
        p.wait("scalar", "TE", tv)
        p.op("scalar", "SE", lambda e: e.activation(scn[0:64, 0, :], PS[0:64, 2, :], AF.Identity, bias=b_x_col))
        se1 = p.op("scalar", "SE", lambda e: e.activation(scn[0:64, 1, :], PS[0:64, 3, :], AF.Identity, bias=b_dt_col))
        p.op("scalar", "SE", lambda e: e.activation(scn[0:64, 2, :], scn[0:64, 1, :], AF.Abs))
        p.op("scalar", "SE", lambda e: e.activation(scn[0:64, 2, :], scn[0:64, 2, :], AF.Exp, scale=-1.0))
        p.op("scalar", "SE", lambda e: e.activation(scn[0:64, 2, :], scn[0:64, 2, :], AF.Ln, bias=1.0))
        sv = p.op("scalar", "SE", lambda e: e.activation(scn[0:64, 5, :], scn[0:64, 1, :], AF.Relu))
        p.wait("vector", "SE", sv)
        vv = p.op("vector", "VE", lambda e: e.tensor_tensor(scn[0:64, 2, :], scn[0:64, 2, :], scn[0:64, 5, :], ALU.add))
        p.wait("scalar", "VE", vv)
        sv = p.op("scalar", "SE", lambda e: e.activation(scn[0:64, 3, :], scn[0:64, 2, :], AF.Exp, scale=a_col))
        p.wait("vector", "SE", sv)
        p.op("vector", "VE", lambda e: e.tensor_scalar(scn[0:64, 4, :], scn[0:64, 3, :], 1.0, nra_col, ALU.subtract, ALU.mult))
        p.op("vector", "VE", lambda e: e.tensor_tensor(scn[0:64, 4, :], scn[0:64, 4, :], scn[0:64, 0, :], ALU.mult))
        p.op("vector", "VE", lambda e: e.tensor_tensor_scan(
            scn[0:64, 5, :], scn[0:64, 3, :], scn[0:64, 4, :], zcar[0:64, 0:1], ALU.mult, ALU.add))
        p.op("vector", "VE", lambda e: e.tensor_copy(zcar[0:64, 0:1], scn[0:64, 5, CH - 1:CH]))
        p.op("vector", "VE", lambda e, col=col: e.tensor_copy(zb[0:64, col], scn[0:64, 5, :]))
        return tv, se1

    for c in range(NCH):
        emit_win(c)
        if c >= 1:
            xdt_te, sc_epi = emit_xdt(c - 1)
        emit_conv(c)
    xdt_te, sc_epi = emit_xdt(NCH - 1)
    scan_done = p.cnt["VE"]

    if debug and STAGE == 2:
        p.wait("sync", "VE", scan_done)
        p.dma("sync", "SDB", sp_lk[0:64, :], zb[0:64, :])
        p.dma("sync", "SDB", sp_x1[0:128, 0:256], S["aff_tok"][:, :])
        dbg_finish()
        return nc, io

    for nm in ("mask3", "pref3", "u3", "selw", "selwb", "pairs_all",
               "smax", "ssum", "hpre", "hc", "t0b", "scn", "zcar",
               "w_inT", "w_xdtT", "xn_fm"):
        AR.free(nm)

    xin = AR.alloc("xin", [128, 2, 8, D], BF16)
    h_h = AR.alloc("h_h", [128, 4, D], BF16)
    gsil = AR.alloc("gsil", [128, D], BF16)
    y_acc = AR.alloc("y_acc", [128, 8, D])
    wg_s = AR.alloc("wg_s", [128, 2, 8, 512], BF16)
    wu_s = AR.alloc("wu_s", [128, 2, 8, 512], BF16)
    wd_s = AR.alloc("wd_s", [128, 2, 4, D], BF16)
    wdf = AR.alloc("wdf", [128, 2, D])

    SPL = 8
    yscat = []
    gate_ve_end = {}
    trans_end = {}
    mm_end = {}
    dn_mm_end = {}
    split_te_end = {}
    gat_ve = {}
    idxc = lambda ei, j: stg_i[:, ei, j, 0:1]

    te_ph3 = p.cnt["TE"]
    wdf_cast = {}
    p.wait("gpsimd", "VE", scan_done)
    p.wait("sync", "VE", scan_done)
    p.wait("sync", "TE", te_ph3)

    for ei in range(MOE_LIMIT):
        sl = ei % 2
        p.wait("tensor", gath_done[ei][0], gath_done[ei][1])
        if ei >= 2:
            p.wait("tensor", "VE", gat_ve.get(ei - 2, 0))
        ev_prev = 0
        if ei >= 2:
            p.wait("vector", "TE", mm_end.get(ei - 2, te_ph3))
        for j in range(8):
            if j > 0:
                p.wait("tensor", "VE", ev_prev)
            v = p.inc("TE", 1)
            h = p.sems["TE"]
            for cb in range(8):
                if cb < 7:
                    p.emit("tensor", lambda e, sl=sl, j=j, cb=cb: e.transpose(
                        PT[:, 128 * cb:128 * (cb + 1)], xg[:, sl, j, 128 * cb:128 * (cb + 1)], S["sb_iden_b"][:, :]))
                else:
                    p.emit("tensor", lambda e, sl=sl, j=j, cb=cb, h=h: e.transpose(
                        PT[:, 128 * cb:128 * (cb + 1)], xg[:, sl, j, 128 * cb:128 * (cb + 1)], S["sb_iden_b"][:, :]).then_inc(h, 1))
            p.wait("vector", "TE", v)
            for cb in range(8):
                p.op("vector", "VE", lambda e, sl=sl, j=j, cb=cb: e.tensor_copy(
                    xin[:, sl, cb, 128 * j:128 * (j + 1)], PT[:, 128 * cb:128 * (cb + 1)]))
            ev_prev = p.cnt["VE"]
        trans_end[ei] = p.cnt["TE"]
        gat_ve[ei] = p.cnt["VE"]
        xin_ready = p.cnt["VE"]
        p.wait("tensor", "VE", xin_ready)

        def load_split(ei, s):
            slw = s % 2
            sem = "GW2%d" % slw
            p.dma("gpsimd", sem, wg_s[:, slw, :, :],
                  wgT_all[ei, :, 512 * s:512 * (s + 1)].rearrange("(k p) c -> p k c", p=128))
            gws = p.dma("gpsimd", sem, wu_s[:, slw, :, :],
                        wuT_all[ei, :, 512 * s:512 * (s + 1)].rearrange("(k p) c -> p k c", p=128))
            plc = 0
            for hf in range(2):
                p.wait("sync", "PL", wdf_cast.get(0, 0))
                rows = slice(512 * s + 256 * hf, 512 * s + 256 * (hf + 1))
                hws = p.dma("sync", "HW%d" % slw, wdf[:, :, :],
                            wdT_all[ei, rows, :].rearrange("(k p) d -> p k d", p=128))
                p.wait("gpsimd", "HW%d" % slw, hws)
                plc = p.op("gpsimd", "PL", lambda e, slw=slw, hf=hf: e.tensor_copy(
                    wd_s[:, slw, 2 * hf:2 * (hf + 1), :], wdf[:, :, :]))
                wdf_cast[0] = plc
            return (sem, gws, plc)

        gw_split = {}
        p.wait("gpsimd", "TE", split_te_end.get((ei - 1, SPL - 2), te_ph3))
        gw_split[0] = load_split(ei, 0)
        p.wait("gpsimd", "TE", split_te_end.get((ei - 1, SPL - 1), te_ph3))
        gw_split[1] = load_split(ei, 1)
        if ei + 2 < MOE_LIMIT:
            p.wait("gpsimd", "TE", trans_end[ei])
            p.wait("gpsimd", "LDSa%d" % (ei + 2), lds_end[ei + 2])
            gsem = "GG%d" % ((ei + 2) % 2)
            for j in range(8):
                p.igather(gsem, xg[:, (ei + 2) % 2, j, :], sp_xn[:, :], idxc(ei + 2, j))
            gath_done[ei + 2] = (gsem, p.cnt[gsem])
        if ei >= 1:
            if yscat:
                p.wait("gpsimd", "GS2", yscat[-1])
            p.wait("gpsimd", "VE", gate_ve_end[ei - 1])
            for j in range(8):
                yscat.append(p.iscat("GS2", sp_g[:, :], idxc(ei - 1, j), y_acc[:, j, :]))
        gu_evac = {}
        dn_evac = {}
        dn_mm = {}
        for s in range(SPL):
            slw = s % 2
            p.wait("tensor", gw_split[s][0], gw_split[s][1])
            p.wait("tensor", "PL", gw_split[s][2])
            if s > 0:
                pass
            for gidx in range(8):
                mi, th = gidx // 2, gidx % 2
                g_lin = s * 8 + gidx
                gb, ub = g_lin % 2, 2 + g_lin % 2
                if g_lin >= 2:
                    p.wait("tensor", "VE", gu_evac[g_lin - 2])
                if gidx == 0 and s > 0:
                    p.wait("tensor", "VE", dn_evac[(s - 1, 7)])
                tok = slice(512 * th, 512 * (th + 1))
                tvg = p.mm(PS[:, gb, :],
                           [(wg_s[:, slw, k, 128 * mi:128 * (mi + 1)], xin[:, sl, k, tok])
                            for k in range(8)])
                tvu = p.mm(PS[:, ub, :],
                           [(wu_s[:, slw, k, 128 * mi:128 * (mi + 1)], xin[:, sl, k, tok])
                            for k in range(8)])
                p.wait("scalar", "TE", tvg)
                if g_lin >= 2:
                    p.wait("scalar", "VE", gu_evac[g_lin - 2])
                sv = p.op("scalar", "SE", lambda e, gb=gb, tok=tok: e.activation(
                    gsil[:, tok], PS[:, gb, :], AF.Silu))
                p.wait("vector", "SE", sv)
                p.wait("vector", "TE", tvu)
                if gidx == 0 and s > 0:
                    p.wait("vector", "TE", dn_mm[(s - 1, 7)])
                vv = p.op("vector", "VE", lambda e, mi=mi, tok=tok, ub=ub: e.tensor_tensor(
                    h_h[:, mi, tok], gsil[:, tok], PS[:, ub, :], ALU.mult))
                gu_evac[g_lin] = vv
            p.wait("tensor", "VE", gu_evac[s * 8 + 7])
            p.wait("tensor", "VE", gu_evac[s * 8 + 6])
            for j in range(8):
                bk = (4, 5) if j % 2 == 0 else (0, 1)
                if j >= 2:
                    p.wait("tensor", "VE", dn_evac[(s, j - 2)])
                elif s > 0:
                    p.wait("tensor", "VE", dn_evac[(s - 1, 6 + j)])
                tv = None
                for nh in range(2):
                    tv = p.mm(PS[:, bk[nh], :],
                              [(h_h[:, k, 128 * j:128 * (j + 1)],
                                wd_s[:, slw, k, 512 * nh:512 * (nh + 1)]) for k in range(4)])
                dn_mm[(s, j)] = tv
                p.wait("vector", "TE", tv)
                if s == 0 and ei > 0:
                    p.wait("vector", "GS2", yscat[(ei - 1) * 8 + 7])
                pj = PS[:, bk[0]:bk[0] + 2, :].rearrange("p a b -> p (a b)") if bk[0] == 4 else PS[:, 0:2, :].rearrange("p a b -> p (a b)")
                if s == 0:
                    vv = p.op("vector", "VE", lambda e, j=j, pj=pj: e.tensor_copy(y_acc[:, j, :], pj))
                else:
                    vv = p.op("vector", "VE", lambda e, j=j, pj=pj: e.tensor_tensor(y_acc[:, j, :], y_acc[:, j, :], pj, ALU.add))
                dn_evac[(s, j)] = vv
            split_te_end[(ei, s)] = p.cnt["TE"]
            if s + 2 < SPL:
                p.wait("gpsimd", "TE", split_te_end[(ei, s)])
                gw_split[s + 2] = load_split(ei, s + 2)
        mm_end[ei] = p.cnt["TE"]
        p.wait("vector", "TE", dn_mm[(SPL - 1, 7)])
        for j in range(8):
            p.wait("vector", "VE", dn_evac[(SPL - 1, j)])
            p.op("vector", "VE", lambda e, ei=ei, j=j: e.tensor_scalar(
                y_acc[:, j, :], y_acc[:, j, :], stg_f[:, ei, j, 1:2], None, ALU.mult))
        gate_ve_end[ei] = p.cnt["VE"]
    if MOE_LIMIT >= 1:
        if yscat:
            p.wait("gpsimd", "GS2", yscat[-1])
        p.wait("gpsimd", "VE", gate_ve_end[MOE_LIMIT - 1])
        for j in range(8):
            yscat.append(p.iscat("GS2", sp_g[:, :], idxc(MOE_LIMIT - 1, j), y_acc[:, j, :]))

    moe_done = yscat[-1] if yscat else 0

    if debug and STAGE == 4:
        p.wait("sync", "GS2", moe_done)
        dbg_finish()
        return nc, io

    for nm in ("xg", "xin", "h_h", "gsil", "wg_s", "wu_s", "wd_s", "wdf", "y_acc", "stg"):
        AR.free(nm)

    w_outT_s = AR.alloc("w_outT", [128, 16, D], BF16)
    p.wait("gpsimd", "GS2", moe_done)
    for k in range(16):
        p.dma("gpsimd", "GW3", w_outT_s[:, k, :], w_outT[128 * k:128 * (k + 1), :])
    gw_wout = p.cnt["GW3"]

    y0_sb = AR.alloc("y0_sb", [128, 16, 512], BF16)
    s_sb = AR.alloc("s_sb", [128, D])
    g_t = AR.alloc("g_t", [128, 2, D])
    x_t2 = AR.alloc("x_t2", [128, 2, D])
    bcl0_s = AR.alloc("bcl0", [128, D])
    s1_sb = AR.alloc("s1_sb", [128, D])
    ln1w_s = AR.alloc("ln1w", [128, D])
    ln1b_s = AR.alloc("ln1b", [128, D])
    xnl_f = AR.alloc("xnl_f", [128, D])
    xnl_b = AR.alloc("xnl_b", [128, 2, D], BF16)
    xn1_b = AR.alloc("xn1_b", [128, 2, D], BF16)
    p.wait("sync", "GS2", moe_done)
    ld_bc = p.dma("sync", "LDC5", bcl0_s[:, :], bc_l0[:, :])
    p.dma("sync", "LDC6", ln1w_s[:, :], bc_ln1w[:, :])
    ld_ln1 = p.dma("sync", "LDC6", ln1b_s[:, :], bc_ln1b[:, :])
    p.wait("gpsimd", "LDC6", ld_ln1)

    p.wait("tensor", "GW3", gw_wout)
    p.wait("tensor", "VE", scan_done)
    wout_te = 0
    for c in range(NCH):
        col = slice(CH * c, CH * (c + 1))
        for mi in range(16):
            bank = mi % 2
            tv = p.mm(PS[:, bank, :], [(dmat_s[0:64, 128 * mi:128 * (mi + 1)], zb[0:64, col])])
            p.wait("vector", "TE", tv)
            if c > 0 and mi == 0:
                p.wait("vector", "TE", wout_te)
            vv = p.op("vector", "VE", lambda e, mi=mi, bank=bank: e.tensor_copy(
                y0_sb[:, mi, :], PS[:, bank, :]))
            p.wait("tensor", "VE", vv)
        for tt in range(CT):
            it = c * CT + tt
            rows = slice(128 * it, 128 * (it + 1))
            sl = tt % 2
            ldg = p.dma("sync", "LG%d" % sl, g_t[:, sl, :], sp_g[rows, :])
            ldx = p.dma("sync", "L2%d" % sl, x_t2[:, sl, :], x_h[rows, :])
            tv = None
            for nh in range(2):
                tv = p.mm(PS[:, 4 + nh, :],
                          [(y0_sb[:, k, 128 * tt:128 * (tt + 1)],
                            w_outT_s[:, k, 512 * nh:512 * (nh + 1)]) for k in range(16)])
            p.wait("vector", "TE", tv)
            p.wait("vector", "LDC5", ld_bc)
            p.wait("vector", "ST1")
            p.op("vector", "VE", lambda e: e.scalar_tensor_tensor(
                s_sb[:, 0:512], PS[:, 4, :], 1.0, bcl0_s[:, 0:512], ALU.mult, ALU.add))
            vv = p.op("vector", "VE", lambda e: e.scalar_tensor_tensor(
                s_sb[:, 512:1024], PS[:, 5, :], 1.0, bcl0_s[:, 512:1024], ALU.mult, ALU.add))
            p.wait("tensor", "VE", vv)
            p.wait("scalar", "LG%d" % sl, ldg)
            sv = p.op("scalar", "SE", lambda e, sl=sl: e.activation(g_t[:, sl, :], g_t[:, sl, :], AF.Silu))
            p.wait("vector", "SE", sv)
            p.wait("vector", "L2%d" % sl, ldx)
            p.op("vector", "VE", lambda e, sl=sl: e.tensor_tensor(s_sb[:, :], s_sb[:, :], g_t[:, sl, :], ALU.mult))
            vv = p.op("vector", "VE", lambda e, sl=sl: e.tensor_tensor(s_sb[:, :], s_sb[:, :], x_t2[:, sl, :], ALU.add))
            p.wait("sync", "VE", vv)
            p.dma("sync", "ST1", sp_x1[rows, :], s_sb[:, :])
            if it >= 1:
                p.wait("vector", "PL")
            if it >= 2:
                p.wait("gpsimd", "SL%d" % sl)
            lnv = emit_ln(s_sb[:, :], s1_sb[:, :])
            p.wait("gpsimd", "VE", lnv)
            p.op("gpsimd", "PL", lambda e, sl=sl: e.tensor_copy(xnl_b[:, sl, :], s1_sb[:, :]))
            lnv2 = emit_ln(s1_sb[:, :], xnl_f[:, :])
            p.wait("gpsimd", "VE", lnv2)
            p.op("gpsimd", "PL", lambda e: e.tensor_tensor(xnl_f[:, :], xnl_f[:, :], ln1w_s[:, :], ALU.mult))
            plv = p.op("gpsimd", "PL", lambda e, sl=sl: e.tensor_tensor(xn1_b[:, sl, :], xnl_f[:, :], ln1b_s[:, :], ALU.add))
            p.wait("sync", "PL", plv)
            p.dma("sync", "SL%d" % sl, sp_ln[rows, :], xnl_b[:, sl, :])
            p.dma("sync", "SL%d" % sl, sp_ln1[rows, :], xn1_b[:, sl, :])
        wout_te = p.cnt["TE"]
    x1_done = p.cnt["ST1"]

    if debug and STAGE == 5:
        dbg_finish()
        return nc, io

    for nm in ("y0_sb", "g_t", "bcl0", "dmat", "zb", "w_outT", "gate_wT"):
        AR.free(nm)

    w_raw = AR.alloc("w_raw", [128, 8, D])
    absw = AR.alloc("absw", [128, 8 * D])
    for k in range(8):
        ld_w = p.dma("sync", "LDW", w_raw[:, k, :], l1wT[128 * k:128 * (k + 1), :])
    p.wait("scalar", "LDW", ld_w)
    sv = p.op("scalar", "SE", lambda e: e.activation(
        absw[:, :], w_raw.rearrange("p a b -> p (a b)") if False else w_raw[:, :, :], AF.Abs, accum_out=S["stat"][:, 8:9]))
    p.wait("tensor", "SE", sv)
    tv = p.mm(PA[0:1, 0:1], [(ones_c, S["stat"][:, 8:9])])
    p.wait("vector", "TE", tv)
    vv = p.op("vector", "VE", lambda e: e.tensor_copy(S["stat"][0:1, 9:10], PA[0:1, 0:1]))
    p.wait("tensor", "VE", vv)
    tv = p.mm(PA[:, 0:1], [(S["onesrow"][0:1, :], S["stat"][0:1, 9:10])])
    p.wait("vector", "TE", tv)
    p.op("vector", "VE", lambda e: e.tensor_scalar(S["stat"][:, 10:11], PA[:, 0:1], 0.7 / (D * D), None, ALU.mult))
    vv = p.op("vector", "VE", lambda e: e.tensor_scalar(absw[:, :], absw[:, :], S["stat"][:, 10:11], None, ALU.is_gt))
    p.wait("scalar", "VE", vv)
    sv = p.op("scalar", "SE", lambda e: e.sign(w_raw[:, :, :], w_raw[:, :, :]))
    p.wait("vector", "SE", sv)
    wtern = AR.alloc("wtern", [128, 8, D], BF16)
    tern_done = p.op("vector", "VE", lambda e: e.tensor_tensor(
        wtern.rearrange("p a b -> p (a b)") if False else wtern[:, :, :],
        absw[:, :].rearrange("p (a b) -> p a b", b=D), w_raw[:, :, :], ALU.mult))
    AR.free("w_raw")
    AR.free("absw")

    for nm in ("x_t2", "s_sb", "s1_sb", "xnl_f", "xnl_b", "xn1_b", "ln1w", "ln1b"):
        AR.free(nm)
    p.wait("sync", "LDW")
    p.wait("sync", "VE", tern_done)
    p.wait("gpsimd", "LDW")
    p.wait("vector", "LDW")
    p.wait("scalar", "SE")
    p.wait("scalar", "VE", tern_done)
    xn1c7 = AR.alloc("xn1c7", [128, 8, CH], BF16)
    p.wait("scalar", "SL0")
    p.wait("scalar", "SL1")

    wqkT_s = AR.alloc("wqkT", [128, 8, QK], BF16)
    p.wait("gpsimd", "VE")
    p.wait("gpsimd", "SE")
    for k in range(8):
        p.dma("gpsimd", "GW4", wqkT_s[:, k, :], wqkT[128 * k:128 * (k + 1), :])
    gw_qk = p.cnt["GW4"]
    wvT_s = AR.alloc("wvT", [128, 8, HD], BF16)
    for k in range(8):
        p.dma("gpsimd", "GW5", wvT_s[:, k, :], wvT[128 * k:128 * (k + 1), :])
    gw_v = p.cnt["GW5"]

    qk_fm = AR.alloc("qk_fm", [128, 2, CH], BF16)
    p.wait("tensor", "GW4", gw_qk)

    bv_s = AR.alloc("bv_s", [128, HD])
    p.wait("sync", "SL0")
    p.wait("sync", "SL1")
    ld_bv = p.dma("sync", "LDC7", bv_s[:, :], bc_bv[:, :])
    linkv = AR.alloc("linkv", [128, 2, HD])
    p.wait("vector", "SL0")
    p.wait("vector", "SL1")
    p.wait("vector", "ST1")
    p.opq("vector", lambda e: e.memset(linkv[:, :, :], 0.0))
    linkv_zero = p.cnt["VE"]

    v_t = AR.alloc("v_t", [128, 4, HD], BF16)
    lk_b = AR.alloc("lk_b", [128, 2, CH], BF16)
    lk_tok = AR.alloc("lk_tok", [128, 4, QK], BF16)
    hd_t = AR.alloc("hd_t", [128, 2, CH])
    rt_t = AR.alloc("rt_t", [128, 2, CH])
    cs_t = AR.alloc("cs_t", [128, 2, CH])

    p.wait("vector", "LDC7", ld_bv)
    p.wait("sync", "VE")
    p.wait("sync", "SE")
    st_v_tiles = []
    st_lk = 0
    linkv_te = 0
    wqkv_te = 0
    for c in range(NCH):
        col = slice(CH * c, CH * (c + 1))
        if c > 0:
            p.wait("scalar", "TE", wqkv_te)
        for dk in range(8):
            tld_x1c = p.tdma("scalar", "TL2", xn1c7[:, dk, :], sp_ln1[col, 128 * dk:128 * (dk + 1)])
        p.wait("tensor", "TL2", tld_x1c)
        if c > 0:
            p.wait("scalar", "SQK", 16 * 2 * c)
        for m2 in range(2):
            bank = m2 % 2
            tv = p.mm(PS[:, bank, :],
                      [(wqkT_s[:, k, 128 * m2:128 * (m2 + 1)], xn1c7[:, k, :])
                       for k in range(8)])
            p.wait("scalar", "TE", tv)
            sv = p.op("scalar", "SE", lambda e, m2=m2, bank=bank: e.activation(
                qk_fm[:, m2, :], PS[:, bank, :], AF.Silu, bias=bqk_c(m2)))
            p.wait("tensor", "SE", sv)
        qk_done_c = p.cnt["SE"]
        p.wait("sync", "SE", qk_done_c)
        p.dma("sync", "SQK", sp_qk[0:128, col], qk_fm[:, 0, :])
        p.dma("sync", "SQK", sp_qk[128:256, col], qk_fm[:, 1, :])
        p.wait("vector", "SE", qk_done_c)
        ld_cs = p.dma("sync", "LC7", cs_t[:, 0, :], cos_h[:, col])
        ld_cs = p.dma("sync", "LC7", cs_t[:, 1, :], sin_h[:, col])
        p.wait("vector", "LC7", ld_cs)
        if c > 0:
            p.wait("vector", "SLK", st_lk)
        for cc in range(2):
            p.op("vector", "VE", lambda e, cc=cc: e.tensor_scalar(
                hd_t[:, cc, :], qk_fm[:, cc, :], gamma_c(3, cc), beta_c(3, cc), ALU.mult, ALU.add))
        p.op("vector", "VE", lambda e: e.tensor_tensor(rt_t[:, 0, :], hd_t[:, 0, :], cs_t[:, 0, :], ALU.mult))
        p.op("vector", "VE", lambda e: e.tensor_tensor(rt_t[:, 1, :], hd_t[:, 1, :], cs_t[:, 1, :], ALU.mult))
        p.op("vector", "VE", lambda e: e.tensor_tensor(lk_b[:, 0, :], rt_t[:, 0, :], rt_t[:, 1, :], ALU.subtract))
        p.op("vector", "VE", lambda e: e.tensor_tensor(rt_t[:, 0, :], hd_t[:, 1, :], cs_t[:, 0, :], ALU.mult))
        p.op("vector", "VE", lambda e: e.tensor_tensor(rt_t[:, 1, :], hd_t[:, 0, :], cs_t[:, 1, :], ALU.mult))
        vv = p.op("vector", "VE", lambda e: e.tensor_tensor(lk_b[:, 1, :], rt_t[:, 0, :], rt_t[:, 1, :], ALU.add))
        p.wait("sync", "VE", vv)
        p.dma("sync", "SLK", sp_lk[0:128, col], lk_b[:, 0, :])
        st_lk = p.dma("sync", "SLK", sp_lk[128:256, col], lk_b[:, 1, :])
        p.wait("tensor", "GW5", gw_v)
        if c > 0:
            p.wait("scalar", "TE", linkv_te)
            p.wait("scalar", "SV", 16 * 4 * c)
        for tt in range(CT):
            tcol = slice(128 * tt, 128 * (tt + 1))
            for nb in range(4):
                bank = nb % 2
                tv = p.mm(PS[:, bank, :],
                          [(xn1c7[:, k, tcol], wvT_s[:, k, 512 * nb:512 * (nb + 1)])
                           for k in range(8)])
                p.wait("vector", "TE", tv)
                p.op("vector", "VE", lambda e, nb=nb, bank=bank: e.scalar_tensor_tensor(
                    rt_t[:, 0, 0:512], PS[:, bank, :], 1.0, bv_s[:, 512 * nb:512 * (nb + 1)], ALU.mult, ALU.add))
                vv = p.cnt["VE"]
                p.wait("scalar", "VE", vv)
                sv = p.op("scalar", "SE", lambda e, tt=tt, nb=nb: e.activation(
                    v_t[:, tt, 512 * nb:512 * (nb + 1)], rt_t[:, 0, 0:512], AF.Silu))
                p.wait("tensor", "SE", sv)
                p.wait("vector", "SE", sv)
            p.wait("sync", "SE")
            it = c * CT + tt
            st_v_tiles.append(p.dma("sync", "SV", sp_v[128 * it:128 * (it + 1), :], v_t[:, tt, :]))
        p.wait("scalar", "SLK", st_lk)
        if c > 0:
            p.wait("scalar", "TE", linkv_te)
        tld = None
        for tt in range(CT):
            it = c * CT + tt
            tld = p.tdma("scalar", "TL3", lk_tok[:, tt, :], sp_lk[0:QK, 128 * it:128 * (it + 1)])
        p.wait("vector", "VE", linkv_zero)
        p.wait("tensor", "TL3", tld)
        p.wait("tensor", "SE")
        for dh in range(2):
            for nq in range(4):
                tv = p.mm(PA[:, 0:512],
                          [(lk_tok[:, tt, 128 * dh:128 * (dh + 1)], v_t[:, tt, 512 * nq:512 * (nq + 1)])
                           for tt in range(CT)])
                p.wait("vector", "TE", tv)
                vv = p.op("vector", "VE", lambda e, dh=dh, nq=nq: e.tensor_tensor(
                    linkv[:, dh, 512 * nq:512 * (nq + 1)], linkv[:, dh, 512 * nq:512 * (nq + 1)],
                    PA[:, 0:512], ALU.add))
                p.wait("tensor", "VE", vv)
        linkv_te = p.cnt["TE"]
        wqkv_te = p.cnt["TE"]
    linkv_b = AR.alloc("linkv_b", [128, 2, HD], BF16)
    p.op("vector", "VE", lambda e: e.tensor_scalar(linkv_b[:, 0, :], linkv[:, 0, :], 1.0 / T, None, ALU.mult))
    linkv_done = p.op("vector", "VE", lambda e: e.tensor_scalar(linkv_b[:, 1, :], linkv[:, 1, :], 1.0 / T, None, ALU.mult))

    if debug and STAGE == 6:
        dbg_finish()
        return nc, io

    for nm in ("wvT", "v_t", "lk_b", "hd_t", "rt_t", "cs_t", "linkv",
               "xn1c7", "wqkT", "bv_s", "qk_fm", "lk_tok"):
        AR.free(nm)

    woT_s = AR.alloc("woT", [128, 16, D], BF16)
    p.wait("gpsimd", "TE")
    p.wait("gpsimd", "VE")
    p.wait("gpsimd", "SE")
    p.wait("gpsimd", "SV")
    p.wait("gpsimd", "SLK")
    p.wait("gpsimd", "SQK")
    for k in range(16):
        p.dma("gpsimd", "GW6", woT_s[:, k, :], woT[128 * k:128 * (k + 1), :])
    gw_wo = p.cnt["GW6"]
    wgT1_s = AR.alloc("wgT1", [128, 8, HD], BF16)
    for k in range(8):
        p.dma("gpsimd", "GW7", wgT1_s[:, k, :], wgT1[128 * k:128 * (k + 1), :])
    gw_wg1 = p.cnt["GW7"]

    ln2w_s = AR.alloc("ln2w", [128, D])
    ln2b_s = AR.alloc("ln2b", [128, D])
    bo_s = AR.alloc("bo_s", [128, D])
    l1b_s = AR.alloc("l1b_s", [128, D])
    p.wait("sync", "TE")
    p.wait("sync", "VE")
    p.dma("sync", "LDC8", ln2w_s[:, :], bc_ln2w[:, :])
    p.dma("sync", "LDC8", ln2b_s[:, :], bc_ln2b[:, :])
    p.dma("sync", "LDC8", bo_s[:, :], bc_bo[:, :])
    ld_bc2 = p.dma("sync", "LDC8", l1b_s[:, :], bc_l1b[:, :])

    xn1fc = AR.alloc("xn1fc", [128, 8, CH], BF16)
    xnfc = AR.alloc("xnfc", [128, 8, CH], BF16)
    v_tc = AR.alloc("v_tc", [128, 4, HD], BF16)
    x1c = AR.alloc("x1c", [128, 2, D])
    qkc = AR.alloc("qkc", [128, 2, CH], BF16)
    qq_b = AR.alloc("qq_b", [128, 2, CH], BF16)
    qkk_b = AR.alloc("qkk_b", [128, 2, CH], BF16)
    lq_b = AR.alloc("lq_b", [128, 2, CH], BF16)
    hd8 = AR.alloc("hd8", [128, 2, CH])
    rt8 = AR.alloc("rt8", [128, 2, CH])
    cs8 = AR.alloc("cs8", [128, 2, CH])
    att_sb = AR.alloc("att_sb", [128, 4, 128], BF16)
    u_sb = AR.alloc("u_sb", [128, 16, CH], BF16)
    gate_b = AR.alloc("gate_b", [128, 512], BF16)
    g1 = AR.alloc("g1", [128, D])
    attf = AR.alloc("attf", [128, 64])
    o1 = rt8.rearrange("p a b -> p (a b)")
    s_c = cs8.rearrange("p a b -> p (a b)")

    p.op("vector", "VE", lambda e: e.memset(att_sb[:, :, :], 0.0))
    p.wait("vector", "LDC8", ld_bc2)
    p.wait("tensor", "GW7", gw_wg1)
    p.wait("tensor", "GW6", gw_wo)

    p.wait("scalar", "TE")
    p.wait("scalar", "VE")
    p.wait("scalar", "SV")
    p.wait("scalar", "SLK")
    p.wait("scalar", "SQK")
    p.wait("sync", "TE")
    p.wait("sync", "VE")
    sto_all = []
    x1_used = {}
    te_chunk_end = 0
    ve_comb_end = 0
    for c in range(NCH):
        col = slice(CH * c, CH * (c + 1))
        if c > 0:
            p.wait("scalar", "TE", te_chunk_end)
            p.wait("sync", "TE", te_chunk_end)
            p.wait("sync", "VE", ve_comb_end)
        for dk in range(8):
            p.tdma("scalar", "TL4", xn1fc[:, dk, :], sp_ln1[col, 128 * dk:128 * (dk + 1)])
        tld_c1 = p.cnt["TL4"]
        for dk in range(8):
            p.tdma("scalar", "TL4", xnfc[:, dk, :], sp_ln[col, 128 * dk:128 * (dk + 1)])
        tld_c2 = p.cnt["TL4"]
        p.wait("sync", "SV")
        p.wait("sync", "SQK")
        for tt in range(CT):
            it = c * CT + tt
            p.dma("sync", "L8", v_tc[:, tt, :], sp_v[128 * it:128 * (it + 1), :])
        p.dma("sync", "L8", qkc[:, 0, :], sp_qk[0:128, col])
        p.dma("sync", "L8", qkc[:, 1, :], sp_qk[128:256, col])
        p.dma("sync", "L8", cs8[:, 0, :], cos_h[:, col])
        ld_cs = p.dma("sync", "L8", cs8[:, 1, :], sin_h[:, col])

        if c > 0:
            p.wait("vector", "TE", te_chunk_end)
        p.wait("vector", "L8", ld_cs)
        for hsel, dst in ((0, qq_b), (2, qkk_b), (1, lq_b)):
            for cc in range(2):
                p.op("vector", "VE", lambda e, cc=cc, hsel=hsel: e.tensor_scalar(
                    hd8[:, cc, :], qkc[:, cc, :], gamma_c(hsel, cc), beta_c(hsel, cc), ALU.mult, ALU.add))
            p.op("vector", "VE", lambda e: e.tensor_tensor(rt8[:, 0, :], hd8[:, 0, :], cs8[:, 0, :], ALU.mult))
            p.op("vector", "VE", lambda e: e.tensor_tensor(rt8[:, 1, :], hd8[:, 1, :], cs8[:, 1, :], ALU.mult))
            p.op("vector", "VE", lambda e, dst=dst: e.tensor_tensor(dst[:, 0, :], rt8[:, 0, :], rt8[:, 1, :], ALU.subtract))
            p.op("vector", "VE", lambda e: e.tensor_tensor(rt8[:, 0, :], hd8[:, 1, :], cs8[:, 0, :], ALU.mult))
            p.op("vector", "VE", lambda e: e.tensor_tensor(rt8[:, 1, :], hd8[:, 0, :], cs8[:, 1, :], ALU.mult))
            p.op("vector", "VE", lambda e, dst=dst: e.tensor_tensor(dst[:, 1, :], rt8[:, 0, :], rt8[:, 1, :], ALU.add))
        rope_done = p.cnt["VE"]
        p.wait("tensor", "VE", rope_done)

        for pr in range(CT):
            p.mm(PA[0:64, 0:64], [(qkk_b[:, k, 128 * pr:128 * pr + 64],
                                   qq_b[:, k, 128 * pr:128 * pr + 64]) for k in range(2)])
            tv = p.mm(PA[64:128, 64:128], [(qkk_b[:, k, 128 * pr + 64:128 * (pr + 1)],
                                            qq_b[:, k, 128 * pr + 64:128 * (pr + 1)]) for k in range(2)],
                      tile_position=(0, 64))
            p.wait("vector", "TE", tv)
            sl = pr
            p.op("vector", "VE", lambda e, sl=sl: e.scalar_tensor_tensor(
                attf[0:64, :], PA[0:64, 0:64], 1.0 / G, S["sb_pos_bias"][0:64, :], ALU.mult, ALU.add))
            vv = p.op("vector", "VE", lambda e, sl=sl: e.scalar_tensor_tensor(
                attf[64:128, :], PA[64:128, 64:128], 1.0 / G, S["sb_pos_bias"][64:128, :], ALU.mult, ALU.add))
            p.wait("scalar", "VE", vv)
            p.op("scalar", "SE", lambda e: e.activation(attf[:, :], attf[:, :], AF.Relu))
            sv = p.op("scalar", "SE", lambda e, sl=sl: e.activation(attf[:, :], attf[:, :], AF.Square))
            p.wait("vector", "SE", sv)
            vv = p.op("vector", "VE", lambda e, sl=sl: e.tensor_copy(att_sb[0:64, sl, 0:64], attf[0:64, :]))
            vv = p.op("vector", "VE", lambda e, sl=sl: e.tensor_copy(att_sb[64:128, sl, 64:128], attf[64:128, :]))
            p.wait("tensor", "VE", vv)
        p.wait("tensor", "L8", ld_cs)
        p.wait("tensor", "TL4", tld_c2)
        u_evac = {}
        for m in range(16):
            bank = m % 2
            gbank = 2 + (m % 2)
            if m >= 2:
                p.wait("tensor", "VE", u_evac[m - 2])
            for pr in range(CT):
                p.mm(PS[:, bank, 128 * pr:128 * (pr + 1)],
                     [(v_tc[:, pr, 128 * m:128 * (m + 1)], att_sb[:, pr, :])] +
                     [(linkv_b[:, k, 128 * m:128 * (m + 1)], lq_b[:, k, 128 * pr:128 * (pr + 1)])
                      for k in range(2)])
            tvqo = p.cnt["TE"]
            tvg = p.mm(PS[:, gbank, :],
                       [(wgT1_s[:, k, 128 * m:128 * (m + 1)], xn1fc[:, k, :]) for k in range(8)])
            p.wait("scalar", "TE", tvg)
            if m >= 1:
                p.wait("scalar", "VE", u_evac[m - 1])
            sv = p.op("scalar", "SE", lambda e, gbank=gbank, m=m: e.activation(
                gate_b[:, :], PS[:, gbank, :], AF.Silu, bias=bg_c(m)))
            p.wait("vector", "SE", sv)
            p.wait("vector", "TE", tvg)
            vv = p.op("vector", "VE", lambda e, m=m, bank=bank: e.tensor_tensor(
                u_sb[:, m, :], gate_b[:, :], PS[:, bank, :], ALU.mult))
            u_evac[m] = vv
        p.wait("tensor", "VE", u_evac[15])
        p.wait("tensor", "TL4", tld_c2)
        for tt in range(CT):
            it = c * CT + tt
            rows = slice(128 * it, 128 * (it + 1))
            if it >= 2:
                p.wait("sync", "VE", x1_used[it - 2])
            ldx1 = p.dma("sync", "LX1C%d" % (tt % 2), x1c[:, tt % 2, :], sp_x1[rows, :])
            tv = None
            for nh in range(2):
                tv = p.mm(PS[:, 4 + nh, :],
                          [(u_sb[:, k, 128 * tt:128 * (tt + 1)],
                            woT_s[:, k, 512 * nh:512 * (nh + 1)]) for k in range(16)])
            p.wait("vector", "TE", tv)
            p.op("vector", "VE", lambda e: e.scalar_tensor_tensor(
                o1[:, 0:512], PS[:, 4, :], 1.0, bo_s[:, 0:512], ALU.mult, ALU.add))
            vv = p.op("vector", "VE", lambda e: e.scalar_tensor_tensor(
                o1[:, 512:1024], PS[:, 5, :], 1.0, bo_s[:, 512:1024], ALU.mult, ALU.add))
            p.wait("tensor", "VE", vv)
            if sto_all:
                p.wait("scalar", "STO", sto_all[-1])
                p.wait("vector", "STO", sto_all[-1])
            emit_ln(o1[:, :], g1[:, :])
            p.op("vector", "VE", lambda e: e.tensor_tensor(g1[:, :], g1[:, :], ln2w_s[:, :], ALU.mult))
            p.op("vector", "VE", lambda e: e.tensor_tensor(s_c[:, :], g1[:, :], ln2b_s[:, :], ALU.add))
            tv = None
            for nh in range(2):
                tv = p.mm(PS[:, 0 + nh, :],
                          [(xnfc[:, k, 128 * tt:128 * (tt + 1)],
                            wtern[:, k, 512 * nh:512 * (nh + 1)]) for k in range(8)])
            p.wait("vector", "TE", tv)
            if sto_all:
                p.wait("vector", "STO", sto_all[-1])
            p.op("vector", "VE", lambda e: e.tensor_scalar(g1[:, 0:512], PS[:, 0, :], alpha_c, None, ALU.mult))
            p.op("vector", "VE", lambda e: e.tensor_scalar(g1[:, 512:1024], PS[:, 1, :], alpha_c, None, ALU.mult))
            vv = p.op("vector", "VE", lambda e: e.tensor_tensor(g1[:, :], g1[:, :], l1b_s[:, :], ALU.add))
            p.wait("tensor", "VE", vv)
            p.wait("scalar", "VE", vv)
            sv = p.op("scalar", "SE", lambda e: e.activation(g1[:, :], g1[:, :], AF.Silu))
            p.wait("vector", "SE", sv)
            p.op("vector", "VE", lambda e: e.tensor_tensor(g1[:, :], g1[:, :], s_c[:, :], ALU.mult))
            p.wait("vector", "LX1C%d" % (tt % 2), ldx1)
            vv = p.op("vector", "VE", lambda e, tt=tt: e.tensor_tensor(g1[:, :], g1[:, :], x1c[:, tt % 2, :], ALU.add))
            x1_used[it] = vv
            p.wait("sync", "VE", vv)
            sto_all.append(p.dma("sync", "STO", out_h[rows, :], g1[:, :]))
        te_chunk_end = p.cnt["TE"]
        ve_comb_end = p.cnt["VE"]
    p.wait("sync", "STO", sto_all[-1])

    emit(nc, p)
    return nc, io


def emit(nc, p):
    with nc.Block() as blk:
        @blk.sync
        def _(e):
            for fn in p.ops["sync"]:
                fn(e)

        @blk.scalar
        def _(e):
            for fn in p.ops["scalar"]:
                fn(e)

        @blk.vector
        def _(e):
            for fn in p.ops["vector"]:
                fn(e)

        @blk.tensor
        def _(e):
            for fn in p.ops["tensor"]:
                fn(e)

        @blk.gpsimd
        def _(e):
            for fn in p.ops["gpsimd"]:
                fn(e)


def host_inputs(inputs, core):
    i = inputs
    d = {}
    d["x"] = i["L0_attn_x"][core]
    d["w_inT"] = i["L0_attn_W_in"].T
    d["w_xdtT"] = np.concatenate([i["L0_attn_W_x"].T, i["L0_attn_W_dt"].T], axis=1)
    d["d_mat"] = i["L0_attn_D_mat"]
    d["w_outT"] = i["L0_attn_W_out"].T
    d["gate_wT"] = i["L0_ffn_gate_w"].T
    d["wgT_all"] = i["L0_ffn_w_gate"].transpose(0, 2, 1)
    d["wuT_all"] = i["L0_ffn_w_up"].transpose(0, 2, 1)
    d["wdT_all"] = i["L0_ffn_w_down"].transpose(0, 2, 1)
    d["wqkT"] = i["L1_attn_Wqk"].T
    d["wvT"] = i["L1_attn_Wv"].T
    d["wgT1"] = i["L1_attn_Wg"].T
    d["woT"] = i["L1_attn_Wo"].T
    d["l1wT"] = i["L1_ffn_weight"].T

    cs = np.zeros((128, 64), np.float32)
    cs[:, 0:16] = i["L0_attn_b_in"].reshape(16, 128).T
    cs[:, 16:32] = i["L0_attn_conv_b"].reshape(16, 128).T
    cs[:, 32:34] = i["L1_attn_bqk"].reshape(2, 128).T
    cs[:, 34:50] = i["L1_attn_bg"].reshape(16, 128).T
    cs[:, 50:58] = i["L1_attn_gamma"].reshape(4, 2, 128).transpose(2, 0, 1).reshape(128, 8)
    d["cst_small"] = cs
    cs2 = np.zeros((128, 64), np.float32)
    cs2[:, 0:8] = i["L1_attn_beta"].reshape(4, 2, 128).transpose(2, 0, 1).reshape(128, 8)
    cs2[:, 8:9] = np.float32(np.asarray(i["L1_ffn_alpha"]).reshape(1)[0])
    cs2[:, 9:10] = 1.0
    cs2[:, 10:42] = np.arange(T, dtype=np.float32).reshape(NT, 128).T
    cs2[:, 42] = np.arange(128, dtype=np.float32) * T
    cs2[:, 43:51] = (np.arange(NE, dtype=np.float32) * T + KSEL - 1)[None, :]
    d["cst_small2"] = cs2
    d["ones128b"] = np.ones((128, 128), np.float32).astype(BF)
    d["ltonesb"] = np.triu(np.ones((128, 128), np.float32), 1).astype(BF)
    d["convw_fm"] = i["L0_attn_conv_w"][:, 0, :].reshape(16, 128, 4).transpose(1, 0, 2).reshape(128, 64)
    c64 = np.zeros((64, 8), np.float32)
    c64[:, 0] = i["L0_attn_b_x"]
    c64[:, 1] = i["L0_attn_b_dt"]
    c64[:, 2] = i["L0_attn_A_log"]
    d["cst64"] = c64
    d["bc_l0"] = np.tile(i["L0_attn_b_out"][None, :], (128, 1))
    d["bc_ln1w"] = np.tile(i["L1_attn_ln1_w"][None, :], (128, 1))
    d["bc_ln1b"] = np.tile(i["L1_attn_ln1_b"][None, :], (128, 1))
    d["bc_ln2w"] = np.tile(i["L1_attn_ln2_w"][None, :], (128, 1))
    d["bc_ln2b"] = np.tile(i["L1_attn_ln2_b"][None, :], (128, 1))
    d["bc_bo"] = np.tile(i["L1_attn_bo"][None, :], (128, 1))
    d["bc_l1b"] = np.tile(i["L1_ffn_bias"][None, :], (128, 1))
    d["bc_bv"] = np.tile(i["L1_attn_bv"][None, :], (128, 1))
    idx = np.arange(G)
    dist = np.clip(np.abs(idx[:, None] - idx[None, :]), 0, 32)
    pb = i["L1_attn_pos_emb"][dist].astype(np.float32)
    d["pos_bias"] = np.concatenate([pb, pb], axis=0)
    d["iden_b"] = np.eye(128, dtype=np.float32).astype(BF)
    half = 128
    inv_freq = 10000.0 ** (-np.arange(half, dtype=np.float32) / half)
    sinus = np.arange(T, dtype=np.float32)[None, :] * inv_freq[:, None]
    d["cos_fm"] = np.cos(sinus)
    d["sin_fm"] = np.sin(sinus)
    for k in list(d):
        a = np.asarray(d[k])
        if a.dtype == np.float64:
            a = a.astype(np.float32)
        d[k] = np.ascontiguousarray(a)
    return d


def kernel(**inputs):
    nc, io = build(debug=False)
    in_maps = [host_inputs(inputs, c) for c in range(8)]
    res = run_bass_kernel_spmd(nc, in_maps, core_ids=list(range(8)))
    return np.concatenate([r["out"][None] for r in res.results], axis=0)


# version 09 (final)
# speedup vs baseline: 1.0119x; 1.0119x over previous
import numpy as np
import ml_dtypes
import concourse.bass as bass
import concourse.mybir as mybir
from concourse.bass_utils import run_bass_kernel_spmd

F32 = mybir.dt.float32
BF16 = mybir.dt.bfloat16
I32 = mybir.dt.int32
AF = mybir.ActivationFunctionType
ALU = mybir.AluOpType
AX = mybir.AxisListType
BF = ml_dtypes.bfloat16

T = 4096
D = 1024
DI = 2048
DS = 64
NE = 8
DF = 4096
KSEL = 1024
NT = T // 128
NCH = 8
CH = T // NCH
CT = CH // 128
QK = 256
HD = 2048
G = 64
EPS = 1e-5
BISECT = 42

ENGS = ("sync", "scalar", "vector", "tensor", "gpsimd")
MOE_LIMIT = NE
MOE_PARTS = "all"

STAGE = 7


def ceil(a, b):
    return -(-a // b)


class Plan:
    def __init__(self, nc):
        self.nc = nc
        self.ops = {e: [] for e in ENGS}
        self.sems = {}
        self.cnt = {}

    def sem(self, name):
        if name not in self.sems:
            self.sems[name] = self.nc.alloc_semaphore(name)
            self.cnt[name] = 0
        return self.sems[name]

    def emit(self, eng, fn):
        self.ops[eng].append(fn)

    def inc(self, name, amt):
        self.sem(name)
        self.cnt[name] += amt
        return self.cnt[name]

    def wait(self, eng, name, val=None):
        h = self.sem(name)
        v = self.cnt[name] if val is None else val
        if v <= 0:
            return
        self.emit(eng, lambda e, h=h, v=v: e.wait_ge(h, v))

    def dma(self, eng, sem_name, out, in_, **kw):
        v = self.inc(sem_name, 16)
        h = self.sems[sem_name]
        self.emit(eng, lambda e, out=out, in_=in_, h=h, kw=dict(kw):
                  e.dma_start(out=out, in_=in_, **kw).then_inc(h, 16))
        return v

    def tdma(self, eng, sem_name, out, in_):
        v = self.inc(sem_name, 16)
        h = self.sems[sem_name]
        self.emit(eng, lambda e, out=out, in_=in_, h=h:
                  e.dma_start(out=out, in_=in_, transpose=True).then_inc(h, 16))
        return v

    def igather(self, sem_name, out, in_full, idx_ap):
        v = self.inc(sem_name, 16)
        h = self.sems[sem_name]
        self.emit("gpsimd", lambda e, out=out, i=in_full, x=idx_ap, h=h:
                  e.indirect_dma_start(out, None, i,
                                       bass.IndirectOffsetOnAxis(x, 0)).then_inc(h, 16))
        return v

    def iscat(self, sem_name, out_full, idx_ap, in_):
        v = self.inc(sem_name, 16)
        h = self.sems[sem_name]
        self.emit("gpsimd", lambda e, o=out_full, x=idx_ap, i=in_, h=h:
                  e.indirect_dma_start(o, bass.IndirectOffsetOnAxis(x, 0),
                                       i, None, compute_op=ALU.add).then_inc(h, 16))
        return v

    def op(self, eng, sem_name, fn):
        prev = self.cnt.get(sem_name, 0) if eng in ("vector", "scalar", "gpsimd") else 0
        v = self.inc(sem_name, 1)
        h = self.sems[sem_name]
        if prev > 0:
            self.emit(eng, lambda e, fn=fn, h=h, prev=prev:
                      fn(e)._wait_ge(h, prev).then_inc(h, 1))
        else:
            self.emit(eng, lambda e, fn=fn, h=h: fn(e).then_inc(h, 1))
        return v

    def opq(self, eng, fn):
        if eng == "vector":
            return self.op(eng, "VE", fn)
        if eng == "scalar":
            return self.op(eng, "SE", fn)
        self.emit(eng, lambda e, fn=fn: fn(e))

    def mm(self, out_ap, pairs, sem="TE", tile_position=None, first_start=True):
        n = len(pairs)
        v = self.inc(sem, 1)
        h = self.sems[sem]
        for j, (l, r) in enumerate(pairs):
            st = (j == 0) and first_start
            if j == n - 1:
                self.emit("tensor", lambda e, l=l, r=r, st=st, h=h, tp=tile_position:
                          e.matmul(out_ap, l, r, start=st, stop=True,
                                   tile_position=tp).then_inc(h, 1))
            else:
                self.emit("tensor", lambda e, l=l, r=r, st=st, tp=tile_position:
                          e.matmul(out_ap, l, r, start=st, stop=False,
                                   tile_position=tp))
        return v

    def tr(self, out_psum, in_ap, ident, sem="TE"):
        v = self.inc(sem, 1)
        h = self.sems[sem]
        self.emit("tensor", lambda e, o=out_psum, i=in_ap, d=ident, h=h:
                  e.transpose(o, i, d).then_inc(h, 1))
        return v


class Arena:

    def __init__(self, nc, name, n_f32):
        self.t = nc.alloc_sbuf_tensor(name, [128, n_f32], F32)
        self.free_list = [(0, n_f32)]
        self.allocs = {}
        self.hwm = 0

    def alloc(self, name, shape, dtype=F32):
        free = 1
        for s in shape[1:]:
            free *= s
        f32n = ceil(free, 2) if dtype == BF16 else free
        f32n = ceil(f32n, 16) * 16
        for i, (off, sz) in enumerate(self.free_list):
            if sz >= f32n:
                if sz == f32n:
                    self.free_list.pop(i)
                else:
                    self.free_list[i] = (off + f32n, sz - f32n)
                self.allocs[name] = (off, f32n)
                self.hwm = max(self.hwm, off + f32n)
                ap = self.t[0:128, off:off + f32n]
                if dtype != F32:
                    ap = ap.bitcast(dtype)
                ap = ap[0:shape[0], 0:free]
                if len(shape) == 3:
                    ap = ap.rearrange("p (a b) -> p a b", b=shape[2])
                elif len(shape) == 4:
                    ap = ap.rearrange("p (a b c) -> p a b c", b=shape[2], c=shape[3])
                return ap
        raise AssertionError(f"arena OOM for {name} ({f32n} f32/partition)")

    def view(self, name, shape, dtype=F32):
        off, f32n = self.allocs[name]
        free = 1
        for s in shape[1:]:
            free *= s
        ap = self.t[0:128, off:off + f32n]
        if dtype != F32:
            ap = ap.bitcast(dtype)
        ap = ap[0:shape[0], 0:free]
        if len(shape) == 3:
            ap = ap.rearrange("p (a b) -> p a b", b=shape[2])
        elif len(shape) == 4:
            ap = ap.rearrange("p (a b c) -> p a b c", b=shape[2], c=shape[3])
        return ap

    def free(self, name):
        off, sz = self.allocs.pop(name)
        self.free_list.append((off, sz))
        self.free_list.sort()
        merged = []
        for o, s in self.free_list:
            if merged and merged[-1][0] + merged[-1][1] == o:
                merged[-1] = (merged[-1][0], merged[-1][1] + s)
            else:
                merged.append((o, s))
        self.free_list = [tuple(x) for x in merged]


def build(debug=False):
    nc = bass.Bass("TRN2", target_bir_lowering=False)
    io = {}

    def din(name, shape, dtype=F32):
        io[name] = nc.dram_tensor(name, list(shape), dtype, kind="ExternalInput").ap()
        return io[name]

    def dint(name, shape, dtype=F32):
        kind = "ExternalOutput" if debug else "Internal"
        io[name] = nc.dram_tensor(name, list(shape), dtype, kind=kind).ap()
        return io[name]

    x_h = din("x", (T, D))
    out_h = nc.dram_tensor("out", [T, D], F32, kind="ExternalOutput").ap()
    io["out"] = out_h

    w_inT = din("w_inT", (D, DI))
    w_xdtT = din("w_xdtT", (DI, 128))
    d_mat = din("d_mat", (DS, DI))
    w_outT = din("w_outT", (DI, D))
    gate_wT = din("gate_wT", (D, NE))
    wgT_all = din("wgT_all", (NE, D, DF))
    wuT_all = din("wuT_all", (NE, D, DF))
    wdT_all = din("wdT_all", (NE, DF, D))
    wqkT = din("wqkT", (D, QK))
    wvT = din("wvT", (D, HD))
    wgT1 = din("wgT1", (D, HD))
    woT = din("woT", (HD, D))
    l1wT = din("l1wT", (D, D))
    cst_small = din("cst_small", (128, 64))
    cst_small2 = din("cst_small2", (128, 64))
    convw_h = din("convw_fm", (128, 64))
    cst64 = din("cst64", (64, 8))
    bc_l0 = din("bc_l0", (128, D))
    bc_ln1w = din("bc_ln1w", (128, D))
    bc_ln1b = din("bc_ln1b", (128, D))
    bc_ln2w = din("bc_ln2w", (128, D))
    bc_ln2b = din("bc_ln2b", (128, D))
    bc_bo = din("bc_bo", (128, D))
    bc_l1b = din("bc_l1b", (128, D))
    bc_bv = din("bc_bv", (128, HD))
    pos_bias_h = din("pos_bias", (128, 64))
    iden_b_h = din("iden_b", (128, 128), BF16)
    ones128_h = din("ones128b", (128, 128), BF16)
    ltones_h = din("ltonesb", (128, 128), BF16)
    cos_h = din("cos_fm", (128, T))
    sin_h = din("sin_fm", (128, T))

    sp_xn = dint("sp_xn", (T, D), BF16)
    sp_g = dint("sp_g", (T, D), F32)
    cmp_h = dint("cmp", (NE * T, 2), I32)
    sp_x1 = dint("sp_x1", (T, D), F32)
    sp_ln = dint("sp_ln", (T, D), BF16)
    sp_ln1 = dint("sp_ln1", (T, D), BF16)
    sp_v = dint("sp_v", (T, HD), BF16)
    sp_lk = dint("sp_lk", (QK, T), BF16)
    sp_qk = dint("sp_qk", (QK, T), BF16)

    p = Plan(nc)

    S = {}

    def sbt(name, shape, dtype=F32):
        S[name] = nc.alloc_sbuf_tensor(name, list(shape), dtype)
        return S[name]

    sbt("sb_iden_b", (128, 128), BF16)
    sbt("sb_cst_small", (128, 64))
    sbt("sb_cst_small2", (128, 64))
    sbt("convw", (128, 64))
    sbt("sb_cst64", (64, 8))
    sbt("stat", (128, 16))
    sbt("onesrow", (1, 128))
    sbt("zero2k", (128, 1024))
    sbt("aff_tok", (128, NT * NE))
    sbt("idxf_tok", (128, NT * NE))
    sbt("idxf_tok_i", (128, NT * NE), I32)
    sbt("sb_pos_bias", (128, 64))
    sbt("sb_ones128b", (128, 128), BF16)
    sbt("sb_ltonesb", (128, 128), BF16)

    AR = Arena(nc, "AR", 50700)

    PS = nc.alloc_psum_tensor("ps_main", [128, 6, 512], F32)
    PT = nc.alloc_psum_tensor("ps_tr", [128, 1024], BF16)
    PA = nc.alloc_psum_tensor("ps_aux", [128, 512], F32)

    b_in_c = lambda c: S["sb_cst_small"][:, c:c + 1]
    convb_c = lambda c: S["sb_cst_small"][:, 16 + c:17 + c]
    bqk_c = lambda c: S["sb_cst_small"][:, 32 + c:33 + c]
    bg_c = lambda c: S["sb_cst_small"][:, 34 + c:35 + c]
    gamma_c = lambda h, c: S["sb_cst_small"][:, 50 + 2 * h + c:51 + 2 * h + c]
    beta_c = lambda h, c: S["sb_cst_small2"][:, 2 * h + c:1 + 2 * h + c]
    alpha_c = S["sb_cst_small2"][:, 8:9]
    ones_c = S["sb_cst_small2"][:, 9:10]
    iota_tok = S["sb_cst_small2"][:, 10:42]
    convw_c = lambda c, k: S["convw"][:, 4 * c + k:4 * c + k + 1]
    b_x_col = S["sb_cst64"][0:64, 0:1]
    b_dt_col = S["sb_cst64"][0:64, 1:2]
    a_col = S["sb_cst64"][0:64, 3:4]
    nra_col = S["sb_cst64"][0:64, 4:5]

    def emit_ln(src_ap, dst_ap, n=D):
        vv = p.op("vector", "VE", lambda e: e.reduce_sum(S["stat"][:, 0:1], src_ap, AX.X))
        p.wait("scalar", "VE", vv)
        sv = p.op("scalar", "SE", lambda e: e.activation(dst_ap, src_ap, AF.Square, accum_out=S["stat"][:, 1:2]))
        p.wait("vector", "SE", sv)
        p.op("vector", "VE", lambda e: e.tensor_scalar(S["stat"][:, 2:3], S["stat"][:, 0:1], 1.0 / n, None, ALU.mult))
        p.op("vector", "VE", lambda e: e.tensor_scalar(S["stat"][:, 3:4], S["stat"][:, 1:2], 1.0 / n, None, ALU.mult))
        p.op("vector", "VE", lambda e: e.tensor_tensor(S["stat"][:, 4:5], S["stat"][:, 2:3], S["stat"][:, 2:3], ALU.mult))
        p.op("vector", "VE", lambda e: e.tensor_tensor(S["stat"][:, 5:6], S["stat"][:, 3:4], S["stat"][:, 4:5], ALU.subtract))
        p.op("vector", "VE", lambda e: e.tensor_scalar(S["stat"][:, 5:6], S["stat"][:, 5:6], EPS, None, ALU.add))
        vv = p.op("vector", "VE", lambda e: e.reciprocal(S["stat"][:, 6:7], S["stat"][:, 5:6]))
        p.wait("scalar", "VE", vv)
        sv = p.op("scalar", "SE", lambda e: e.activation(S["stat"][:, 7:8], S["stat"][:, 6:7], AF.Sqrt))
        p.wait("vector", "SE", sv)
        return p.op("vector", "VE", lambda e: e.tensor_scalar(
            dst_ap, src_ap, S["stat"][:, 2:3], S["stat"][:, 7:8], ALU.subtract, ALU.mult))

    def dbg_finish():
        for nm in list(p.cnt):
            if nm != "STO":
                p.wait("sync", nm)
        for j in range(32):
            p.dma("sync", "STO", out_h[128 * j:128 * (j + 1), :], S["zero2k"][:, :])
        p.wait("sync", "STO")
        emit(nc, p)

    p.dma("sync", "LDC", S["sb_iden_b"][:, :], iden_b_h[:, :])
    p.dma("sync", "LDC", S["sb_ones128b"][:, :], ones128_h[:, :])
    p.dma("sync", "LDC", S["sb_ltonesb"][:, :], ltones_h[:, :])
    p.dma("sync", "LDC", S["sb_cst_small"][:, :], cst_small[:, :])
    p.dma("sync", "LDC", S["sb_cst_small2"][:, :], cst_small2[:, :])
    p.dma("sync", "LDC", S["convw"][:, :], convw_h[:, :])
    p.dma("sync", "LDC", S["sb_pos_bias"][:, :], pos_bias_h[:, :])
    ld0 = p.dma("sync", "LDC", S["sb_cst64"][0:64, :], cst64[:, :])
    p.opq("vector", lambda e: e.memset(S["zero2k"][:, :], 0.0))
    p.opq("vector", lambda e: e.memset(S["onesrow"][0:1, :], 1.0))
    zready = p.op("vector", "VE", lambda e: e.memset(S["stat"][:, :], 0.0))
    p.wait("scalar", "LDC", ld0)
    p.op("scalar", "SE", lambda e: e.activation(S["sb_cst64"][0:64, 2:3], S["sb_cst64"][0:64, 2:3], AF.Exp))
    sv = p.op("scalar", "SE", lambda e: e.mul(a_col, S["sb_cst64"][0:64, 2:3], -1.0))
    p.wait("vector", "SE", sv)
    p.op("vector", "VE", lambda e: e.reciprocal(nra_col, S["sb_cst64"][0:64, 2:3]))

    w_inT_s = AR.alloc("w_inT", [128, 8, DI], BF16)
    w_xdtT_s = AR.alloc("w_xdtT", [128, 16, 128], BF16)
    dmat_s = AR.alloc("dmat", [128, DI], BF16)
    gate_wT_s = AR.alloc("gate_wT", [128, 8, NE], BF16)
    for k in range(8):
        p.dma("gpsimd", "GW1", w_inT_s[:, k, :], w_inT[128 * k:128 * (k + 1), :])
        p.dma("gpsimd", "GW1", gate_wT_s[:, k, :], gate_wT[128 * k:128 * (k + 1), :])
    for k in range(16):
        p.dma("gpsimd", "GW1", w_xdtT_s[:, k, :], w_xdtT[128 * k:128 * (k + 1), :])
    gw_l0 = p.dma("gpsimd", "GW1", dmat_s[0:64, :], d_mat[:, :])

    p.wait("gpsimd", "VE", zready)
    for j in range(32):
        p.dma("gpsimd", "GZ", sp_g[128 * j:128 * (j + 1), :], S["zero2k"][:, :])
    for j in range(4):
        p.dma("gpsimd", "GZ",
              cmp_h[8192 * j:8192 * (j + 1), :].rearrange("(a b) c -> a (b c)", b=64),
              S["zero2k"][:, 0:128])
    gz_done = p.cnt["GZ"]

    x_t = AR.alloc("x_t", [128, 2, D])
    xn_f = AR.alloc("xn_f", [128, D])
    xn_b = AR.alloc("xn_b", [128, 2, D], BF16)

    xn_fm = AR.alloc("xn_fm", [128, 8, T], BF16)
    xn_done_tile = []
    tr_te = []
    pxf = []
    p.wait("tensor", "GW1", gw_l0)
    for it in range(NT):
        sl = it % 2
        if it >= 2:
            p.wait("sync", "VE", xn_done_tile[it - 2])
        ldv = p.dma("sync", "LX%d" % sl, x_t[:, sl, :], x_h[128 * it:128 * (it + 1), :])
        p.wait("vector", "LX%d" % sl, ldv)
        emit_ln(x_t[:, sl, :], xn_f[:, :])
        if it >= 2:
            p.wait("vector", "SX%d" % sl)
            p.wait("vector", "TE", tr_te[it - 2])
        vv = p.op("vector", "VE", lambda e, sl=sl: e.tensor_copy(xn_b[:, sl, :], xn_f[:, :]))
        xn_done_tile.append(vv)
        p.wait("sync", "VE", vv)
        p.dma("sync", "SX%d" % sl, sp_xn[128 * it:128 * (it + 1), :], xn_b[:, sl, :])
        p.wait("tensor", "VE", vv)
        if it >= 1:
            p.wait("tensor", "PXF", pxf[it - 1])
        tei = p.inc("TE", 1)
        hTE = p.sems["TE"]
        for dk in range(8):
            if dk < 7:
                p.emit("tensor", lambda e, sl=sl, dk=dk: e.transpose(
                    PT[:, 128 * dk:128 * (dk + 1)], xn_b[:, sl, 128 * dk:128 * (dk + 1)],
                    S["sb_iden_b"][:, :]))
            else:
                p.emit("tensor", lambda e, sl=sl, dk=dk, hTE=hTE: e.transpose(
                    PT[:, 128 * dk:128 * (dk + 1)], xn_b[:, sl, 128 * dk:128 * (dk + 1)],
                    S["sb_iden_b"][:, :]).then_inc(hTE, 1))
        tr_te.append(tei)
        p.wait("scalar", "TE", tei)
        for dk in range(8):
            p.op("scalar", "PXF", lambda e, dk=dk, it=it: e.activation(
                xn_fm[:, dk, 128 * it:128 * (it + 1)], PT[:, 128 * dk:128 * (dk + 1)], AF.Identity))
        pxf.append(p.cnt["PXF"])
        p.wait("tensor", "PXF", pxf[it])
        p.mm(PA[:, 8 * it:8 * it + 8],
             [(xn_fm[:, k, 128 * it:128 * (it + 1)], gate_wT_s[:, k, :]) for k in range(8)])

    AR.free("x_t")
    AR.free("xn_f")
    AR.free("xn_b")

    if debug and STAGE == 1:
        dbg_finish()
        return nc, io

    te_logits = p.cnt["TE"]

    aff3 = S["aff_tok"][:, :].rearrange("p (a b) -> p a b", b=NE)
    pa3 = PA[:, 0:256].rearrange("p (a b) -> p a b", b=NE)
    smax = AR.alloc("smax", [128, NT])
    ssum = AR.alloc("ssum", [128, NT])
    p.wait("vector", "TE", te_logits)
    p.op("vector", "VE", lambda e: e.reduce_max(smax[:, :], pa3, AX.X))
    vv = p.op("vector", "VE", lambda e: e.tensor_tensor(
        aff3, pa3, smax[:, :].unsqueeze(2).broadcast_to([128, NT, NE]), ALU.subtract))
    p.wait("scalar", "VE", vv)
    sv = p.op("scalar", "SE", lambda e: e.activation(S["aff_tok"][:, :], S["aff_tok"][:, :], AF.Exp))
    p.wait("vector", "SE", sv)
    p.op("vector", "VE", lambda e: e.reduce_sum(ssum[:, :], aff3, AX.X))
    p.op("vector", "VE", lambda e: e.reciprocal(ssum[:, :], ssum[:, :]))
    aff_done = p.op("vector", "VE", lambda e: e.tensor_tensor(
        aff3, aff3, ssum[:, :].unsqueeze(2).broadcast_to([128, NT, NE]), ALU.mult))

    mask3f = AR.alloc("mask3", [128, NT * NE], BF16)
    pref3 = AR.alloc("pref3", [128, NT * NE])
    u3 = AR.alloc("u3", [128, NT * NE])
    selw = AR.alloc("selw", [128, 80])
    selwb = AR.alloc("selwb", [128, 16], BF16)
    LO = selw[:, 0:8]
    HI = selw[:, 8:16]
    MID = selw[:, 16:24]
    CNTC = selw[:, 24:32]
    SW = selw[:, 32:40]
    TMP8 = selw[:, 40:48]
    ONES32 = selw[:, 48:80]
    PARTB = selwb[:, 0:8]
    TOTSB = selwb[:, 8:16]
    m3 = mask3f[:, :].rearrange("p (a b) -> p a b", b=NE)
    me = lambda e_: mask3f[:, e_::NE]
    pe = lambda e_: pref3[:, e_::NE]
    ue = lambda e_: u3[:, e_::NE]
    mei = mask3f[:, :].rearrange("p (a b) -> p b a", b=NE)
    eoffk = S["sb_cst_small2"][:, 43:51]
    idxf3 = S["idxf_tok"][:, :].rearrange("p (a b) -> p a b", b=NE)
    idxi3 = S["idxf_tok_i"][:, :].rearrange("p (a b) -> p a b", b=NE)

    p.op("vector", "VE", lambda e: e.memset(LO, 0.0))
    p.op("vector", "VE", lambda e: e.memset(HI, 1.0))
    p.op("vector", "VE", lambda e: e.memset(ONES32, 1.0))
    p.wait("vector", "VE", aff_done)
    for i in range(BISECT):
        p.op("vector", "VE", lambda e: e.tensor_tensor(MID, LO, HI, ALU.add))
        p.op("vector", "VE", lambda e: e.tensor_scalar(MID, MID, 0.5, None, ALU.mult))
        p.op("vector", "VE", lambda e: e.tensor_tensor(
            m3, aff3, MID.unsqueeze(1).broadcast_to([128, NT, NE]), ALU.is_ge))
        def _cnt_red(e):
            with nc.allow_low_precision(reason="0/1 count reduce, <=32, exact in bf16"):
                return e.tensor_reduce(PARTB, mei, AX.X, ALU.add)
        vv = p.op("vector", "VE", _cnt_red)
        p.wait("tensor", "VE", vv)
        tv = p.mm(PA[:, 0:8], [(S["sb_ones128b"][:, :], PARTB)])
        p.wait("vector", "TE", tv)
        p.op("vector", "VE", lambda e: e.tensor_copy(CNTC, PA[:, 0:8]))
        p.op("vector", "VE", lambda e: e.tensor_scalar(SW, CNTC, float(KSEL), None, ALU.is_ge))
        p.op("vector", "VE", lambda e: e.tensor_tensor(TMP8, MID, LO, ALU.subtract))
        p.op("vector", "VE", lambda e: e.tensor_tensor(TMP8, TMP8, SW, ALU.mult))
        p.op("vector", "VE", lambda e: e.tensor_tensor(LO, LO, TMP8, ALU.add))
        p.op("vector", "VE", lambda e: e.tensor_tensor(TMP8, HI, MID, ALU.subtract))
        p.op("vector", "VE", lambda e: e.tensor_tensor(TMP8, TMP8, SW, ALU.mult))
        p.op("vector", "VE", lambda e: e.tensor_tensor(HI, MID, TMP8, ALU.add))
    p.op("vector", "VE", lambda e: e.tensor_tensor(
        m3, aff3, LO.unsqueeze(1).broadcast_to([128, NT, NE]), ALU.is_ge))
    p.op("vector", "VE", lambda e: e.tensor_scalar(u3[:, :], mask3f[:, :], -1.0, 1.0, ALU.mult, ALU.add))
    for e_ in range(NE):
        p.op("vector", "VE", lambda e, e_=e_: e.tensor_tensor_scan(
            pe(e_), ONES32, me(e_), 0.0, ALU.mult, ALU.add))
    for e_ in range(NE):
        p.op("vector", "VE", lambda e, e_=e_: e.tensor_tensor_scan(
            ue(e_), ONES32, ue(e_), 0.0, ALU.mult, ALU.add))
    p.op("vector", "VE", lambda e: e.tensor_copy(TOTSB, pref3[:, (NT - 1) * NE:NT * NE]))
    vv = p.op("vector", "VE", lambda e: e.tensor_copy(PARTB, u3[:, (NT - 1) * NE:NT * NE]))
    p.wait("tensor", "VE", vv)
    p.mm(PA[:, 8:16], [(S["sb_ltonesb"][:, :], TOTSB)])
    tv = p.mm(PA[:, 16:24], [(S["sb_ltonesb"][:, :], PARTB)])
    p.wait("vector", "TE", tv)
    p.op("vector", "VE", lambda e: e.tensor_tensor(
        pref3[:, :].rearrange("p (a b) -> p a b", b=NE),
        pref3[:, :].rearrange("p (a b) -> p a b", b=NE),
        PA[:, 8:16].unsqueeze(1).broadcast_to([128, NT, NE]), ALU.add))
    p.op("vector", "VE", lambda e: e.tensor_tensor(
        u3[:, :].rearrange("p (a b) -> p a b", b=NE),
        u3[:, :].rearrange("p (a b) -> p a b", b=NE),
        PA[:, 16:24].unsqueeze(1).broadcast_to([128, NT, NE]), ALU.add))
    p.op("vector", "VE", lambda e: e.tensor_tensor(S["idxf_tok"][:, :], pref3[:, :], u3[:, :], ALU.subtract))
    p.op("vector", "VE", lambda e: e.tensor_scalar(S["idxf_tok"][:, :], S["idxf_tok"][:, :], float(KSEL), None, ALU.subtract))
    p.op("vector", "VE", lambda e: e.tensor_tensor(S["idxf_tok"][:, :], S["idxf_tok"][:, :], mask3f[:, :], ALU.mult))
    p.op("vector", "VE", lambda e: e.tensor_tensor(S["idxf_tok"][:, :], S["idxf_tok"][:, :], u3[:, :], ALU.add))
    p.op("vector", "VE", lambda e: e.tensor_tensor(
        idxf3, idxf3, eoffk.unsqueeze(1).broadcast_to([128, NT, NE]), ALU.add))
    cast_idx = p.op("vector", "VE", lambda e: e.tensor_copy(S["idxf_tok_i"][:, :], S["idxf_tok"][:, :]))

    pairs_i = AR.alloc("pairs_all", [128, NE, NT, 2], I32)
    pairs_f = AR.view("pairs_all", [128, NE, NT, 2], F32)
    stg_i = AR.alloc("stg", [128, NE, 8, 2], I32)
    stg_f = AR.view("stg", [128, NE, 8, 2], F32)
    stg_flat = AR.view("stg", [128, NE * 16], I32)
    xg = AR.alloc("xg", [128, 2, 8, D], BF16)
    for ei in range(NE):
        p.op("vector", "VE", lambda e, ei=ei: e.tensor_copy(pairs_i[:, ei, :, 0], iota_tok))
        p.op("vector", "VE", lambda e, ei=ei: e.tensor_copy(pairs_f[:, ei, :, 1], aff3[:, :, ei]))
    pairs_ve = p.cnt["VE"]
    p.wait("gpsimd", "GZ", gz_done)
    p.wait("gpsimd", "SX0")
    p.wait("gpsimd", "SX1")
    p.wait("gpsimd", "VE", pairs_ve)
    gath_done = {}
    lds_end = {}
    for ei in range(NE):
        for blk in range(NT):
            p.iscat("GSa%d" % ei, cmp_h[:, :], idxi3[:, blk, ei:ei + 1], pairs_i[:, ei, blk, :])
        p.wait("sync", "GSa%d" % ei, p.cnt["GSa%d" % ei])
        sld = p.dma("sync", "LDSa%d" % ei, stg_flat[:, 16 * ei:16 * (ei + 1)],
                    cmp_h[ei * T:ei * T + KSEL, :].rearrange("(p j) c -> p (j c)", p=128))
        lds_end[ei] = sld
        if ei < 2:
            p.wait("gpsimd", "LDSa%d" % ei, sld)
            for j in range(8):
                p.igather("GG%d" % ei, xg[:, ei, j, :], sp_xn[:, :], stg_i[:, ei, j, 0:1])
            gath_done[ei] = ("GG%d" % ei, p.cnt["GG%d" % ei])

    if debug and STAGE == 3:
        p.dma("sync", "SDB", sp_x1[0:128, 0:256], S["aff_tok"][:, :])
        dbg_finish()
        return nc, io

    hpre = AR.alloc("hpre", [128, 16, CH + 3], BF16)
    hc = AR.alloc("hc", [128, 16, CH], BF16)
    t0b = AR.alloc("t0b", [128, 2, CH], BF16)
    zb = AR.alloc("zb", [64, T], BF16)
    scn = AR.alloc("scn", [64, 6, CH])
    zcar = AR.alloc("zcar", [64, 2])

    p.wait("vector", "SX0")
    p.wait("vector", "SX1")
    p.wait("scalar", "SX0")
    p.wait("scalar", "SX1")
    p.opq("vector", lambda e: e.memset(hpre[:, :, 0:3], 0.0))
    p.opq("vector", lambda e: e.memset(zcar[0:64, 0:2], 0.0))

    evac_hist = {}
    halo_v = {}
    conv_v = {}
    xdt_te = 0
    sc_epi = 0

    def emit_win(c):
        col = slice(CH * c, CH * (c + 1))
        for mi in range(16):
            bank = mi % 2
            key = c * 16 + mi
            if key >= 2:
                p.wait("tensor", "SE", evac_hist[key - 2])
            tv = p.mm(PS[:, bank, :],
                      [(w_inT_s[:, k, 128 * mi:128 * (mi + 1)], xn_fm[:, k, col])
                       for k in range(8)])
            p.wait("scalar", "TE", tv)
            if mi == 0 and c > 0:
                p.wait("scalar", "VE", halo_v[c - 1])
            evac_hist[key] = p.op("scalar", "SE",
                lambda e, mi=mi, bank=bank: e.activation(
                    hpre[:, mi, 3:3 + CH], PS[:, bank, :], AF.Identity, bias=b_in_c(mi)))

    def emit_conv(c):
        p.wait("vector", "SE", evac_hist[c * 16 + 15])
        if c > 0:
            p.wait("vector", "TE", xdt_te)
        for mi in range(16):
            tsl = mi % 2
            p.op("vector", "VE", lambda e, mi=mi, tsl=tsl: e.tensor_scalar(
                t0b[:, tsl, :], hpre[:, mi, 0:CH], convw_c(mi, 0), convb_c(mi), ALU.mult, ALU.add))
            p.op("vector", "VE", lambda e, mi=mi, tsl=tsl: e.scalar_tensor_tensor(
                t0b[:, tsl, :], hpre[:, mi, 1:1 + CH], convw_c(mi, 1), t0b[:, tsl, :], ALU.mult, ALU.add))
            p.op("vector", "VE", lambda e, mi=mi, tsl=tsl: e.scalar_tensor_tensor(
                t0b[:, tsl, :], hpre[:, mi, 2:2 + CH], convw_c(mi, 2), t0b[:, tsl, :], ALU.mult, ALU.add))
            p.op("vector", "VE", lambda e, mi=mi, tsl=tsl: e.scalar_tensor_tensor(
                hc[:, mi, :], hpre[:, mi, 3:3 + CH], convw_c(mi, 3), t0b[:, tsl, :], ALU.mult, ALU.add))
        conv_v[c] = p.cnt["VE"]
        if c + 1 < NCH:
            p.op("vector", "VE", lambda e: e.tensor_copy(hpre[:, :, 0:3], hpre[:, :, CH:CH + 3]))
        halo_v[c] = p.cnt["VE"]

    def emit_xdt(c):
        col = slice(CH * c, CH * (c + 1))
        p.wait("tensor", "VE", conv_v[c])
        if c > 0:
            p.wait("tensor", "SE", sc_epi)
        p.mm(PS[0:64, 2, :], [(w_xdtT_s[:, k, 0:64], hc[:, k, :]) for k in range(16)])
        tv = p.mm(PS[0:64, 3, :], [(w_xdtT_s[:, k, 64:128], hc[:, k, :]) for k in range(16)])
        p.wait("scalar", "TE", tv)
        p.op("scalar", "SE", lambda e: e.activation(scn[0:64, 0, :], PS[0:64, 2, :], AF.Identity, bias=b_x_col))
        se1 = p.op("scalar", "SE", lambda e: e.activation(scn[0:64, 1, :], PS[0:64, 3, :], AF.Identity, bias=b_dt_col))
        p.op("scalar", "SE", lambda e: e.activation(scn[0:64, 2, :], scn[0:64, 1, :], AF.Abs))
        p.op("scalar", "SE", lambda e: e.activation(scn[0:64, 2, :], scn[0:64, 2, :], AF.Exp, scale=-1.0))
        p.op("scalar", "SE", lambda e: e.activation(scn[0:64, 2, :], scn[0:64, 2, :], AF.Ln, bias=1.0))
        sv = p.op("scalar", "SE", lambda e: e.activation(scn[0:64, 5, :], scn[0:64, 1, :], AF.Relu))
        p.wait("vector", "SE", sv)
        vv = p.op("vector", "VE", lambda e: e.tensor_tensor(scn[0:64, 2, :], scn[0:64, 2, :], scn[0:64, 5, :], ALU.add))
        p.wait("scalar", "VE", vv)
        sv = p.op("scalar", "SE", lambda e: e.activation(scn[0:64, 3, :], scn[0:64, 2, :], AF.Exp, scale=a_col))
        p.wait("vector", "SE", sv)
        p.op("vector", "VE", lambda e: e.tensor_scalar(scn[0:64, 4, :], scn[0:64, 3, :], 1.0, nra_col, ALU.subtract, ALU.mult))
        p.op("vector", "VE", lambda e: e.tensor_tensor(scn[0:64, 4, :], scn[0:64, 4, :], scn[0:64, 0, :], ALU.mult))
        p.op("vector", "VE", lambda e: e.tensor_tensor_scan(
            scn[0:64, 5, :], scn[0:64, 3, :], scn[0:64, 4, :], zcar[0:64, 0:1], ALU.mult, ALU.add))
        p.op("vector", "VE", lambda e: e.tensor_copy(zcar[0:64, 0:1], scn[0:64, 5, CH - 1:CH]))
        p.op("vector", "VE", lambda e, col=col: e.tensor_copy(zb[0:64, col], scn[0:64, 5, :]))
        return tv, se1

    for c in range(NCH):
        emit_win(c)
        if c >= 1:
            xdt_te, sc_epi = emit_xdt(c - 1)
        emit_conv(c)
    xdt_te, sc_epi = emit_xdt(NCH - 1)
    scan_done = p.cnt["VE"]

    if debug and STAGE == 2:
        p.wait("sync", "VE", scan_done)
        p.dma("sync", "SDB", sp_lk[0:64, :], zb[0:64, :])
        p.dma("sync", "SDB", sp_x1[0:128, 0:256], S["aff_tok"][:, :])
        dbg_finish()
        return nc, io

    for nm in ("mask3", "pref3", "u3", "selw", "selwb", "pairs_all",
               "smax", "ssum", "hpre", "hc", "t0b", "scn", "zcar",
               "w_inT", "w_xdtT", "xn_fm"):
        AR.free(nm)

    xin = AR.alloc("xin", [128, 2, 8, D], BF16)
    h_h = AR.alloc("h_h", [128, 4, D], BF16)
    gsil = AR.alloc("gsil", [128, D], BF16)
    y_acc = AR.alloc("y_acc", [128, 8, D])
    wg_s = AR.alloc("wg_s", [128, 2, 8, 512], BF16)
    wu_s = AR.alloc("wu_s", [128, 2, 8, 512], BF16)
    wd_s = AR.alloc("wd_s", [128, 2, 4, D], BF16)
    wdf = AR.alloc("wdf", [128, 2, D])

    SPL = 8
    yscat = []
    gate_ve_end = {}
    trans_end = {}
    mm_end = {}
    dn_mm_end = {}
    split_te_end = {}
    gat_ve = {}
    idxc = lambda ei, j: stg_i[:, ei, j, 0:1]

    te_ph3 = p.cnt["TE"]
    wdf_cast = {}
    p.wait("gpsimd", "VE", scan_done)
    p.wait("sync", "VE", scan_done)
    p.wait("sync", "TE", te_ph3)

    for ei in range(MOE_LIMIT):
        sl = ei % 2
        p.wait("tensor", gath_done[ei][0], gath_done[ei][1])
        if ei >= 2:
            p.wait("tensor", "VE", gat_ve.get(ei - 2, 0))
        ev_prev = 0
        if ei >= 2:
            p.wait("vector", "TE", mm_end.get(ei - 2, te_ph3))
        for j in range(8):
            if j > 0:
                p.wait("tensor", "VE", ev_prev)
            v = p.inc("TE", 1)
            h = p.sems["TE"]
            for cb in range(8):
                if cb < 7:
                    p.emit("tensor", lambda e, sl=sl, j=j, cb=cb: e.transpose(
                        PT[:, 128 * cb:128 * (cb + 1)], xg[:, sl, j, 128 * cb:128 * (cb + 1)], S["sb_iden_b"][:, :]))
                else:
                    p.emit("tensor", lambda e, sl=sl, j=j, cb=cb, h=h: e.transpose(
                        PT[:, 128 * cb:128 * (cb + 1)], xg[:, sl, j, 128 * cb:128 * (cb + 1)], S["sb_iden_b"][:, :]).then_inc(h, 1))
            p.wait("vector", "TE", v)
            for cb in range(8):
                p.op("vector", "VE", lambda e, sl=sl, j=j, cb=cb: e.tensor_copy(
                    xin[:, sl, cb, 128 * j:128 * (j + 1)], PT[:, 128 * cb:128 * (cb + 1)]))
            ev_prev = p.cnt["VE"]
        trans_end[ei] = p.cnt["TE"]
        gat_ve[ei] = p.cnt["VE"]
        xin_ready = p.cnt["VE"]
        p.wait("tensor", "VE", xin_ready)

        def load_split(ei, s):
            slw = s % 2
            sem = "GW2%d" % slw
            p.dma("gpsimd", sem, wg_s[:, slw, :, :],
                  wgT_all[ei, :, 512 * s:512 * (s + 1)].rearrange("(k p) c -> p k c", p=128))
            gws = p.dma("gpsimd", sem, wu_s[:, slw, :, :],
                        wuT_all[ei, :, 512 * s:512 * (s + 1)].rearrange("(k p) c -> p k c", p=128))
            plc = 0
            for hf in range(2):
                p.wait("sync", "PL", wdf_cast.get(0, 0))
                rows = slice(512 * s + 256 * hf, 512 * s + 256 * (hf + 1))
                hws = p.dma("sync", "HW%d" % slw, wdf[:, :, :],
                            wdT_all[ei, rows, :].rearrange("(k p) d -> p k d", p=128))
                p.wait("gpsimd", "HW%d" % slw, hws)
                plc = p.op("gpsimd", "PL", lambda e, slw=slw, hf=hf: e.tensor_copy(
                    wd_s[:, slw, 2 * hf:2 * (hf + 1), :], wdf[:, :, :]))
                wdf_cast[0] = plc
            return (sem, gws, plc)

        gw_split = {}
        p.wait("gpsimd", "TE", split_te_end.get((ei - 1, SPL - 2), te_ph3))
        gw_split[0] = load_split(ei, 0)
        p.wait("gpsimd", "TE", split_te_end.get((ei - 1, SPL - 1), te_ph3))
        gw_split[1] = load_split(ei, 1)
        if ei + 2 < MOE_LIMIT:
            p.wait("gpsimd", "TE", trans_end[ei])
            p.wait("gpsimd", "LDSa%d" % (ei + 2), lds_end[ei + 2])
            gsem = "GG%d" % ((ei + 2) % 2)
            for j in range(8):
                p.igather(gsem, xg[:, (ei + 2) % 2, j, :], sp_xn[:, :], idxc(ei + 2, j))
            gath_done[ei + 2] = (gsem, p.cnt[gsem])
        if ei >= 1:
            if yscat:
                p.wait("gpsimd", "GS2", yscat[-1])
            p.wait("gpsimd", "VE", gate_ve_end[ei - 1])
            for j in range(8):
                yscat.append(p.iscat("GS2", sp_g[:, :], idxc(ei - 1, j), y_acc[:, j, :]))
        gu_evac = {}
        dn_evac = {}
        dn_mm = {}
        for s in range(SPL):
            slw = s % 2
            p.wait("tensor", gw_split[s][0], gw_split[s][1])
            p.wait("tensor", "PL", gw_split[s][2])
            if s > 0:
                pass
            for gidx in range(8):
                mi, th = gidx // 2, gidx % 2
                g_lin = s * 8 + gidx
                gb, ub = g_lin % 2, 2 + g_lin % 2
                if g_lin >= 2:
                    p.wait("tensor", "VE", gu_evac[g_lin - 2])
                if gidx == 0 and s > 0:
                    p.wait("tensor", "VE", dn_evac[(s - 1, 7)])
                tok = slice(512 * th, 512 * (th + 1))
                tvg = p.mm(PS[:, gb, :],
                           [(wg_s[:, slw, k, 128 * mi:128 * (mi + 1)], xin[:, sl, k, tok])
                            for k in range(8)])
                tvu = p.mm(PS[:, ub, :],
                           [(wu_s[:, slw, k, 128 * mi:128 * (mi + 1)], xin[:, sl, k, tok])
                            for k in range(8)])
                p.wait("scalar", "TE", tvg)
                if g_lin >= 2:
                    p.wait("scalar", "VE", gu_evac[g_lin - 2])
                sv = p.op("scalar", "SE", lambda e, gb=gb, tok=tok: e.activation(
                    gsil[:, tok], PS[:, gb, :], AF.Silu))
                p.wait("vector", "SE", sv)
                p.wait("vector", "TE", tvu)
                if gidx == 0 and s > 0:
                    p.wait("vector", "TE", dn_mm[(s - 1, 7)])
                vv = p.op("vector", "VE", lambda e, mi=mi, tok=tok, ub=ub: e.tensor_tensor(
                    h_h[:, mi, tok], gsil[:, tok], PS[:, ub, :], ALU.mult))
                gu_evac[g_lin] = vv
            p.wait("tensor", "VE", gu_evac[s * 8 + 7])
            p.wait("tensor", "VE", gu_evac[s * 8 + 6])
            for j in range(8):
                bk = (4, 5) if j % 2 == 0 else (0, 1)
                if j >= 2:
                    p.wait("tensor", "VE", dn_evac[(s, j - 2)])
                elif s > 0:
                    p.wait("tensor", "VE", dn_evac[(s - 1, 6 + j)])
                tv = None
                for nh in range(2):
                    tv = p.mm(PS[:, bk[nh], :],
                              [(h_h[:, k, 128 * j:128 * (j + 1)],
                                wd_s[:, slw, k, 512 * nh:512 * (nh + 1)]) for k in range(4)])
                dn_mm[(s, j)] = tv
                p.wait("vector", "TE", tv)
                if s == 0 and ei > 0:
                    p.wait("vector", "GS2", yscat[(ei - 1) * 8 + 7])
                pj = PS[:, bk[0]:bk[0] + 2, :].rearrange("p a b -> p (a b)") if bk[0] == 4 else PS[:, 0:2, :].rearrange("p a b -> p (a b)")
                if s == 0:
                    vv = p.op("vector", "VE", lambda e, j=j, pj=pj: e.tensor_copy(y_acc[:, j, :], pj))
                else:
                    vv = p.op("vector", "VE", lambda e, j=j, pj=pj: e.tensor_tensor(y_acc[:, j, :], y_acc[:, j, :], pj, ALU.add))
                dn_evac[(s, j)] = vv
            split_te_end[(ei, s)] = p.cnt["TE"]
            if s + 2 < SPL:
                p.wait("gpsimd", "TE", split_te_end[(ei, s)])
                gw_split[s + 2] = load_split(ei, s + 2)
        mm_end[ei] = p.cnt["TE"]
        p.wait("vector", "TE", dn_mm[(SPL - 1, 7)])
        for j in range(8):
            p.wait("vector", "VE", dn_evac[(SPL - 1, j)])
            p.op("vector", "VE", lambda e, ei=ei, j=j: e.tensor_scalar(
                y_acc[:, j, :], y_acc[:, j, :], stg_f[:, ei, j, 1:2], None, ALU.mult))
        gate_ve_end[ei] = p.cnt["VE"]
    if MOE_LIMIT >= 1:
        if yscat:
            p.wait("gpsimd", "GS2", yscat[-1])
        p.wait("gpsimd", "VE", gate_ve_end[MOE_LIMIT - 1])
        for j in range(8):
            yscat.append(p.iscat("GS2", sp_g[:, :], idxc(MOE_LIMIT - 1, j), y_acc[:, j, :]))

    moe_done = yscat[-1] if yscat else 0

    if debug and STAGE == 4:
        p.wait("sync", "GS2", moe_done)
        dbg_finish()
        return nc, io

    for nm in ("xg", "xin", "h_h", "gsil", "wg_s", "wu_s", "wd_s", "wdf", "y_acc", "stg"):
        AR.free(nm)

    w_outT_s = AR.alloc("w_outT", [128, 16, D], BF16)
    p.wait("gpsimd", "GS2", moe_done)
    for k in range(16):
        p.dma("gpsimd", "GW3", w_outT_s[:, k, :], w_outT[128 * k:128 * (k + 1), :])
    gw_wout = p.cnt["GW3"]

    y0_sb = AR.alloc("y0_sb", [128, 16, 512], BF16)
    s_sb = AR.alloc("s_sb", [128, D])
    g_t = AR.alloc("g_t", [128, 2, D])
    x_t2 = AR.alloc("x_t2", [128, 2, D])
    bcl0_s = AR.alloc("bcl0", [128, D])
    s1_sb = AR.alloc("s1_sb", [128, D])
    ln1w_s = AR.alloc("ln1w", [128, D])
    ln1b_s = AR.alloc("ln1b", [128, D])
    xnl_f = AR.alloc("xnl_f", [128, D])
    xnl_b = AR.alloc("xnl_b", [128, 2, D], BF16)
    xn1_b = AR.alloc("xn1_b", [128, 2, D], BF16)
    p.wait("sync", "GS2", moe_done)
    ld_bc = p.dma("sync", "LDC5", bcl0_s[:, :], bc_l0[:, :])
    p.dma("sync", "LDC6", ln1w_s[:, :], bc_ln1w[:, :])
    ld_ln1 = p.dma("sync", "LDC6", ln1b_s[:, :], bc_ln1b[:, :])
    p.wait("gpsimd", "LDC6", ld_ln1)

    p.wait("tensor", "GW3", gw_wout)
    p.wait("tensor", "VE", scan_done)
    wout_te = 0
    for c in range(NCH):
        col = slice(CH * c, CH * (c + 1))
        for mi in range(16):
            bank = mi % 2
            tv = p.mm(PS[:, bank, :], [(dmat_s[0:64, 128 * mi:128 * (mi + 1)], zb[0:64, col])])
            p.wait("vector", "TE", tv)
            if c > 0 and mi == 0:
                p.wait("vector", "TE", wout_te)
            vv = p.op("vector", "VE", lambda e, mi=mi, bank=bank: e.tensor_copy(
                y0_sb[:, mi, :], PS[:, bank, :]))
            p.wait("tensor", "VE", vv)
        for tt in range(CT):
            it = c * CT + tt
            rows = slice(128 * it, 128 * (it + 1))
            sl = tt % 2
            ldg = p.dma("sync", "LG%d" % sl, g_t[:, sl, :], sp_g[rows, :])
            ldx = p.dma("sync", "L2%d" % sl, x_t2[:, sl, :], x_h[rows, :])
            tv = None
            for nh in range(2):
                tv = p.mm(PS[:, 4 + nh, :],
                          [(y0_sb[:, k, 128 * tt:128 * (tt + 1)],
                            w_outT_s[:, k, 512 * nh:512 * (nh + 1)]) for k in range(16)])
            p.wait("vector", "TE", tv)
            p.wait("vector", "LDC5", ld_bc)
            p.wait("vector", "ST1")
            p.op("vector", "VE", lambda e: e.scalar_tensor_tensor(
                s_sb[:, 0:512], PS[:, 4, :], 1.0, bcl0_s[:, 0:512], ALU.mult, ALU.add))
            vv = p.op("vector", "VE", lambda e: e.scalar_tensor_tensor(
                s_sb[:, 512:1024], PS[:, 5, :], 1.0, bcl0_s[:, 512:1024], ALU.mult, ALU.add))
            p.wait("tensor", "VE", vv)
            p.wait("scalar", "LG%d" % sl, ldg)
            sv = p.op("scalar", "SE", lambda e, sl=sl: e.activation(g_t[:, sl, :], g_t[:, sl, :], AF.Silu))
            p.wait("vector", "SE", sv)
            p.wait("vector", "L2%d" % sl, ldx)
            p.op("vector", "VE", lambda e, sl=sl: e.tensor_tensor(s_sb[:, :], s_sb[:, :], g_t[:, sl, :], ALU.mult))
            vv = p.op("vector", "VE", lambda e, sl=sl: e.tensor_tensor(s_sb[:, :], s_sb[:, :], x_t2[:, sl, :], ALU.add))
            p.wait("sync", "VE", vv)
            p.dma("sync", "ST1", sp_x1[rows, :], s_sb[:, :])
            if it >= 1:
                p.wait("vector", "PL")
            if it >= 2:
                p.wait("gpsimd", "SL%d" % sl)
            lnv = emit_ln(s_sb[:, :], s1_sb[:, :])
            p.wait("gpsimd", "VE", lnv)
            p.op("gpsimd", "PL", lambda e, sl=sl: e.tensor_copy(xnl_b[:, sl, :], s1_sb[:, :]))
            lnv2 = emit_ln(s1_sb[:, :], xnl_f[:, :])
            p.wait("gpsimd", "VE", lnv2)
            p.op("gpsimd", "PL", lambda e: e.tensor_tensor(xnl_f[:, :], xnl_f[:, :], ln1w_s[:, :], ALU.mult))
            plv = p.op("gpsimd", "PL", lambda e, sl=sl: e.tensor_tensor(xn1_b[:, sl, :], xnl_f[:, :], ln1b_s[:, :], ALU.add))
            p.wait("sync", "PL", plv)
            p.dma("sync", "SL%d" % sl, sp_ln[rows, :], xnl_b[:, sl, :])
            p.dma("sync", "SL%d" % sl, sp_ln1[rows, :], xn1_b[:, sl, :])
        wout_te = p.cnt["TE"]
    x1_done = p.cnt["ST1"]

    if debug and STAGE == 5:
        dbg_finish()
        return nc, io

    for nm in ("y0_sb", "g_t", "bcl0", "dmat", "zb", "w_outT", "gate_wT"):
        AR.free(nm)

    w_raw = AR.alloc("w_raw", [128, 8, D])
    absw = AR.alloc("absw", [128, 8 * D])
    for k in range(8):
        ld_w = p.dma("sync", "LDW", w_raw[:, k, :], l1wT[128 * k:128 * (k + 1), :])
    p.wait("scalar", "LDW", ld_w)
    sv = p.op("scalar", "SE", lambda e: e.activation(
        absw[:, :], w_raw.rearrange("p a b -> p (a b)") if False else w_raw[:, :, :], AF.Abs, accum_out=S["stat"][:, 8:9]))
    p.wait("tensor", "SE", sv)
    tv = p.mm(PA[0:1, 0:1], [(ones_c, S["stat"][:, 8:9])])
    p.wait("vector", "TE", tv)
    vv = p.op("vector", "VE", lambda e: e.tensor_copy(S["stat"][0:1, 9:10], PA[0:1, 0:1]))
    p.wait("tensor", "VE", vv)
    tv = p.mm(PA[:, 0:1], [(S["onesrow"][0:1, :], S["stat"][0:1, 9:10])])
    p.wait("vector", "TE", tv)
    p.op("vector", "VE", lambda e: e.tensor_scalar(S["stat"][:, 10:11], PA[:, 0:1], 0.7 / (D * D), None, ALU.mult))
    vv = p.op("vector", "VE", lambda e: e.tensor_scalar(absw[:, :], absw[:, :], S["stat"][:, 10:11], None, ALU.is_gt))
    p.wait("scalar", "VE", vv)
    sv = p.op("scalar", "SE", lambda e: e.sign(w_raw[:, :, :], w_raw[:, :, :]))
    p.wait("vector", "SE", sv)
    wtern = AR.alloc("wtern", [128, 8, D], BF16)
    tern_done = p.op("vector", "VE", lambda e: e.tensor_tensor(
        wtern.rearrange("p a b -> p (a b)") if False else wtern[:, :, :],
        absw[:, :].rearrange("p (a b) -> p a b", b=D), w_raw[:, :, :], ALU.mult))
    AR.free("w_raw")
    AR.free("absw")

    for nm in ("x_t2", "s_sb", "s1_sb", "xnl_f", "xnl_b", "xn1_b", "ln1w", "ln1b"):
        AR.free(nm)
    p.wait("sync", "LDW")
    p.wait("sync", "VE", tern_done)
    p.wait("gpsimd", "LDW")
    p.wait("vector", "LDW")
    p.wait("scalar", "SE")
    p.wait("scalar", "VE", tern_done)
    xn1c7 = AR.alloc("xn1c7", [128, 8, CH], BF16)
    p.wait("scalar", "SL0")
    p.wait("scalar", "SL1")

    wqkT_s = AR.alloc("wqkT", [128, 8, QK], BF16)
    p.wait("gpsimd", "VE")
    p.wait("gpsimd", "SE")
    p.wait("gpsimd", "ST1")
    p.wait("gpsimd", "SL0")
    p.wait("gpsimd", "SL1")
    for k in range(8):
        p.dma("gpsimd", "GW4", wqkT_s[:, k, :], wqkT[128 * k:128 * (k + 1), :])
    gw_qk = p.cnt["GW4"]
    wvT_s = AR.alloc("wvT", [128, 8, HD], BF16)
    for k in range(8):
        p.dma("gpsimd", "GW5", wvT_s[:, k, :], wvT[128 * k:128 * (k + 1), :])
    gw_v = p.cnt["GW5"]

    qk_fm = AR.alloc("qk_fm", [128, 2, CH], BF16)
    p.wait("tensor", "GW4", gw_qk)

    bv_s = AR.alloc("bv_s", [128, HD])
    p.wait("sync", "SL0")
    p.wait("sync", "SL1")
    ld_bv = p.dma("sync", "LDC7", bv_s[:, :], bc_bv[:, :])
    linkv = AR.alloc("linkv", [128, 2, HD])
    p.wait("vector", "SL0")
    p.wait("vector", "SL1")
    p.wait("vector", "ST1")
    p.opq("vector", lambda e: e.memset(linkv[:, :, :], 0.0))
    linkv_zero = p.cnt["VE"]

    v_t = AR.alloc("v_t", [128, 4, HD], BF16)
    lk_b = AR.alloc("lk_b", [128, 2, CH], BF16)
    lk_tok = AR.alloc("lk_tok", [128, 4, QK], BF16)
    hd_t = AR.alloc("hd_t", [128, 2, CH])
    rt_t = AR.alloc("rt_t", [128, 2, CH])
    cs_t = AR.alloc("cs_t", [128, 2, CH])

    p.wait("vector", "LDC7", ld_bv)
    p.wait("sync", "VE")
    p.wait("sync", "SE")
    st_v_tiles = []
    st_lk = 0
    linkv_te = 0
    wqkv_te = 0
    for c in range(NCH):
        col = slice(CH * c, CH * (c + 1))
        if c > 0:
            p.wait("scalar", "TE", wqkv_te)
        for dk in range(8):
            tld_x1c = p.tdma("scalar", "TL2", xn1c7[:, dk, :], sp_ln1[col, 128 * dk:128 * (dk + 1)])
        p.wait("tensor", "TL2", tld_x1c)
        if c > 0:
            p.wait("scalar", "SQK", 16 * 2 * c)
        for m2 in range(2):
            bank = m2 % 2
            tv = p.mm(PS[:, bank, :],
                      [(wqkT_s[:, k, 128 * m2:128 * (m2 + 1)], xn1c7[:, k, :])
                       for k in range(8)])
            p.wait("scalar", "TE", tv)
            sv = p.op("scalar", "SE", lambda e, m2=m2, bank=bank: e.activation(
                qk_fm[:, m2, :], PS[:, bank, :], AF.Silu, bias=bqk_c(m2)))
            p.wait("tensor", "SE", sv)
        qk_done_c = p.cnt["SE"]
        p.wait("sync", "SE", qk_done_c)
        p.dma("sync", "SQK", sp_qk[0:128, col], qk_fm[:, 0, :])
        p.dma("sync", "SQK", sp_qk[128:256, col], qk_fm[:, 1, :])
        p.wait("vector", "SE", qk_done_c)
        ld_cs = p.dma("sync", "LC7", cs_t[:, 0, :], cos_h[:, col])
        ld_cs = p.dma("sync", "LC7", cs_t[:, 1, :], sin_h[:, col])
        p.wait("vector", "LC7", ld_cs)
        if c > 0:
            p.wait("vector", "SLK", st_lk)
        for cc in range(2):
            p.op("vector", "VE", lambda e, cc=cc: e.tensor_scalar(
                hd_t[:, cc, :], qk_fm[:, cc, :], gamma_c(3, cc), beta_c(3, cc), ALU.mult, ALU.add))
        p.op("vector", "VE", lambda e: e.tensor_tensor(rt_t[:, 0, :], hd_t[:, 0, :], cs_t[:, 0, :], ALU.mult))
        p.op("vector", "VE", lambda e: e.tensor_tensor(rt_t[:, 1, :], hd_t[:, 1, :], cs_t[:, 1, :], ALU.mult))
        p.op("vector", "VE", lambda e: e.tensor_tensor(lk_b[:, 0, :], rt_t[:, 0, :], rt_t[:, 1, :], ALU.subtract))
        p.op("vector", "VE", lambda e: e.tensor_tensor(rt_t[:, 0, :], hd_t[:, 1, :], cs_t[:, 0, :], ALU.mult))
        p.op("vector", "VE", lambda e: e.tensor_tensor(rt_t[:, 1, :], hd_t[:, 0, :], cs_t[:, 1, :], ALU.mult))
        vv = p.op("vector", "VE", lambda e: e.tensor_tensor(lk_b[:, 1, :], rt_t[:, 0, :], rt_t[:, 1, :], ALU.add))
        p.wait("sync", "VE", vv)
        p.dma("sync", "SLK", sp_lk[0:128, col], lk_b[:, 0, :])
        st_lk = p.dma("sync", "SLK", sp_lk[128:256, col], lk_b[:, 1, :])
        p.wait("tensor", "GW5", gw_v)
        if c > 0:
            p.wait("scalar", "TE", linkv_te)
            p.wait("scalar", "SV", 16 * 4 * c)
        for tt in range(CT):
            tcol = slice(128 * tt, 128 * (tt + 1))
            for nb in range(4):
                bank = nb % 2
                tv = p.mm(PS[:, bank, :],
                          [(xn1c7[:, k, tcol], wvT_s[:, k, 512 * nb:512 * (nb + 1)])
                           for k in range(8)])
                p.wait("vector", "TE", tv)
                p.op("vector", "VE", lambda e, nb=nb, bank=bank: e.scalar_tensor_tensor(
                    rt_t[:, 0, 0:512], PS[:, bank, :], 1.0, bv_s[:, 512 * nb:512 * (nb + 1)], ALU.mult, ALU.add))
                vv = p.cnt["VE"]
                p.wait("scalar", "VE", vv)
                sv = p.op("scalar", "SE", lambda e, tt=tt, nb=nb: e.activation(
                    v_t[:, tt, 512 * nb:512 * (nb + 1)], rt_t[:, 0, 0:512], AF.Silu))
                p.wait("tensor", "SE", sv)
                p.wait("vector", "SE", sv)
            p.wait("sync", "SE")
            it = c * CT + tt
            st_v_tiles.append(p.dma("sync", "SV", sp_v[128 * it:128 * (it + 1), :], v_t[:, tt, :]))
        p.wait("scalar", "SLK", st_lk)
        if c > 0:
            p.wait("scalar", "TE", linkv_te)
        tld = None
        for tt in range(CT):
            it = c * CT + tt
            tld = p.tdma("scalar", "TL3", lk_tok[:, tt, :], sp_lk[0:QK, 128 * it:128 * (it + 1)])
        p.wait("vector", "VE", linkv_zero)
        p.wait("tensor", "TL3", tld)
        p.wait("tensor", "SE")
        for dh in range(2):
            for nq in range(4):
                tv = p.mm(PA[:, 0:512],
                          [(lk_tok[:, tt, 128 * dh:128 * (dh + 1)], v_t[:, tt, 512 * nq:512 * (nq + 1)])
                           for tt in range(CT)])
                p.wait("vector", "TE", tv)
                vv = p.op("vector", "VE", lambda e, dh=dh, nq=nq: e.tensor_tensor(
                    linkv[:, dh, 512 * nq:512 * (nq + 1)], linkv[:, dh, 512 * nq:512 * (nq + 1)],
                    PA[:, 0:512], ALU.add))
                p.wait("tensor", "VE", vv)
        linkv_te = p.cnt["TE"]
        wqkv_te = p.cnt["TE"]
    linkv_b = AR.alloc("linkv_b", [128, 2, HD], BF16)
    p.op("vector", "VE", lambda e: e.tensor_scalar(linkv_b[:, 0, :], linkv[:, 0, :], 1.0 / T, None, ALU.mult))
    linkv_done = p.op("vector", "VE", lambda e: e.tensor_scalar(linkv_b[:, 1, :], linkv[:, 1, :], 1.0 / T, None, ALU.mult))

    if debug and STAGE == 6:
        dbg_finish()
        return nc, io

    for nm in ("wvT", "v_t", "lk_b", "hd_t", "rt_t", "cs_t", "linkv",
               "xn1c7", "wqkT", "bv_s", "qk_fm", "lk_tok"):
        AR.free(nm)

    woT_s = AR.alloc("woT", [128, 16, D], BF16)
    p.wait("gpsimd", "TE")
    p.wait("gpsimd", "VE")
    p.wait("gpsimd", "SE")
    p.wait("gpsimd", "SV")
    p.wait("gpsimd", "SLK")
    p.wait("gpsimd", "SQK")
    for k in range(16):
        p.dma("gpsimd", "GW6", woT_s[:, k, :], woT[128 * k:128 * (k + 1), :])
    gw_wo = p.cnt["GW6"]
    wgT1_s = AR.alloc("wgT1", [128, 8, HD], BF16)
    for k in range(8):
        p.dma("gpsimd", "GW7", wgT1_s[:, k, :], wgT1[128 * k:128 * (k + 1), :])
    gw_wg1 = p.cnt["GW7"]

    ln2w_s = AR.alloc("ln2w", [128, D])
    ln2b_s = AR.alloc("ln2b", [128, D])
    bo_s = AR.alloc("bo_s", [128, D])
    l1b_s = AR.alloc("l1b_s", [128, D])
    p.wait("sync", "TE")
    p.wait("sync", "VE")
    p.wait("sync", "SV")
    p.wait("sync", "SLK")
    p.wait("sync", "SQK")
    p.dma("sync", "LDC8", ln2w_s[:, :], bc_ln2w[:, :])
    p.dma("sync", "LDC8", ln2b_s[:, :], bc_ln2b[:, :])
    p.dma("sync", "LDC8", bo_s[:, :], bc_bo[:, :])
    ld_bc2 = p.dma("sync", "LDC8", l1b_s[:, :], bc_l1b[:, :])

    xn1fc = AR.alloc("xn1fc", [128, 8, CH], BF16)
    xnfc = AR.alloc("xnfc", [128, 8, CH], BF16)
    v_tc = AR.alloc("v_tc", [128, 4, HD], BF16)
    x1c = AR.alloc("x1c", [128, 2, D])
    qkc = AR.alloc("qkc", [128, 2, CH], BF16)
    qq_b = AR.alloc("qq_b", [128, 2, CH], BF16)
    qkk_b = AR.alloc("qkk_b", [128, 2, CH], BF16)
    lq_b = AR.alloc("lq_b", [128, 2, CH], BF16)
    hd8 = AR.alloc("hd8", [128, 2, CH])
    rt8 = AR.alloc("rt8", [128, 2, CH])
    cs8 = AR.alloc("cs8", [128, 2, CH])
    att_sb = AR.alloc("att_sb", [128, 4, 128], BF16)
    u_sb = AR.alloc("u_sb", [128, 16, CH], BF16)
    gate_b = AR.alloc("gate_b", [128, 512], BF16)
    g1 = AR.alloc("g1", [128, D])
    attf = AR.alloc("attf", [128, 64])
    o1 = rt8.rearrange("p a b -> p (a b)")
    s_c = cs8.rearrange("p a b -> p (a b)")

    p.op("vector", "VE", lambda e: e.memset(att_sb[:, :, :], 0.0))
    p.wait("vector", "LDC8", ld_bc2)
    p.wait("tensor", "GW7", gw_wg1)
    p.wait("tensor", "GW6", gw_wo)

    p.wait("scalar", "TE")
    p.wait("scalar", "VE")
    p.wait("scalar", "SV")
    p.wait("scalar", "SLK")
    p.wait("scalar", "SQK")
    p.wait("sync", "TE")
    p.wait("sync", "VE")
    sto_all = []
    x1_used = {}
    te_chunk_end = 0
    ve_comb_end = 0
    for c in range(NCH):
        col = slice(CH * c, CH * (c + 1))
        if c > 0:
            p.wait("scalar", "TE", te_chunk_end)
            p.wait("sync", "TE", te_chunk_end)
            p.wait("sync", "VE", ve_comb_end)
        for dk in range(8):
            p.tdma("scalar", "TL4", xn1fc[:, dk, :], sp_ln1[col, 128 * dk:128 * (dk + 1)])
        tld_c1 = p.cnt["TL4"]
        for dk in range(8):
            p.tdma("scalar", "TL4", xnfc[:, dk, :], sp_ln[col, 128 * dk:128 * (dk + 1)])
        tld_c2 = p.cnt["TL4"]
        p.wait("sync", "SV")
        p.wait("sync", "SQK")
        for tt in range(CT):
            it = c * CT + tt
            p.dma("sync", "L8", v_tc[:, tt, :], sp_v[128 * it:128 * (it + 1), :])
        p.dma("sync", "L8", qkc[:, 0, :], sp_qk[0:128, col])
        p.dma("sync", "L8", qkc[:, 1, :], sp_qk[128:256, col])
        p.dma("sync", "L8", cs8[:, 0, :], cos_h[:, col])
        ld_cs = p.dma("sync", "L8", cs8[:, 1, :], sin_h[:, col])

        if c > 0:
            p.wait("vector", "TE", te_chunk_end)
        p.wait("vector", "L8", ld_cs)
        for hsel, dst in ((0, qq_b), (2, qkk_b), (1, lq_b)):
            for cc in range(2):
                p.op("vector", "VE", lambda e, cc=cc, hsel=hsel: e.tensor_scalar(
                    hd8[:, cc, :], qkc[:, cc, :], gamma_c(hsel, cc), beta_c(hsel, cc), ALU.mult, ALU.add))
            p.op("vector", "VE", lambda e: e.tensor_tensor(rt8[:, 0, :], hd8[:, 0, :], cs8[:, 0, :], ALU.mult))
            p.op("vector", "VE", lambda e: e.tensor_tensor(rt8[:, 1, :], hd8[:, 1, :], cs8[:, 1, :], ALU.mult))
            p.op("vector", "VE", lambda e, dst=dst: e.tensor_tensor(dst[:, 0, :], rt8[:, 0, :], rt8[:, 1, :], ALU.subtract))
            p.op("vector", "VE", lambda e: e.tensor_tensor(rt8[:, 0, :], hd8[:, 1, :], cs8[:, 0, :], ALU.mult))
            p.op("vector", "VE", lambda e: e.tensor_tensor(rt8[:, 1, :], hd8[:, 0, :], cs8[:, 1, :], ALU.mult))
            p.op("vector", "VE", lambda e, dst=dst: e.tensor_tensor(dst[:, 1, :], rt8[:, 0, :], rt8[:, 1, :], ALU.add))
        rope_done = p.cnt["VE"]
        p.wait("tensor", "VE", rope_done)

        for pr in range(CT):
            p.mm(PA[0:64, 0:64], [(qkk_b[:, k, 128 * pr:128 * pr + 64],
                                   qq_b[:, k, 128 * pr:128 * pr + 64]) for k in range(2)])
            tv = p.mm(PA[64:128, 64:128], [(qkk_b[:, k, 128 * pr + 64:128 * (pr + 1)],
                                            qq_b[:, k, 128 * pr + 64:128 * (pr + 1)]) for k in range(2)],
                      tile_position=(0, 64))
            p.wait("vector", "TE", tv)
            sl = pr
            p.op("vector", "VE", lambda e, sl=sl: e.scalar_tensor_tensor(
                attf[0:64, :], PA[0:64, 0:64], 1.0 / G, S["sb_pos_bias"][0:64, :], ALU.mult, ALU.add))
            vv = p.op("vector", "VE", lambda e, sl=sl: e.scalar_tensor_tensor(
                attf[64:128, :], PA[64:128, 64:128], 1.0 / G, S["sb_pos_bias"][64:128, :], ALU.mult, ALU.add))
            p.wait("scalar", "VE", vv)
            p.op("scalar", "SE", lambda e: e.activation(attf[:, :], attf[:, :], AF.Relu))
            sv = p.op("scalar", "SE", lambda e, sl=sl: e.activation(attf[:, :], attf[:, :], AF.Square))
            p.wait("vector", "SE", sv)
            vv = p.op("vector", "VE", lambda e, sl=sl: e.tensor_copy(att_sb[0:64, sl, 0:64], attf[0:64, :]))
            vv = p.op("vector", "VE", lambda e, sl=sl: e.tensor_copy(att_sb[64:128, sl, 64:128], attf[64:128, :]))
            p.wait("tensor", "VE", vv)
        p.wait("tensor", "L8", ld_cs)
        p.wait("tensor", "TL4", tld_c2)
        u_evac = {}
        for m in range(16):
            bank = m % 2
            gbank = 2 + (m % 2)
            if m >= 2:
                p.wait("tensor", "VE", u_evac[m - 2])
            for pr in range(CT):
                p.mm(PS[:, bank, 128 * pr:128 * (pr + 1)],
                     [(v_tc[:, pr, 128 * m:128 * (m + 1)], att_sb[:, pr, :])] +
                     [(linkv_b[:, k, 128 * m:128 * (m + 1)], lq_b[:, k, 128 * pr:128 * (pr + 1)])
                      for k in range(2)])
            tvqo = p.cnt["TE"]
            tvg = p.mm(PS[:, gbank, :],
                       [(wgT1_s[:, k, 128 * m:128 * (m + 1)], xn1fc[:, k, :]) for k in range(8)])
            p.wait("scalar", "TE", tvg)
            if m >= 1:
                p.wait("scalar", "VE", u_evac[m - 1])
            sv = p.op("scalar", "SE", lambda e, gbank=gbank, m=m: e.activation(
                gate_b[:, :], PS[:, gbank, :], AF.Silu, bias=bg_c(m)))
            p.wait("vector", "SE", sv)
            p.wait("vector", "TE", tvg)
            vv = p.op("vector", "VE", lambda e, m=m, bank=bank: e.tensor_tensor(
                u_sb[:, m, :], gate_b[:, :], PS[:, bank, :], ALU.mult))
            u_evac[m] = vv
        p.wait("tensor", "VE", u_evac[15])
        p.wait("tensor", "TL4", tld_c2)
        for tt in range(CT):
            it = c * CT + tt
            rows = slice(128 * it, 128 * (it + 1))
            if it >= 2:
                p.wait("sync", "VE", x1_used[it - 2])
            ldx1 = p.dma("sync", "LX1C%d" % (tt % 2), x1c[:, tt % 2, :], sp_x1[rows, :])
            tv = None
            for nh in range(2):
                tv = p.mm(PS[:, 4 + nh, :],
                          [(u_sb[:, k, 128 * tt:128 * (tt + 1)],
                            woT_s[:, k, 512 * nh:512 * (nh + 1)]) for k in range(16)])
            p.wait("vector", "TE", tv)
            p.op("vector", "VE", lambda e: e.scalar_tensor_tensor(
                o1[:, 0:512], PS[:, 4, :], 1.0, bo_s[:, 0:512], ALU.mult, ALU.add))
            vv = p.op("vector", "VE", lambda e: e.scalar_tensor_tensor(
                o1[:, 512:1024], PS[:, 5, :], 1.0, bo_s[:, 512:1024], ALU.mult, ALU.add))
            p.wait("tensor", "VE", vv)
            if sto_all:
                p.wait("scalar", "STO", sto_all[-1])
                p.wait("vector", "STO", sto_all[-1])
            emit_ln(o1[:, :], g1[:, :])
            p.op("vector", "VE", lambda e: e.tensor_tensor(g1[:, :], g1[:, :], ln2w_s[:, :], ALU.mult))
            p.op("vector", "VE", lambda e: e.tensor_tensor(s_c[:, :], g1[:, :], ln2b_s[:, :], ALU.add))
            tv = None
            for nh in range(2):
                tv = p.mm(PS[:, 0 + nh, :],
                          [(xnfc[:, k, 128 * tt:128 * (tt + 1)],
                            wtern[:, k, 512 * nh:512 * (nh + 1)]) for k in range(8)])
            p.wait("vector", "TE", tv)
            if sto_all:
                p.wait("vector", "STO", sto_all[-1])
            p.op("vector", "VE", lambda e: e.tensor_scalar(g1[:, 0:512], PS[:, 0, :], alpha_c, None, ALU.mult))
            p.op("vector", "VE", lambda e: e.tensor_scalar(g1[:, 512:1024], PS[:, 1, :], alpha_c, None, ALU.mult))
            vv = p.op("vector", "VE", lambda e: e.tensor_tensor(g1[:, :], g1[:, :], l1b_s[:, :], ALU.add))
            p.wait("tensor", "VE", vv)
            p.wait("scalar", "VE", vv)
            sv = p.op("scalar", "SE", lambda e: e.activation(g1[:, :], g1[:, :], AF.Silu))
            p.wait("vector", "SE", sv)
            p.op("vector", "VE", lambda e: e.tensor_tensor(g1[:, :], g1[:, :], s_c[:, :], ALU.mult))
            p.wait("vector", "LX1C%d" % (tt % 2), ldx1)
            vv = p.op("vector", "VE", lambda e, tt=tt: e.tensor_tensor(g1[:, :], g1[:, :], x1c[:, tt % 2, :], ALU.add))
            x1_used[it] = vv
            p.wait("sync", "VE", vv)
            sto_all.append(p.dma("sync", "STO", out_h[rows, :], g1[:, :]))
        te_chunk_end = p.cnt["TE"]
        ve_comb_end = p.cnt["VE"]
    p.wait("sync", "STO", sto_all[-1])

    emit(nc, p)
    return nc, io


def emit(nc, p):
    with nc.Block() as blk:
        @blk.sync
        def _(e):
            for fn in p.ops["sync"]:
                fn(e)

        @blk.scalar
        def _(e):
            for fn in p.ops["scalar"]:
                fn(e)

        @blk.vector
        def _(e):
            for fn in p.ops["vector"]:
                fn(e)

        @blk.tensor
        def _(e):
            for fn in p.ops["tensor"]:
                fn(e)

        @blk.gpsimd
        def _(e):
            for fn in p.ops["gpsimd"]:
                fn(e)


def host_inputs(inputs, core):
    i = inputs
    d = {}
    d["x"] = i["L0_attn_x"][core]
    d["w_inT"] = i["L0_attn_W_in"].T
    d["w_xdtT"] = np.concatenate([i["L0_attn_W_x"].T, i["L0_attn_W_dt"].T], axis=1)
    d["d_mat"] = i["L0_attn_D_mat"]
    d["w_outT"] = i["L0_attn_W_out"].T
    d["gate_wT"] = i["L0_ffn_gate_w"].T
    d["wgT_all"] = i["L0_ffn_w_gate"].transpose(0, 2, 1)
    d["wuT_all"] = i["L0_ffn_w_up"].transpose(0, 2, 1)
    d["wdT_all"] = i["L0_ffn_w_down"].transpose(0, 2, 1)
    d["wqkT"] = i["L1_attn_Wqk"].T
    d["wvT"] = i["L1_attn_Wv"].T
    d["wgT1"] = i["L1_attn_Wg"].T
    d["woT"] = i["L1_attn_Wo"].T
    d["l1wT"] = i["L1_ffn_weight"].T

    cs = np.zeros((128, 64), np.float32)
    cs[:, 0:16] = i["L0_attn_b_in"].reshape(16, 128).T
    cs[:, 16:32] = i["L0_attn_conv_b"].reshape(16, 128).T
    cs[:, 32:34] = i["L1_attn_bqk"].reshape(2, 128).T
    cs[:, 34:50] = i["L1_attn_bg"].reshape(16, 128).T
    cs[:, 50:58] = i["L1_attn_gamma"].reshape(4, 2, 128).transpose(2, 0, 1).reshape(128, 8)
    d["cst_small"] = cs
    cs2 = np.zeros((128, 64), np.float32)
    cs2[:, 0:8] = i["L1_attn_beta"].reshape(4, 2, 128).transpose(2, 0, 1).reshape(128, 8)
    cs2[:, 8:9] = np.float32(np.asarray(i["L1_ffn_alpha"]).reshape(1)[0])
    cs2[:, 9:10] = 1.0
    cs2[:, 10:42] = np.arange(T, dtype=np.float32).reshape(NT, 128).T
    cs2[:, 42] = np.arange(128, dtype=np.float32) * T
    cs2[:, 43:51] = (np.arange(NE, dtype=np.float32) * T + KSEL - 1)[None, :]
    d["cst_small2"] = cs2
    d["ones128b"] = np.ones((128, 128), np.float32).astype(BF)
    d["ltonesb"] = np.triu(np.ones((128, 128), np.float32), 1).astype(BF)
    d["convw_fm"] = i["L0_attn_conv_w"][:, 0, :].reshape(16, 128, 4).transpose(1, 0, 2).reshape(128, 64)
    c64 = np.zeros((64, 8), np.float32)
    c64[:, 0] = i["L0_attn_b_x"]
    c64[:, 1] = i["L0_attn_b_dt"]
    c64[:, 2] = i["L0_attn_A_log"]
    d["cst64"] = c64
    d["bc_l0"] = np.tile(i["L0_attn_b_out"][None, :], (128, 1))
    d["bc_ln1w"] = np.tile(i["L1_attn_ln1_w"][None, :], (128, 1))
    d["bc_ln1b"] = np.tile(i["L1_attn_ln1_b"][None, :], (128, 1))
    d["bc_ln2w"] = np.tile(i["L1_attn_ln2_w"][None, :], (128, 1))
    d["bc_ln2b"] = np.tile(i["L1_attn_ln2_b"][None, :], (128, 1))
    d["bc_bo"] = np.tile(i["L1_attn_bo"][None, :], (128, 1))
    d["bc_l1b"] = np.tile(i["L1_ffn_bias"][None, :], (128, 1))
    d["bc_bv"] = np.tile(i["L1_attn_bv"][None, :], (128, 1))
    idx = np.arange(G)
    dist = np.clip(np.abs(idx[:, None] - idx[None, :]), 0, 32)
    pb = i["L1_attn_pos_emb"][dist].astype(np.float32)
    d["pos_bias"] = np.concatenate([pb, pb], axis=0)
    d["iden_b"] = np.eye(128, dtype=np.float32).astype(BF)
    half = 128
    inv_freq = 10000.0 ** (-np.arange(half, dtype=np.float32) / half)
    sinus = np.arange(T, dtype=np.float32)[None, :] * inv_freq[:, None]
    d["cos_fm"] = np.cos(sinus)
    d["sin_fm"] = np.sin(sinus)
    for k in list(d):
        a = np.asarray(d[k])
        if a.dtype == np.float64:
            a = a.astype(np.float32)
        d[k] = np.ascontiguousarray(a)
    return d


def kernel(**inputs):
    nc, io = build(debug=False)
    in_maps = [host_inputs(inputs, c) for c in range(8)]
    res = run_bass_kernel_spmd(nc, in_maps, core_ids=list(range(8)))
    return np.concatenate([r["out"][None] for r in res.results], axis=0)


# version 10
# speedup vs baseline: 1.0129x; 1.0010x over previous
import numpy as np
import ml_dtypes
import concourse.bass as bass
import concourse.mybir as mybir
from concourse.bass_utils import run_bass_kernel_spmd

F32 = mybir.dt.float32
BF16 = mybir.dt.bfloat16
I32 = mybir.dt.int32
AF = mybir.ActivationFunctionType
ALU = mybir.AluOpType
AX = mybir.AxisListType
BF = ml_dtypes.bfloat16

T = 4096
D = 1024
DI = 2048
DS = 64
NE = 8
DF = 4096
KSEL = 1024
NT = T // 128
NCH = 8
CH = T // NCH
CT = CH // 128
QK = 256
HD = 2048
G = 64
EPS = 1e-5
BISECT = 36

ENGS = ("sync", "scalar", "vector", "tensor", "gpsimd")
MOE_LIMIT = NE
MOE_PARTS = "all"

STAGE = 7


def ceil(a, b):
    return -(-a // b)


class Plan:
    def __init__(self, nc):
        self.nc = nc
        self.ops = {e: [] for e in ENGS}
        self.sems = {}
        self.cnt = {}

    def sem(self, name):
        if name not in self.sems:
            self.sems[name] = self.nc.alloc_semaphore(name)
            self.cnt[name] = 0
        return self.sems[name]

    def emit(self, eng, fn):
        self.ops[eng].append(fn)

    def inc(self, name, amt):
        self.sem(name)
        self.cnt[name] += amt
        return self.cnt[name]

    def wait(self, eng, name, val=None):
        h = self.sem(name)
        v = self.cnt[name] if val is None else val
        if v <= 0:
            return
        self.emit(eng, lambda e, h=h, v=v: e.wait_ge(h, v))

    def dma(self, eng, sem_name, out, in_, **kw):
        v = self.inc(sem_name, 16)
        h = self.sems[sem_name]
        self.emit(eng, lambda e, out=out, in_=in_, h=h, kw=dict(kw):
                  e.dma_start(out=out, in_=in_, **kw).then_inc(h, 16))
        return v

    def tdma(self, eng, sem_name, out, in_):
        v = self.inc(sem_name, 16)
        h = self.sems[sem_name]
        self.emit(eng, lambda e, out=out, in_=in_, h=h:
                  e.dma_start(out=out, in_=in_, transpose=True).then_inc(h, 16))
        return v

    def igather(self, sem_name, out, in_full, idx_ap):
        v = self.inc(sem_name, 16)
        h = self.sems[sem_name]
        self.emit("gpsimd", lambda e, out=out, i=in_full, x=idx_ap, h=h:
                  e.indirect_dma_start(out, None, i,
                                       bass.IndirectOffsetOnAxis(x, 0)).then_inc(h, 16))
        return v

    def iscat(self, sem_name, out_full, idx_ap, in_):
        v = self.inc(sem_name, 16)
        h = self.sems[sem_name]
        self.emit("gpsimd", lambda e, o=out_full, x=idx_ap, i=in_, h=h:
                  e.indirect_dma_start(o, bass.IndirectOffsetOnAxis(x, 0),
                                       i, None, compute_op=ALU.add).then_inc(h, 16))
        return v

    def op(self, eng, sem_name, fn):
        prev = self.cnt.get(sem_name, 0) if eng in ("vector", "scalar", "gpsimd") else 0
        v = self.inc(sem_name, 1)
        h = self.sems[sem_name]
        if prev > 0:
            self.emit(eng, lambda e, fn=fn, h=h, prev=prev:
                      fn(e)._wait_ge(h, prev).then_inc(h, 1))
        else:
            self.emit(eng, lambda e, fn=fn, h=h: fn(e).then_inc(h, 1))
        return v

    def opq(self, eng, fn):
        if eng == "vector":
            return self.op(eng, "VE", fn)
        if eng == "scalar":
            return self.op(eng, "SE", fn)
        self.emit(eng, lambda e, fn=fn: fn(e))

    def mm(self, out_ap, pairs, sem="TE", tile_position=None, first_start=True):
        n = len(pairs)
        v = self.inc(sem, 1)
        h = self.sems[sem]
        for j, (l, r) in enumerate(pairs):
            st = (j == 0) and first_start
            if j == n - 1:
                self.emit("tensor", lambda e, l=l, r=r, st=st, h=h, tp=tile_position:
                          e.matmul(out_ap, l, r, start=st, stop=True,
                                   tile_position=tp).then_inc(h, 1))
            else:
                self.emit("tensor", lambda e, l=l, r=r, st=st, tp=tile_position:
                          e.matmul(out_ap, l, r, start=st, stop=False,
                                   tile_position=tp))
        return v

    def tr(self, out_psum, in_ap, ident, sem="TE"):
        v = self.inc(sem, 1)
        h = self.sems[sem]
        self.emit("tensor", lambda e, o=out_psum, i=in_ap, d=ident, h=h:
                  e.transpose(o, i, d).then_inc(h, 1))
        return v


class Arena:

    def __init__(self, nc, name, n_f32):
        self.t = nc.alloc_sbuf_tensor(name, [128, n_f32], F32)
        self.free_list = [(0, n_f32)]
        self.allocs = {}
        self.hwm = 0

    def alloc(self, name, shape, dtype=F32):
        free = 1
        for s in shape[1:]:
            free *= s
        f32n = ceil(free, 2) if dtype == BF16 else free
        f32n = ceil(f32n, 16) * 16
        for i, (off, sz) in enumerate(self.free_list):
            if sz >= f32n:
                if sz == f32n:
                    self.free_list.pop(i)
                else:
                    self.free_list[i] = (off + f32n, sz - f32n)
                self.allocs[name] = (off, f32n)
                self.hwm = max(self.hwm, off + f32n)
                ap = self.t[0:128, off:off + f32n]
                if dtype != F32:
                    ap = ap.bitcast(dtype)
                ap = ap[0:shape[0], 0:free]
                if len(shape) == 3:
                    ap = ap.rearrange("p (a b) -> p a b", b=shape[2])
                elif len(shape) == 4:
                    ap = ap.rearrange("p (a b c) -> p a b c", b=shape[2], c=shape[3])
                return ap
        raise AssertionError(f"arena OOM for {name} ({f32n} f32/partition)")

    def view(self, name, shape, dtype=F32):
        off, f32n = self.allocs[name]
        free = 1
        for s in shape[1:]:
            free *= s
        ap = self.t[0:128, off:off + f32n]
        if dtype != F32:
            ap = ap.bitcast(dtype)
        ap = ap[0:shape[0], 0:free]
        if len(shape) == 3:
            ap = ap.rearrange("p (a b) -> p a b", b=shape[2])
        elif len(shape) == 4:
            ap = ap.rearrange("p (a b c) -> p a b c", b=shape[2], c=shape[3])
        return ap

    def free(self, name):
        off, sz = self.allocs.pop(name)
        self.free_list.append((off, sz))
        self.free_list.sort()
        merged = []
        for o, s in self.free_list:
            if merged and merged[-1][0] + merged[-1][1] == o:
                merged[-1] = (merged[-1][0], merged[-1][1] + s)
            else:
                merged.append((o, s))
        self.free_list = [tuple(x) for x in merged]


def build(debug=False):
    nc = bass.Bass("TRN2", target_bir_lowering=False)
    io = {}

    def din(name, shape, dtype=F32):
        io[name] = nc.dram_tensor(name, list(shape), dtype, kind="ExternalInput").ap()
        return io[name]

    def dint(name, shape, dtype=F32):
        kind = "ExternalOutput" if debug else "Internal"
        io[name] = nc.dram_tensor(name, list(shape), dtype, kind=kind).ap()
        return io[name]

    x_h = din("x", (T, D))
    out_h = nc.dram_tensor("out", [T, D], F32, kind="ExternalOutput").ap()
    io["out"] = out_h

    w_inT = din("w_inT", (D, DI))
    w_xdtT = din("w_xdtT", (DI, 128))
    d_mat = din("d_mat", (DS, DI))
    w_outT = din("w_outT", (DI, D))
    gate_wT = din("gate_wT", (D, NE))
    wgT_all = din("wgT_all", (NE, D, DF))
    wuT_all = din("wuT_all", (NE, D, DF))
    wdT_all = din("wdT_all", (NE, DF, D))
    wqkT = din("wqkT", (D, QK))
    wvT = din("wvT", (D, HD))
    wgT1 = din("wgT1", (D, HD))
    woT = din("woT", (HD, D))
    l1wT = din("l1wT", (D, D))
    cst_small = din("cst_small", (128, 64))
    cst_small2 = din("cst_small2", (128, 64))
    convw_h = din("convw_fm", (128, 64))
    cst64 = din("cst64", (64, 8))
    bc_l0 = din("bc_l0", (128, D))
    bc_ln1w = din("bc_ln1w", (128, D))
    bc_ln1b = din("bc_ln1b", (128, D))
    bc_ln2w = din("bc_ln2w", (128, D))
    bc_ln2b = din("bc_ln2b", (128, D))
    bc_bo = din("bc_bo", (128, D))
    bc_l1b = din("bc_l1b", (128, D))
    bc_bv = din("bc_bv", (128, HD))
    pos_bias_h = din("pos_bias", (128, 64))
    iden_b_h = din("iden_b", (128, 128), BF16)
    ones128_h = din("ones128b", (128, 128), BF16)
    ltones_h = din("ltonesb", (128, 128), BF16)
    cos_h = din("cos_fm", (128, T))
    sin_h = din("sin_fm", (128, T))

    sp_xn = dint("sp_xn", (T, D), BF16)
    sp_g = dint("sp_g", (T, D), F32)
    cmp_h = dint("cmp", (NE * T, 2), I32)
    sp_x1 = dint("sp_x1", (T, D), F32)
    sp_ln = dint("sp_ln", (T, D), BF16)
    sp_ln1 = dint("sp_ln1", (T, D), BF16)
    sp_v = dint("sp_v", (T, HD), BF16)
    sp_lk = dint("sp_lk", (QK, T), BF16)
    sp_qk = dint("sp_qk", (QK, T), BF16)

    p = Plan(nc)

    S = {}

    def sbt(name, shape, dtype=F32):
        S[name] = nc.alloc_sbuf_tensor(name, list(shape), dtype)
        return S[name]

    sbt("sb_iden_b", (128, 128), BF16)
    sbt("sb_cst_small", (128, 64))
    sbt("sb_cst_small2", (128, 64))
    sbt("convw", (128, 64))
    sbt("sb_cst64", (64, 8))
    sbt("stat", (128, 16))
    sbt("onesrow", (1, 128))
    sbt("zero2k", (128, 1024))
    sbt("aff_tok", (128, NT * NE))
    sbt("idxf_tok", (128, NT * NE))
    sbt("idxf_tok_i", (128, NT * NE), I32)
    sbt("sb_pos_bias", (128, 64))
    sbt("sb_ones128b", (128, 128), BF16)
    sbt("sb_ltonesb", (128, 128), BF16)

    AR = Arena(nc, "AR", 50700)

    PS = nc.alloc_psum_tensor("ps_main", [128, 6, 512], F32)
    PT = nc.alloc_psum_tensor("ps_tr", [128, 1024], BF16)
    PA = nc.alloc_psum_tensor("ps_aux", [128, 512], F32)

    b_in_c = lambda c: S["sb_cst_small"][:, c:c + 1]
    convb_c = lambda c: S["sb_cst_small"][:, 16 + c:17 + c]
    bqk_c = lambda c: S["sb_cst_small"][:, 32 + c:33 + c]
    bg_c = lambda c: S["sb_cst_small"][:, 34 + c:35 + c]
    gamma_c = lambda h, c: S["sb_cst_small"][:, 50 + 2 * h + c:51 + 2 * h + c]
    beta_c = lambda h, c: S["sb_cst_small2"][:, 2 * h + c:1 + 2 * h + c]
    alpha_c = S["sb_cst_small2"][:, 8:9]
    ones_c = S["sb_cst_small2"][:, 9:10]
    iota_tok = S["sb_cst_small2"][:, 10:42]
    convw_c = lambda c, k: S["convw"][:, 4 * c + k:4 * c + k + 1]
    b_x_col = S["sb_cst64"][0:64, 0:1]
    b_dt_col = S["sb_cst64"][0:64, 1:2]
    a_col = S["sb_cst64"][0:64, 3:4]
    nra_col = S["sb_cst64"][0:64, 4:5]

    def emit_ln(src_ap, dst_ap, n=D):
        vv = p.op("vector", "VE", lambda e: e.reduce_sum(S["stat"][:, 0:1], src_ap, AX.X))
        p.wait("scalar", "VE", vv)
        sv = p.op("scalar", "SE", lambda e: e.activation(dst_ap, src_ap, AF.Square, accum_out=S["stat"][:, 1:2]))
        p.wait("vector", "SE", sv)
        p.op("vector", "VE", lambda e: e.tensor_scalar(S["stat"][:, 2:3], S["stat"][:, 0:1], 1.0 / n, None, ALU.mult))
        p.op("vector", "VE", lambda e: e.tensor_scalar(S["stat"][:, 3:4], S["stat"][:, 1:2], 1.0 / n, None, ALU.mult))
        p.op("vector", "VE", lambda e: e.tensor_tensor(S["stat"][:, 4:5], S["stat"][:, 2:3], S["stat"][:, 2:3], ALU.mult))
        p.op("vector", "VE", lambda e: e.tensor_tensor(S["stat"][:, 5:6], S["stat"][:, 3:4], S["stat"][:, 4:5], ALU.subtract))
        p.op("vector", "VE", lambda e: e.tensor_scalar(S["stat"][:, 5:6], S["stat"][:, 5:6], EPS, None, ALU.add))
        vv = p.op("vector", "VE", lambda e: e.reciprocal(S["stat"][:, 6:7], S["stat"][:, 5:6]))
        p.wait("scalar", "VE", vv)
        sv = p.op("scalar", "SE", lambda e: e.activation(S["stat"][:, 7:8], S["stat"][:, 6:7], AF.Sqrt))
        p.wait("vector", "SE", sv)
        return p.op("vector", "VE", lambda e: e.tensor_scalar(
            dst_ap, src_ap, S["stat"][:, 2:3], S["stat"][:, 7:8], ALU.subtract, ALU.mult))

    def dbg_finish():
        for nm in list(p.cnt):
            if nm != "STO":
                p.wait("sync", nm)
        for j in range(32):
            p.dma("sync", "STO", out_h[128 * j:128 * (j + 1), :], S["zero2k"][:, :])
        p.wait("sync", "STO")
        emit(nc, p)

    p.dma("sync", "LDC", S["sb_iden_b"][:, :], iden_b_h[:, :])
    p.dma("sync", "LDC", S["sb_ones128b"][:, :], ones128_h[:, :])
    p.dma("sync", "LDC", S["sb_ltonesb"][:, :], ltones_h[:, :])
    p.dma("sync", "LDC", S["sb_cst_small"][:, :], cst_small[:, :])
    p.dma("sync", "LDC", S["sb_cst_small2"][:, :], cst_small2[:, :])
    p.dma("sync", "LDC", S["convw"][:, :], convw_h[:, :])
    p.dma("sync", "LDC", S["sb_pos_bias"][:, :], pos_bias_h[:, :])
    ld0 = p.dma("sync", "LDC", S["sb_cst64"][0:64, :], cst64[:, :])
    p.opq("vector", lambda e: e.memset(S["zero2k"][:, :], 0.0))
    p.opq("vector", lambda e: e.memset(S["onesrow"][0:1, :], 1.0))
    zready = p.op("vector", "VE", lambda e: e.memset(S["stat"][:, :], 0.0))
    p.wait("scalar", "LDC", ld0)
    p.op("scalar", "SE", lambda e: e.activation(S["sb_cst64"][0:64, 2:3], S["sb_cst64"][0:64, 2:3], AF.Exp))
    sv = p.op("scalar", "SE", lambda e: e.mul(a_col, S["sb_cst64"][0:64, 2:3], -1.0))
    p.wait("vector", "SE", sv)
    p.op("vector", "VE", lambda e: e.reciprocal(nra_col, S["sb_cst64"][0:64, 2:3]))

    w_inT_s = AR.alloc("w_inT", [128, 8, DI], BF16)
    w_xdtT_s = AR.alloc("w_xdtT", [128, 16, 128], BF16)
    dmat_s = AR.alloc("dmat", [128, DI], BF16)
    gate_wT_s = AR.alloc("gate_wT", [128, 8, NE], BF16)
    for k in range(8):
        p.dma("gpsimd", "GW1", w_inT_s[:, k, :], w_inT[128 * k:128 * (k + 1), :])
        p.dma("gpsimd", "GW1", gate_wT_s[:, k, :], gate_wT[128 * k:128 * (k + 1), :])
    for k in range(16):
        p.dma("gpsimd", "GW1", w_xdtT_s[:, k, :], w_xdtT[128 * k:128 * (k + 1), :])
    gw_l0 = p.dma("gpsimd", "GW1", dmat_s[0:64, :], d_mat[:, :])

    p.wait("gpsimd", "VE", zready)
    for j in range(32):
        p.dma("gpsimd", "GZ", sp_g[128 * j:128 * (j + 1), :], S["zero2k"][:, :])
    for j in range(4):
        p.dma("gpsimd", "GZ",
              cmp_h[8192 * j:8192 * (j + 1), :].rearrange("(a b) c -> a (b c)", b=64),
              S["zero2k"][:, 0:128])
    gz_done = p.cnt["GZ"]

    x_t = AR.alloc("x_t", [128, 2, D])
    xn_f = AR.alloc("xn_f", [128, D])
    xn_b = AR.alloc("xn_b", [128, 2, D], BF16)

    xn_fm = AR.alloc("xn_fm", [128, 8, T], BF16)
    xn_done_tile = []
    tr_te = []
    pxf = []
    p.wait("tensor", "GW1", gw_l0)
    for it in range(NT):
        sl = it % 2
        if it >= 2:
            p.wait("sync", "VE", xn_done_tile[it - 2])
        ldv = p.dma("sync", "LX%d" % sl, x_t[:, sl, :], x_h[128 * it:128 * (it + 1), :])
        p.wait("vector", "LX%d" % sl, ldv)
        emit_ln(x_t[:, sl, :], xn_f[:, :])
        if it >= 2:
            p.wait("vector", "SX%d" % sl)
            p.wait("vector", "TE", tr_te[it - 2])
        vv = p.op("vector", "VE", lambda e, sl=sl: e.tensor_copy(xn_b[:, sl, :], xn_f[:, :]))
        xn_done_tile.append(vv)
        p.wait("sync", "VE", vv)
        p.dma("sync", "SX%d" % sl, sp_xn[128 * it:128 * (it + 1), :], xn_b[:, sl, :])
        p.wait("tensor", "VE", vv)
        if it >= 1:
            p.wait("tensor", "PXF", pxf[it - 1])
        tei = p.inc("TE", 1)
        hTE = p.sems["TE"]
        for dk in range(8):
            if dk < 7:
                p.emit("tensor", lambda e, sl=sl, dk=dk: e.transpose(
                    PT[:, 128 * dk:128 * (dk + 1)], xn_b[:, sl, 128 * dk:128 * (dk + 1)],
                    S["sb_iden_b"][:, :]))
            else:
                p.emit("tensor", lambda e, sl=sl, dk=dk, hTE=hTE: e.transpose(
                    PT[:, 128 * dk:128 * (dk + 1)], xn_b[:, sl, 128 * dk:128 * (dk + 1)],
                    S["sb_iden_b"][:, :]).then_inc(hTE, 1))
        tr_te.append(tei)
        p.wait("scalar", "TE", tei)
        for dk in range(8):
            p.op("scalar", "PXF", lambda e, dk=dk, it=it: e.activation(
                xn_fm[:, dk, 128 * it:128 * (it + 1)], PT[:, 128 * dk:128 * (dk + 1)], AF.Identity))
        pxf.append(p.cnt["PXF"])
        p.wait("tensor", "PXF", pxf[it])
        p.mm(PA[:, 8 * it:8 * it + 8],
             [(xn_fm[:, k, 128 * it:128 * (it + 1)], gate_wT_s[:, k, :]) for k in range(8)])

    AR.free("x_t")
    AR.free("xn_f")
    AR.free("xn_b")

    if debug and STAGE == 1:
        dbg_finish()
        return nc, io

    te_logits = p.cnt["TE"]

    aff3 = S["aff_tok"][:, :].rearrange("p (a b) -> p a b", b=NE)
    pa3 = PA[:, 0:256].rearrange("p (a b) -> p a b", b=NE)
    smax = AR.alloc("smax", [128, NT])
    ssum = AR.alloc("ssum", [128, NT])
    p.wait("vector", "TE", te_logits)
    p.op("vector", "VE", lambda e: e.reduce_max(smax[:, :], pa3, AX.X))
    vv = p.op("vector", "VE", lambda e: e.tensor_tensor(
        aff3, pa3, smax[:, :].unsqueeze(2).broadcast_to([128, NT, NE]), ALU.subtract))
    p.wait("scalar", "VE", vv)
    sv = p.op("scalar", "SE", lambda e: e.activation(S["aff_tok"][:, :], S["aff_tok"][:, :], AF.Exp))
    p.wait("vector", "SE", sv)
    p.op("vector", "VE", lambda e: e.reduce_sum(ssum[:, :], aff3, AX.X))
    p.op("vector", "VE", lambda e: e.reciprocal(ssum[:, :], ssum[:, :]))
    aff_done = p.op("vector", "VE", lambda e: e.tensor_tensor(
        aff3, aff3, ssum[:, :].unsqueeze(2).broadcast_to([128, NT, NE]), ALU.mult))

    mask3f = AR.alloc("mask3", [128, NT * NE], BF16)
    pref3 = AR.alloc("pref3", [128, NT * NE])
    u3 = AR.alloc("u3", [128, NT * NE])
    selw = AR.alloc("selw", [128, 80])
    selwb = AR.alloc("selwb", [128, 16], BF16)
    LO = selw[:, 0:8]
    HI = selw[:, 8:16]
    MID = selw[:, 16:24]
    CNTC = selw[:, 24:32]
    SW = selw[:, 32:40]
    TMP8 = selw[:, 40:48]
    ONES32 = selw[:, 48:80]
    PARTB = selwb[:, 0:8]
    TOTSB = selwb[:, 8:16]
    m3 = mask3f[:, :].rearrange("p (a b) -> p a b", b=NE)
    me = lambda e_: mask3f[:, e_::NE]
    pe = lambda e_: pref3[:, e_::NE]
    ue = lambda e_: u3[:, e_::NE]
    mei = mask3f[:, :].rearrange("p (a b) -> p b a", b=NE)
    eoffk = S["sb_cst_small2"][:, 43:51]
    idxf3 = S["idxf_tok"][:, :].rearrange("p (a b) -> p a b", b=NE)
    idxi3 = S["idxf_tok_i"][:, :].rearrange("p (a b) -> p a b", b=NE)

    p.op("vector", "VE", lambda e: e.memset(LO, 0.0))
    p.op("vector", "VE", lambda e: e.memset(HI, 1.0))
    p.op("vector", "VE", lambda e: e.memset(ONES32, 1.0))
    p.wait("vector", "VE", aff_done)
    for i in range(BISECT):
        p.op("vector", "VE", lambda e: e.tensor_tensor(MID, LO, HI, ALU.add))
        p.op("vector", "VE", lambda e: e.tensor_scalar(MID, MID, 0.5, None, ALU.mult))
        p.op("vector", "VE", lambda e: e.tensor_tensor(
            m3, aff3, MID.unsqueeze(1).broadcast_to([128, NT, NE]), ALU.is_ge))
        def _cnt_red(e):
            with nc.allow_low_precision(reason="0/1 count reduce, <=32, exact in bf16"):
                return e.tensor_reduce(PARTB, mei, AX.X, ALU.add)
        vv = p.op("vector", "VE", _cnt_red)
        p.wait("tensor", "VE", vv)
        tv = p.mm(PA[:, 0:8], [(S["sb_ones128b"][:, :], PARTB)])
        p.wait("vector", "TE", tv)
        p.op("vector", "VE", lambda e: e.tensor_copy(CNTC, PA[:, 0:8]))
        p.op("vector", "VE", lambda e: e.tensor_scalar(SW, CNTC, float(KSEL), None, ALU.is_ge))
        p.op("vector", "VE", lambda e: e.tensor_tensor(TMP8, MID, LO, ALU.subtract))
        p.op("vector", "VE", lambda e: e.tensor_tensor(TMP8, TMP8, SW, ALU.mult))
        p.op("vector", "VE", lambda e: e.tensor_tensor(LO, LO, TMP8, ALU.add))
        p.op("vector", "VE", lambda e: e.tensor_tensor(TMP8, HI, MID, ALU.subtract))
        p.op("vector", "VE", lambda e: e.tensor_tensor(TMP8, TMP8, SW, ALU.mult))
        p.op("vector", "VE", lambda e: e.tensor_tensor(HI, MID, TMP8, ALU.add))
    p.op("vector", "VE", lambda e: e.tensor_tensor(
        m3, aff3, LO.unsqueeze(1).broadcast_to([128, NT, NE]), ALU.is_ge))
    p.op("vector", "VE", lambda e: e.tensor_scalar(u3[:, :], mask3f[:, :], -1.0, 1.0, ALU.mult, ALU.add))
    for e_ in range(NE):
        p.op("vector", "VE", lambda e, e_=e_: e.tensor_tensor_scan(
            pe(e_), ONES32, me(e_), 0.0, ALU.mult, ALU.add))
    for e_ in range(NE):
        p.op("vector", "VE", lambda e, e_=e_: e.tensor_tensor_scan(
            ue(e_), ONES32, ue(e_), 0.0, ALU.mult, ALU.add))
    p.op("vector", "VE", lambda e: e.tensor_copy(TOTSB, pref3[:, (NT - 1) * NE:NT * NE]))
    vv = p.op("vector", "VE", lambda e: e.tensor_copy(PARTB, u3[:, (NT - 1) * NE:NT * NE]))
    p.wait("tensor", "VE", vv)
    p.mm(PA[:, 8:16], [(S["sb_ltonesb"][:, :], TOTSB)])
    tv = p.mm(PA[:, 16:24], [(S["sb_ltonesb"][:, :], PARTB)])
    p.wait("vector", "TE", tv)
    p.op("vector", "VE", lambda e: e.tensor_tensor(
        pref3[:, :].rearrange("p (a b) -> p a b", b=NE),
        pref3[:, :].rearrange("p (a b) -> p a b", b=NE),
        PA[:, 8:16].unsqueeze(1).broadcast_to([128, NT, NE]), ALU.add))
    p.op("vector", "VE", lambda e: e.tensor_tensor(
        u3[:, :].rearrange("p (a b) -> p a b", b=NE),
        u3[:, :].rearrange("p (a b) -> p a b", b=NE),
        PA[:, 16:24].unsqueeze(1).broadcast_to([128, NT, NE]), ALU.add))
    p.op("vector", "VE", lambda e: e.tensor_tensor(S["idxf_tok"][:, :], pref3[:, :], u3[:, :], ALU.subtract))
    p.op("vector", "VE", lambda e: e.tensor_scalar(S["idxf_tok"][:, :], S["idxf_tok"][:, :], float(KSEL), None, ALU.subtract))
    p.op("vector", "VE", lambda e: e.tensor_tensor(S["idxf_tok"][:, :], S["idxf_tok"][:, :], mask3f[:, :], ALU.mult))
    p.op("vector", "VE", lambda e: e.tensor_tensor(S["idxf_tok"][:, :], S["idxf_tok"][:, :], u3[:, :], ALU.add))
    p.op("vector", "VE", lambda e: e.tensor_tensor(
        idxf3, idxf3, eoffk.unsqueeze(1).broadcast_to([128, NT, NE]), ALU.add))
    cast_idx = p.op("vector", "VE", lambda e: e.tensor_copy(S["idxf_tok_i"][:, :], S["idxf_tok"][:, :]))

    pairs_i = AR.alloc("pairs_all", [128, NE, NT, 2], I32)
    pairs_f = AR.view("pairs_all", [128, NE, NT, 2], F32)
    stg_i = AR.alloc("stg", [128, NE, 8, 2], I32)
    stg_f = AR.view("stg", [128, NE, 8, 2], F32)
    stg_flat = AR.view("stg", [128, NE * 16], I32)
    xg = AR.alloc("xg", [128, 2, 8, D], BF16)
    for ei in range(NE):
        p.op("vector", "VE", lambda e, ei=ei: e.tensor_copy(pairs_i[:, ei, :, 0], iota_tok))
        p.op("vector", "VE", lambda e, ei=ei: e.tensor_copy(pairs_f[:, ei, :, 1], aff3[:, :, ei]))
    pairs_ve = p.cnt["VE"]
    p.wait("gpsimd", "GZ", gz_done)
    p.wait("gpsimd", "SX0")
    p.wait("gpsimd", "SX1")
    p.wait("gpsimd", "VE", pairs_ve)
    gath_done = {}
    lds_end = {}
    for ei in range(NE):
        for blk in range(NT):
            p.iscat("GSa%d" % ei, cmp_h[:, :], idxi3[:, blk, ei:ei + 1], pairs_i[:, ei, blk, :])
        p.wait("sync", "GSa%d" % ei, p.cnt["GSa%d" % ei])
        sld = p.dma("sync", "LDSa%d" % ei, stg_flat[:, 16 * ei:16 * (ei + 1)],
                    cmp_h[ei * T:ei * T + KSEL, :].rearrange("(p j) c -> p (j c)", p=128))
        lds_end[ei] = sld
        if ei < 2:
            p.wait("gpsimd", "LDSa%d" % ei, sld)
            for j in range(8):
                p.igather("GG%d" % ei, xg[:, ei, j, :], sp_xn[:, :], stg_i[:, ei, j, 0:1])
            gath_done[ei] = ("GG%d" % ei, p.cnt["GG%d" % ei])

    if debug and STAGE == 3:
        p.dma("sync", "SDB", sp_x1[0:128, 0:256], S["aff_tok"][:, :])
        dbg_finish()
        return nc, io

    hpre = AR.alloc("hpre", [128, 16, CH + 3], BF16)
    hc = AR.alloc("hc", [128, 16, CH], BF16)
    t0b = AR.alloc("t0b", [128, 2, CH], BF16)
    zb = AR.alloc("zb", [64, T], BF16)
    scn = AR.alloc("scn", [64, 6, CH])
    zcar = AR.alloc("zcar", [64, 2])

    p.wait("vector", "SX0")
    p.wait("vector", "SX1")
    p.wait("scalar", "SX0")
    p.wait("scalar", "SX1")
    p.opq("vector", lambda e: e.memset(hpre[:, :, 0:3], 0.0))
    p.opq("vector", lambda e: e.memset(zcar[0:64, 0:2], 0.0))

    evac_hist = {}
    halo_v = {}
    conv_v = {}
    xdt_te = 0
    sc_epi = 0

    def emit_win(c):
        col = slice(CH * c, CH * (c + 1))
        for mi in range(16):
            bank = mi % 2
            key = c * 16 + mi
            if key >= 2:
                p.wait("tensor", "SE", evac_hist[key - 2])
            tv = p.mm(PS[:, bank, :],
                      [(w_inT_s[:, k, 128 * mi:128 * (mi + 1)], xn_fm[:, k, col])
                       for k in range(8)])
            p.wait("scalar", "TE", tv)
            if mi == 0 and c > 0:
                p.wait("scalar", "VE", halo_v[c - 1])
            evac_hist[key] = p.op("scalar", "SE",
                lambda e, mi=mi, bank=bank: e.activation(
                    hpre[:, mi, 3:3 + CH], PS[:, bank, :], AF.Identity, bias=b_in_c(mi)))

    def emit_conv(c):
        p.wait("vector", "SE", evac_hist[c * 16 + 15])
        if c > 0:
            p.wait("vector", "TE", xdt_te)
        for mi in range(16):
            tsl = mi % 2
            p.op("vector", "VE", lambda e, mi=mi, tsl=tsl: e.tensor_scalar(
                t0b[:, tsl, :], hpre[:, mi, 0:CH], convw_c(mi, 0), convb_c(mi), ALU.mult, ALU.add))
            p.op("vector", "VE", lambda e, mi=mi, tsl=tsl: e.scalar_tensor_tensor(
                t0b[:, tsl, :], hpre[:, mi, 1:1 + CH], convw_c(mi, 1), t0b[:, tsl, :], ALU.mult, ALU.add))
            p.op("vector", "VE", lambda e, mi=mi, tsl=tsl: e.scalar_tensor_tensor(
                t0b[:, tsl, :], hpre[:, mi, 2:2 + CH], convw_c(mi, 2), t0b[:, tsl, :], ALU.mult, ALU.add))
            p.op("vector", "VE", lambda e, mi=mi, tsl=tsl: e.scalar_tensor_tensor(
                hc[:, mi, :], hpre[:, mi, 3:3 + CH], convw_c(mi, 3), t0b[:, tsl, :], ALU.mult, ALU.add))
        conv_v[c] = p.cnt["VE"]
        if c + 1 < NCH:
            p.op("vector", "VE", lambda e: e.tensor_copy(hpre[:, :, 0:3], hpre[:, :, CH:CH + 3]))
        halo_v[c] = p.cnt["VE"]

    def emit_xdt(c):
        col = slice(CH * c, CH * (c + 1))
        p.wait("tensor", "VE", conv_v[c])
        if c > 0:
            p.wait("tensor", "SE", sc_epi)
        p.mm(PS[0:64, 2, :], [(w_xdtT_s[:, k, 0:64], hc[:, k, :]) for k in range(16)])
        tv = p.mm(PS[0:64, 3, :], [(w_xdtT_s[:, k, 64:128], hc[:, k, :]) for k in range(16)])
        p.wait("scalar", "TE", tv)
        p.op("scalar", "SE", lambda e: e.activation(scn[0:64, 0, :], PS[0:64, 2, :], AF.Identity, bias=b_x_col))
        se1 = p.op("scalar", "SE", lambda e: e.activation(scn[0:64, 1, :], PS[0:64, 3, :], AF.Identity, bias=b_dt_col))
        p.op("scalar", "SE", lambda e: e.activation(scn[0:64, 2, :], scn[0:64, 1, :], AF.Abs))
        p.op("scalar", "SE", lambda e: e.activation(scn[0:64, 2, :], scn[0:64, 2, :], AF.Exp, scale=-1.0))
        p.op("scalar", "SE", lambda e: e.activation(scn[0:64, 2, :], scn[0:64, 2, :], AF.Ln, bias=1.0))
        sv = p.op("scalar", "SE", lambda e: e.activation(scn[0:64, 5, :], scn[0:64, 1, :], AF.Relu))
        p.wait("vector", "SE", sv)
        vv = p.op("vector", "VE", lambda e: e.tensor_tensor(scn[0:64, 2, :], scn[0:64, 2, :], scn[0:64, 5, :], ALU.add))
        p.wait("scalar", "VE", vv)
        sv = p.op("scalar", "SE", lambda e: e.activation(scn[0:64, 3, :], scn[0:64, 2, :], AF.Exp, scale=a_col))
        p.wait("vector", "SE", sv)
        p.op("vector", "VE", lambda e: e.tensor_scalar(scn[0:64, 4, :], scn[0:64, 3, :], 1.0, nra_col, ALU.subtract, ALU.mult))
        p.op("vector", "VE", lambda e: e.tensor_tensor(scn[0:64, 4, :], scn[0:64, 4, :], scn[0:64, 0, :], ALU.mult))
        p.op("vector", "VE", lambda e: e.tensor_tensor_scan(
            scn[0:64, 5, :], scn[0:64, 3, :], scn[0:64, 4, :], zcar[0:64, 0:1], ALU.mult, ALU.add))
        p.op("vector", "VE", lambda e: e.tensor_copy(zcar[0:64, 0:1], scn[0:64, 5, CH - 1:CH]))
        p.op("vector", "VE", lambda e, col=col: e.tensor_copy(zb[0:64, col], scn[0:64, 5, :]))
        return tv, se1

    for c in range(NCH):
        emit_win(c)
        if c >= 1:
            xdt_te, sc_epi = emit_xdt(c - 1)
        emit_conv(c)
    xdt_te, sc_epi = emit_xdt(NCH - 1)
    scan_done = p.cnt["VE"]

    if debug and STAGE == 2:
        p.wait("sync", "VE", scan_done)
        p.dma("sync", "SDB", sp_lk[0:64, :], zb[0:64, :])
        p.dma("sync", "SDB", sp_x1[0:128, 0:256], S["aff_tok"][:, :])
        dbg_finish()
        return nc, io

    for nm in ("mask3", "pref3", "u3", "selw", "selwb", "pairs_all",
               "smax", "ssum", "hpre", "hc", "t0b", "scn", "zcar",
               "w_inT", "w_xdtT", "xn_fm"):
        AR.free(nm)

    xin = AR.alloc("xin", [128, 2, 8, D], BF16)
    h_h = AR.alloc("h_h", [128, 4, D], BF16)
    gsil = AR.alloc("gsil", [128, D], BF16)
    y_acc = AR.alloc("y_acc", [128, 8, D])
    wg_s = AR.alloc("wg_s", [128, 2, 8, 512], BF16)
    wu_s = AR.alloc("wu_s", [128, 2, 8, 512], BF16)
    wd_s = AR.alloc("wd_s", [128, 2, 4, D], BF16)
    wdf = AR.alloc("wdf", [128, 2, D])

    SPL = 8
    yscat = []
    gate_ve_end = {}
    trans_end = {}
    mm_end = {}
    dn_mm_end = {}
    split_te_end = {}
    gat_ve = {}
    idxc = lambda ei, j: stg_i[:, ei, j, 0:1]

    te_ph3 = p.cnt["TE"]
    wdf_cast = {}
    p.wait("gpsimd", "VE", scan_done)
    p.wait("sync", "VE", scan_done)
    p.wait("sync", "TE", te_ph3)

    for ei in range(MOE_LIMIT):
        sl = ei % 2
        p.wait("tensor", gath_done[ei][0], gath_done[ei][1])
        if ei >= 2:
            p.wait("tensor", "VE", gat_ve.get(ei - 2, 0))
        ev_prev = 0
        if ei >= 2:
            p.wait("vector", "TE", mm_end.get(ei - 2, te_ph3))
        for j in range(8):
            if j > 0:
                p.wait("tensor", "VE", ev_prev)
            v = p.inc("TE", 1)
            h = p.sems["TE"]
            for cb in range(8):
                if cb < 7:
                    p.emit("tensor", lambda e, sl=sl, j=j, cb=cb: e.transpose(
                        PT[:, 128 * cb:128 * (cb + 1)], xg[:, sl, j, 128 * cb:128 * (cb + 1)], S["sb_iden_b"][:, :]))
                else:
                    p.emit("tensor", lambda e, sl=sl, j=j, cb=cb, h=h: e.transpose(
                        PT[:, 128 * cb:128 * (cb + 1)], xg[:, sl, j, 128 * cb:128 * (cb + 1)], S["sb_iden_b"][:, :]).then_inc(h, 1))
            p.wait("vector", "TE", v)
            for cb in range(8):
                p.op("vector", "VE", lambda e, sl=sl, j=j, cb=cb: e.tensor_copy(
                    xin[:, sl, cb, 128 * j:128 * (j + 1)], PT[:, 128 * cb:128 * (cb + 1)]))
            ev_prev = p.cnt["VE"]
        trans_end[ei] = p.cnt["TE"]
        gat_ve[ei] = p.cnt["VE"]
        xin_ready = p.cnt["VE"]
        p.wait("tensor", "VE", xin_ready)

        def load_split(ei, s):
            slw = s % 2
            sem = "GW2%d" % slw
            p.dma("gpsimd", sem, wg_s[:, slw, :, :],
                  wgT_all[ei, :, 512 * s:512 * (s + 1)].rearrange("(k p) c -> p k c", p=128))
            gws = p.dma("gpsimd", sem, wu_s[:, slw, :, :],
                        wuT_all[ei, :, 512 * s:512 * (s + 1)].rearrange("(k p) c -> p k c", p=128))
            plc = 0
            for hf in range(2):
                p.wait("sync", "PL", wdf_cast.get(0, 0))
                rows = slice(512 * s + 256 * hf, 512 * s + 256 * (hf + 1))
                hws = p.dma("sync", "HW%d" % slw, wdf[:, :, :],
                            wdT_all[ei, rows, :].rearrange("(k p) d -> p k d", p=128))
                p.wait("gpsimd", "HW%d" % slw, hws)
                plc = p.op("gpsimd", "PL", lambda e, slw=slw, hf=hf: e.tensor_copy(
                    wd_s[:, slw, 2 * hf:2 * (hf + 1), :], wdf[:, :, :]))
                wdf_cast[0] = plc
            return (sem, gws, plc)

        gw_split = {}
        p.wait("gpsimd", "TE", split_te_end.get((ei - 1, SPL - 2), te_ph3))
        gw_split[0] = load_split(ei, 0)
        p.wait("gpsimd", "TE", split_te_end.get((ei - 1, SPL - 1), te_ph3))
        gw_split[1] = load_split(ei, 1)
        if ei + 2 < MOE_LIMIT:
            p.wait("gpsimd", "TE", trans_end[ei])
            p.wait("gpsimd", "LDSa%d" % (ei + 2), lds_end[ei + 2])
            gsem = "GG%d" % ((ei + 2) % 2)
            for j in range(8):
                p.igather(gsem, xg[:, (ei + 2) % 2, j, :], sp_xn[:, :], idxc(ei + 2, j))
            gath_done[ei + 2] = (gsem, p.cnt[gsem])
        if ei >= 1:
            if yscat:
                p.wait("gpsimd", "GS2", yscat[-1])
            p.wait("gpsimd", "VE", gate_ve_end[ei - 1])
            for j in range(8):
                yscat.append(p.iscat("GS2", sp_g[:, :], idxc(ei - 1, j), y_acc[:, j, :]))
        gu_evac = {}
        dn_evac = {}
        dn_mm = {}
        for s in range(SPL):
            slw = s % 2
            p.wait("tensor", gw_split[s][0], gw_split[s][1])
            p.wait("tensor", "PL", gw_split[s][2])
            if s > 0:
                pass
            for gidx in range(8):
                mi, th = gidx // 2, gidx % 2
                g_lin = s * 8 + gidx
                gb, ub = g_lin % 2, 2 + g_lin % 2
                if g_lin >= 2:
                    p.wait("tensor", "VE", gu_evac[g_lin - 2])
                if gidx == 0 and s > 0:
                    p.wait("tensor", "VE", dn_evac[(s - 1, 7)])
                tok = slice(512 * th, 512 * (th + 1))
                tvg = p.mm(PS[:, gb, :],
                           [(wg_s[:, slw, k, 128 * mi:128 * (mi + 1)], xin[:, sl, k, tok])
                            for k in range(8)])
                tvu = p.mm(PS[:, ub, :],
                           [(wu_s[:, slw, k, 128 * mi:128 * (mi + 1)], xin[:, sl, k, tok])
                            for k in range(8)])
                p.wait("scalar", "TE", tvg)
                if g_lin >= 2:
                    p.wait("scalar", "VE", gu_evac[g_lin - 2])
                sv = p.op("scalar", "SE", lambda e, gb=gb, tok=tok: e.activation(
                    gsil[:, tok], PS[:, gb, :], AF.Silu))
                p.wait("vector", "SE", sv)
                p.wait("vector", "TE", tvu)
                if gidx == 0 and s > 0:
                    p.wait("vector", "TE", dn_mm[(s - 1, 7)])
                vv = p.op("vector", "VE", lambda e, mi=mi, tok=tok, ub=ub: e.tensor_tensor(
                    h_h[:, mi, tok], gsil[:, tok], PS[:, ub, :], ALU.mult))
                gu_evac[g_lin] = vv
            p.wait("tensor", "VE", gu_evac[s * 8 + 7])
            p.wait("tensor", "VE", gu_evac[s * 8 + 6])
            for j in range(8):
                bk = (4, 5) if j % 2 == 0 else (0, 1)
                if j >= 2:
                    p.wait("tensor", "VE", dn_evac[(s, j - 2)])
                elif s > 0:
                    p.wait("tensor", "VE", dn_evac[(s - 1, 6 + j)])
                tv = None
                for nh in range(2):
                    tv = p.mm(PS[:, bk[nh], :],
                              [(h_h[:, k, 128 * j:128 * (j + 1)],
                                wd_s[:, slw, k, 512 * nh:512 * (nh + 1)]) for k in range(4)])
                dn_mm[(s, j)] = tv
                p.wait("vector", "TE", tv)
                if s == 0 and ei > 0:
                    p.wait("vector", "GS2", yscat[(ei - 1) * 8 + 7])
                pj = PS[:, bk[0]:bk[0] + 2, :].rearrange("p a b -> p (a b)") if bk[0] == 4 else PS[:, 0:2, :].rearrange("p a b -> p (a b)")
                if s == 0:
                    vv = p.op("vector", "VE", lambda e, j=j, pj=pj: e.tensor_copy(y_acc[:, j, :], pj))
                else:
                    vv = p.op("vector", "VE", lambda e, j=j, pj=pj: e.tensor_tensor(y_acc[:, j, :], y_acc[:, j, :], pj, ALU.add))
                dn_evac[(s, j)] = vv
            split_te_end[(ei, s)] = p.cnt["TE"]
            if s + 2 < SPL:
                p.wait("gpsimd", "TE", split_te_end[(ei, s)])
                gw_split[s + 2] = load_split(ei, s + 2)
        mm_end[ei] = p.cnt["TE"]
        p.wait("vector", "TE", dn_mm[(SPL - 1, 7)])
        for j in range(8):
            p.wait("vector", "VE", dn_evac[(SPL - 1, j)])
            p.op("vector", "VE", lambda e, ei=ei, j=j: e.tensor_scalar(
                y_acc[:, j, :], y_acc[:, j, :], stg_f[:, ei, j, 1:2], None, ALU.mult))
        gate_ve_end[ei] = p.cnt["VE"]
    if MOE_LIMIT >= 1:
        if yscat:
            p.wait("gpsimd", "GS2", yscat[-1])
        p.wait("gpsimd", "VE", gate_ve_end[MOE_LIMIT - 1])
        for j in range(8):
            yscat.append(p.iscat("GS2", sp_g[:, :], idxc(MOE_LIMIT - 1, j), y_acc[:, j, :]))

    moe_done = yscat[-1] if yscat else 0

    if debug and STAGE == 4:
        p.wait("sync", "GS2", moe_done)
        dbg_finish()
        return nc, io

    for nm in ("xg", "xin", "h_h", "gsil", "wg_s", "wu_s", "wd_s", "wdf", "y_acc", "stg"):
        AR.free(nm)

    w_outT_s = AR.alloc("w_outT", [128, 16, D], BF16)
    p.wait("gpsimd", "GS2", moe_done)
    for k in range(16):
        p.dma("gpsimd", "GW3", w_outT_s[:, k, :], w_outT[128 * k:128 * (k + 1), :])
    gw_wout = p.cnt["GW3"]

    y0_sb = AR.alloc("y0_sb", [128, 16, 512], BF16)
    s_sb = AR.alloc("s_sb", [128, D])
    g_t = AR.alloc("g_t", [128, 2, D])
    x_t2 = AR.alloc("x_t2", [128, 2, D])
    bcl0_s = AR.alloc("bcl0", [128, D])
    s1_sb = AR.alloc("s1_sb", [128, D])
    ln1w_s = AR.alloc("ln1w", [128, D])
    ln1b_s = AR.alloc("ln1b", [128, D])
    xnl_f = AR.alloc("xnl_f", [128, D])
    xnl_b = AR.alloc("xnl_b", [128, 2, D], BF16)
    xn1_b = AR.alloc("xn1_b", [128, 2, D], BF16)
    p.wait("sync", "GS2", moe_done)
    ld_bc = p.dma("sync", "LDC5", bcl0_s[:, :], bc_l0[:, :])
    p.dma("sync", "LDC6", ln1w_s[:, :], bc_ln1w[:, :])
    ld_ln1 = p.dma("sync", "LDC6", ln1b_s[:, :], bc_ln1b[:, :])
    p.wait("gpsimd", "LDC6", ld_ln1)

    p.wait("tensor", "VE", scan_done)
    p.wait("tensor", "VE", gate_ve_end.get(MOE_LIMIT - 1, 0))
    p.wait("vector", "GS2", moe_done)
    wout_te = 0
    for c in range(NCH):
        col = slice(CH * c, CH * (c + 1))
        for mi in range(16):
            bank = mi % 2
            tv = p.mm(PS[:, bank, :], [(dmat_s[0:64, 128 * mi:128 * (mi + 1)], zb[0:64, col])])
            p.wait("vector", "TE", tv)
            if c > 0 and mi == 0:
                p.wait("vector", "TE", wout_te)
            vv = p.op("vector", "VE", lambda e, mi=mi, bank=bank: e.tensor_copy(
                y0_sb[:, mi, :], PS[:, bank, :]))
            p.wait("tensor", "VE", vv)
        for tt in range(CT):
            it = c * CT + tt
            rows = slice(128 * it, 128 * (it + 1))
            sl = tt % 2
            ldg = p.dma("sync", "LG%d" % sl, g_t[:, sl, :], sp_g[rows, :])
            ldx = p.dma("sync", "L2%d" % sl, x_t2[:, sl, :], x_h[rows, :])
            if c == 0 and tt == 0:
                p.wait("tensor", "GW3", gw_wout)
            tv = None
            for nh in range(2):
                tv = p.mm(PS[:, 4 + nh, :],
                          [(y0_sb[:, k, 128 * tt:128 * (tt + 1)],
                            w_outT_s[:, k, 512 * nh:512 * (nh + 1)]) for k in range(16)])
            p.wait("vector", "TE", tv)
            p.wait("vector", "LDC5", ld_bc)
            p.wait("vector", "ST1")
            p.op("vector", "VE", lambda e: e.scalar_tensor_tensor(
                s_sb[:, 0:512], PS[:, 4, :], 1.0, bcl0_s[:, 0:512], ALU.mult, ALU.add))
            vv = p.op("vector", "VE", lambda e: e.scalar_tensor_tensor(
                s_sb[:, 512:1024], PS[:, 5, :], 1.0, bcl0_s[:, 512:1024], ALU.mult, ALU.add))
            p.wait("tensor", "VE", vv)
            p.wait("scalar", "LG%d" % sl, ldg)
            sv = p.op("scalar", "SE", lambda e, sl=sl: e.activation(g_t[:, sl, :], g_t[:, sl, :], AF.Silu))
            p.wait("vector", "SE", sv)
            p.wait("vector", "L2%d" % sl, ldx)
            p.op("vector", "VE", lambda e, sl=sl: e.tensor_tensor(s_sb[:, :], s_sb[:, :], g_t[:, sl, :], ALU.mult))
            vv = p.op("vector", "VE", lambda e, sl=sl: e.tensor_tensor(s_sb[:, :], s_sb[:, :], x_t2[:, sl, :], ALU.add))
            p.wait("sync", "VE", vv)
            p.dma("sync", "ST1", sp_x1[rows, :], s_sb[:, :])
            if it >= 1:
                p.wait("vector", "PL")
            if it >= 2:
                p.wait("gpsimd", "SL%d" % sl)
            lnv = emit_ln(s_sb[:, :], s1_sb[:, :])
            p.wait("gpsimd", "VE", lnv)
            p.op("gpsimd", "PL", lambda e, sl=sl: e.tensor_copy(xnl_b[:, sl, :], s1_sb[:, :]))
            lnv2 = emit_ln(s1_sb[:, :], xnl_f[:, :])
            p.wait("gpsimd", "VE", lnv2)
            p.op("gpsimd", "PL", lambda e: e.tensor_tensor(xnl_f[:, :], xnl_f[:, :], ln1w_s[:, :], ALU.mult))
            plv = p.op("gpsimd", "PL", lambda e, sl=sl: e.tensor_tensor(xn1_b[:, sl, :], xnl_f[:, :], ln1b_s[:, :], ALU.add))
            p.wait("sync", "PL", plv)
            p.dma("sync", "SL%d" % sl, sp_ln[rows, :], xnl_b[:, sl, :])
            p.dma("sync", "SL%d" % sl, sp_ln1[rows, :], xn1_b[:, sl, :])
        wout_te = p.cnt["TE"]
    x1_done = p.cnt["ST1"]

    if debug and STAGE == 5:
        dbg_finish()
        return nc, io

    for nm in ("y0_sb", "g_t", "bcl0", "dmat", "zb", "w_outT", "gate_wT"):
        AR.free(nm)

    w_raw = AR.alloc("w_raw", [128, 8, D])
    absw = AR.alloc("absw", [128, 8 * D])
    for k in range(8):
        ld_w = p.dma("sync", "LDW", w_raw[:, k, :], l1wT[128 * k:128 * (k + 1), :])
    p.wait("scalar", "LDW", ld_w)
    sv = p.op("scalar", "SE", lambda e: e.activation(
        absw[:, :], w_raw.rearrange("p a b -> p (a b)") if False else w_raw[:, :, :], AF.Abs, accum_out=S["stat"][:, 8:9]))
    p.wait("tensor", "SE", sv)
    tv = p.mm(PA[0:1, 0:1], [(ones_c, S["stat"][:, 8:9])])
    p.wait("vector", "TE", tv)
    vv = p.op("vector", "VE", lambda e: e.tensor_copy(S["stat"][0:1, 9:10], PA[0:1, 0:1]))
    p.wait("tensor", "VE", vv)
    tv = p.mm(PA[:, 0:1], [(S["onesrow"][0:1, :], S["stat"][0:1, 9:10])])
    p.wait("vector", "TE", tv)
    p.op("vector", "VE", lambda e: e.tensor_scalar(S["stat"][:, 10:11], PA[:, 0:1], 0.7 / (D * D), None, ALU.mult))
    vv = p.op("vector", "VE", lambda e: e.tensor_scalar(absw[:, :], absw[:, :], S["stat"][:, 10:11], None, ALU.is_gt))
    p.wait("scalar", "VE", vv)
    sv = p.op("scalar", "SE", lambda e: e.sign(w_raw[:, :, :], w_raw[:, :, :]))
    p.wait("vector", "SE", sv)
    wtern = AR.alloc("wtern", [128, 8, D], BF16)
    tern_done = p.op("vector", "VE", lambda e: e.tensor_tensor(
        wtern.rearrange("p a b -> p (a b)") if False else wtern[:, :, :],
        absw[:, :].rearrange("p (a b) -> p a b", b=D), w_raw[:, :, :], ALU.mult))
    AR.free("w_raw")
    AR.free("absw")

    for nm in ("x_t2", "s_sb", "s1_sb", "xnl_f", "xnl_b", "xn1_b", "ln1w", "ln1b"):
        AR.free(nm)
    p.wait("sync", "LDW")
    p.wait("sync", "VE", tern_done)
    p.wait("gpsimd", "LDW")
    p.wait("vector", "LDW")
    p.wait("scalar", "SE")
    p.wait("scalar", "VE", tern_done)
    xn1c7 = AR.alloc("xn1c7", [128, 8, CH], BF16)
    p.wait("scalar", "SL0")
    p.wait("scalar", "SL1")

    wqkT_s = AR.alloc("wqkT", [128, 8, QK], BF16)
    p.wait("gpsimd", "VE")
    p.wait("gpsimd", "SE")
    p.wait("gpsimd", "ST1")
    p.wait("gpsimd", "SL0")
    p.wait("gpsimd", "SL1")
    for k in range(8):
        p.dma("gpsimd", "GW4", wqkT_s[:, k, :], wqkT[128 * k:128 * (k + 1), :])
    gw_qk = p.cnt["GW4"]
    wvT_s = AR.alloc("wvT", [128, 8, HD], BF16)
    for k in range(8):
        p.dma("gpsimd", "GW5", wvT_s[:, k, :], wvT[128 * k:128 * (k + 1), :])
    gw_v = p.cnt["GW5"]

    qk_fm = AR.alloc("qk_fm", [128, 2, CH], BF16)
    p.wait("tensor", "GW4", gw_qk)

    bv_s = AR.alloc("bv_s", [128, HD])
    p.wait("sync", "SL0")
    p.wait("sync", "SL1")
    ld_bv = p.dma("sync", "LDC7", bv_s[:, :], bc_bv[:, :])
    linkv = AR.alloc("linkv", [128, 2, HD])
    p.wait("vector", "SL0")
    p.wait("vector", "SL1")
    p.wait("vector", "ST1")
    p.opq("vector", lambda e: e.memset(linkv[:, :, :], 0.0))
    linkv_zero = p.cnt["VE"]

    v_t = AR.alloc("v_t", [128, 4, HD], BF16)
    lk_b = AR.alloc("lk_b", [128, 2, CH], BF16)
    lk_tok = AR.alloc("lk_tok", [128, 4, QK], BF16)
    hd_t = AR.alloc("hd_t", [128, 2, CH])
    rt_t = AR.alloc("rt_t", [128, 2, CH])
    cs_t = AR.alloc("cs_t", [128, 2, CH])

    p.wait("vector", "LDC7", ld_bv)
    p.wait("sync", "VE")
    p.wait("sync", "SE")
    st_v_tiles = []
    st_lk = 0
    linkv_te = 0
    wqkv_te = 0
    for c in range(NCH):
        col = slice(CH * c, CH * (c + 1))
        if c > 0:
            p.wait("scalar", "TE", wqkv_te)
        for dk in range(8):
            tld_x1c = p.tdma("scalar", "TL2", xn1c7[:, dk, :], sp_ln1[col, 128 * dk:128 * (dk + 1)])
        p.wait("tensor", "TL2", tld_x1c)
        if c > 0:
            p.wait("scalar", "SQK", 16 * 2 * c)
        for m2 in range(2):
            bank = m2 % 2
            tv = p.mm(PS[:, bank, :],
                      [(wqkT_s[:, k, 128 * m2:128 * (m2 + 1)], xn1c7[:, k, :])
                       for k in range(8)])
            p.wait("scalar", "TE", tv)
            sv = p.op("scalar", "SE", lambda e, m2=m2, bank=bank: e.activation(
                qk_fm[:, m2, :], PS[:, bank, :], AF.Silu, bias=bqk_c(m2)))
            p.wait("tensor", "SE", sv)
        qk_done_c = p.cnt["SE"]
        p.wait("sync", "SE", qk_done_c)
        p.dma("sync", "SQK", sp_qk[0:128, col], qk_fm[:, 0, :])
        p.dma("sync", "SQK", sp_qk[128:256, col], qk_fm[:, 1, :])
        p.wait("vector", "SE", qk_done_c)
        ld_cs = p.dma("sync", "LC7", cs_t[:, 0, :], cos_h[:, col])
        ld_cs = p.dma("sync", "LC7", cs_t[:, 1, :], sin_h[:, col])
        p.wait("vector", "LC7", ld_cs)
        if c > 0:
            p.wait("vector", "SLK", st_lk)
        for cc in range(2):
            p.op("vector", "VE", lambda e, cc=cc: e.tensor_scalar(
                hd_t[:, cc, :], qk_fm[:, cc, :], gamma_c(3, cc), beta_c(3, cc), ALU.mult, ALU.add))
        p.op("vector", "VE", lambda e: e.tensor_tensor(rt_t[:, 0, :], hd_t[:, 0, :], cs_t[:, 0, :], ALU.mult))
        p.op("vector", "VE", lambda e: e.tensor_tensor(rt_t[:, 1, :], hd_t[:, 1, :], cs_t[:, 1, :], ALU.mult))
        p.op("vector", "VE", lambda e: e.tensor_tensor(lk_b[:, 0, :], rt_t[:, 0, :], rt_t[:, 1, :], ALU.subtract))
        p.op("vector", "VE", lambda e: e.tensor_tensor(rt_t[:, 0, :], hd_t[:, 1, :], cs_t[:, 0, :], ALU.mult))
        p.op("vector", "VE", lambda e: e.tensor_tensor(rt_t[:, 1, :], hd_t[:, 0, :], cs_t[:, 1, :], ALU.mult))
        vv = p.op("vector", "VE", lambda e: e.tensor_tensor(lk_b[:, 1, :], rt_t[:, 0, :], rt_t[:, 1, :], ALU.add))
        p.wait("sync", "VE", vv)
        p.dma("sync", "SLK", sp_lk[0:128, col], lk_b[:, 0, :])
        st_lk = p.dma("sync", "SLK", sp_lk[128:256, col], lk_b[:, 1, :])
        p.wait("tensor", "GW5", gw_v)
        if c > 0:
            p.wait("scalar", "TE", linkv_te)
            p.wait("scalar", "SV", 16 * 4 * c)
        for tt in range(CT):
            tcol = slice(128 * tt, 128 * (tt + 1))
            for nb in range(4):
                bank = nb % 2
                tv = p.mm(PS[:, bank, :],
                          [(xn1c7[:, k, tcol], wvT_s[:, k, 512 * nb:512 * (nb + 1)])
                           for k in range(8)])
                p.wait("vector", "TE", tv)
                p.op("vector", "VE", lambda e, nb=nb, bank=bank: e.scalar_tensor_tensor(
                    rt_t[:, 0, 0:512], PS[:, bank, :], 1.0, bv_s[:, 512 * nb:512 * (nb + 1)], ALU.mult, ALU.add))
                vv = p.cnt["VE"]
                p.wait("scalar", "VE", vv)
                sv = p.op("scalar", "SE", lambda e, tt=tt, nb=nb: e.activation(
                    v_t[:, tt, 512 * nb:512 * (nb + 1)], rt_t[:, 0, 0:512], AF.Silu))
                p.wait("tensor", "SE", sv)
                p.wait("vector", "SE", sv)
            p.wait("sync", "SE")
            it = c * CT + tt
            st_v_tiles.append(p.dma("sync", "SV", sp_v[128 * it:128 * (it + 1), :], v_t[:, tt, :]))
        p.wait("scalar", "SLK", st_lk)
        if c > 0:
            p.wait("scalar", "TE", linkv_te)
        tld = None
        for tt in range(CT):
            it = c * CT + tt
            tld = p.tdma("scalar", "TL3", lk_tok[:, tt, :], sp_lk[0:QK, 128 * it:128 * (it + 1)])
        p.wait("vector", "VE", linkv_zero)
        p.wait("tensor", "TL3", tld)
        p.wait("tensor", "SE")
        for dh in range(2):
            for nq in range(4):
                tv = p.mm(PA[:, 0:512],
                          [(lk_tok[:, tt, 128 * dh:128 * (dh + 1)], v_t[:, tt, 512 * nq:512 * (nq + 1)])
                           for tt in range(CT)])
                p.wait("vector", "TE", tv)
                vv = p.op("vector", "VE", lambda e, dh=dh, nq=nq: e.tensor_tensor(
                    linkv[:, dh, 512 * nq:512 * (nq + 1)], linkv[:, dh, 512 * nq:512 * (nq + 1)],
                    PA[:, 0:512], ALU.add))
                p.wait("tensor", "VE", vv)
        linkv_te = p.cnt["TE"]
        wqkv_te = p.cnt["TE"]
    linkv_b = AR.alloc("linkv_b", [128, 2, HD], BF16)
    p.op("vector", "VE", lambda e: e.tensor_scalar(linkv_b[:, 0, :], linkv[:, 0, :], 1.0 / T, None, ALU.mult))
    linkv_done = p.op("vector", "VE", lambda e: e.tensor_scalar(linkv_b[:, 1, :], linkv[:, 1, :], 1.0 / T, None, ALU.mult))

    if debug and STAGE == 6:
        dbg_finish()
        return nc, io

    for nm in ("wvT", "v_t", "lk_b", "hd_t", "rt_t", "cs_t", "linkv",
               "xn1c7", "wqkT", "bv_s", "qk_fm", "lk_tok"):
        AR.free(nm)

    woT_s = AR.alloc("woT", [128, 16, D], BF16)
    p.wait("gpsimd", "TE")
    p.wait("gpsimd", "VE")
    p.wait("gpsimd", "SE")
    p.wait("gpsimd", "SV")
    p.wait("gpsimd", "SLK")
    p.wait("gpsimd", "SQK")
    for k in range(16):
        p.dma("gpsimd", "GW6", woT_s[:, k, :], woT[128 * k:128 * (k + 1), :])
    gw_wo = p.cnt["GW6"]
    wgT1_s = AR.alloc("wgT1", [128, 8, HD], BF16)
    for k in range(8):
        p.dma("gpsimd", "GW7", wgT1_s[:, k, :], wgT1[128 * k:128 * (k + 1), :])
    gw_wg1 = p.cnt["GW7"]

    ln2w_s = AR.alloc("ln2w", [128, D])
    ln2b_s = AR.alloc("ln2b", [128, D])
    bo_s = AR.alloc("bo_s", [128, D])
    l1b_s = AR.alloc("l1b_s", [128, D])
    p.wait("sync", "TE")
    p.wait("sync", "VE")
    p.wait("sync", "SV")
    p.wait("sync", "SLK")
    p.wait("sync", "SQK")
    p.dma("sync", "LDC8", ln2w_s[:, :], bc_ln2w[:, :])
    p.dma("sync", "LDC8", ln2b_s[:, :], bc_ln2b[:, :])
    p.dma("sync", "LDC8", bo_s[:, :], bc_bo[:, :])
    ld_bc2 = p.dma("sync", "LDC8", l1b_s[:, :], bc_l1b[:, :])

    xn1fc = AR.alloc("xn1fc", [128, 8, CH], BF16)
    xnfc = AR.alloc("xnfc", [128, 8, CH], BF16)
    v_tc = AR.alloc("v_tc", [128, 4, HD], BF16)
    x1c = AR.alloc("x1c", [128, 2, D])
    qkc = AR.alloc("qkc", [128, 2, CH], BF16)
    qq_b = AR.alloc("qq_b", [128, 2, CH], BF16)
    qkk_b = AR.alloc("qkk_b", [128, 2, CH], BF16)
    lq_b = AR.alloc("lq_b", [128, 2, CH], BF16)
    hd8 = AR.alloc("hd8", [128, 2, CH])
    rt8 = AR.alloc("rt8", [128, 2, CH])
    cs8 = AR.alloc("cs8", [128, 2, CH])
    att_sb = AR.alloc("att_sb", [128, 4, 128], BF16)
    u_sb = AR.alloc("u_sb", [128, 16, CH], BF16)
    gate_b = AR.alloc("gate_b", [128, 512], BF16)
    g1 = AR.alloc("g1", [128, D])
    attf = AR.alloc("attf", [128, 64])
    o1 = rt8.rearrange("p a b -> p (a b)")
    s_c = cs8.rearrange("p a b -> p (a b)")

    p.op("vector", "VE", lambda e: e.memset(att_sb[:, :, :], 0.0))
    p.wait("vector", "LDC8", ld_bc2)
    p.wait("tensor", "GW7", gw_wg1)
    p.wait("tensor", "GW6", gw_wo)

    p.wait("scalar", "TE")
    p.wait("scalar", "VE")
    p.wait("scalar", "SV")
    p.wait("scalar", "SLK")
    p.wait("scalar", "SQK")
    p.wait("sync", "TE")
    p.wait("sync", "VE")
    sto_all = []
    x1_used = {}
    te_chunk_end = 0
    ve_comb_end = 0
    for c in range(NCH):
        col = slice(CH * c, CH * (c + 1))
        if c > 0:
            p.wait("scalar", "TE", te_chunk_end)
            p.wait("sync", "TE", te_chunk_end)
            p.wait("sync", "VE", ve_comb_end)
        for dk in range(8):
            p.tdma("scalar", "TL4", xn1fc[:, dk, :], sp_ln1[col, 128 * dk:128 * (dk + 1)])
        tld_c1 = p.cnt["TL4"]
        for dk in range(8):
            p.tdma("scalar", "TL4", xnfc[:, dk, :], sp_ln[col, 128 * dk:128 * (dk + 1)])
        tld_c2 = p.cnt["TL4"]
        p.wait("sync", "SV")
        p.wait("sync", "SQK")
        for tt in range(CT):
            it = c * CT + tt
            p.dma("sync", "L8", v_tc[:, tt, :], sp_v[128 * it:128 * (it + 1), :])
        p.dma("sync", "L8", qkc[:, 0, :], sp_qk[0:128, col])
        p.dma("sync", "L8", qkc[:, 1, :], sp_qk[128:256, col])
        p.dma("sync", "L8", cs8[:, 0, :], cos_h[:, col])
        ld_cs = p.dma("sync", "L8", cs8[:, 1, :], sin_h[:, col])

        if c > 0:
            p.wait("vector", "TE", te_chunk_end)
        p.wait("vector", "L8", ld_cs)
        for hsel, dst in ((0, qq_b), (2, qkk_b), (1, lq_b)):
            for cc in range(2):
                p.op("vector", "VE", lambda e, cc=cc, hsel=hsel: e.tensor_scalar(
                    hd8[:, cc, :], qkc[:, cc, :], gamma_c(hsel, cc), beta_c(hsel, cc), ALU.mult, ALU.add))
            p.op("vector", "VE", lambda e: e.tensor_tensor(rt8[:, 0, :], hd8[:, 0, :], cs8[:, 0, :], ALU.mult))
            p.op("vector", "VE", lambda e: e.tensor_tensor(rt8[:, 1, :], hd8[:, 1, :], cs8[:, 1, :], ALU.mult))
            p.op("vector", "VE", lambda e, dst=dst: e.tensor_tensor(dst[:, 0, :], rt8[:, 0, :], rt8[:, 1, :], ALU.subtract))
            p.op("vector", "VE", lambda e: e.tensor_tensor(rt8[:, 0, :], hd8[:, 1, :], cs8[:, 0, :], ALU.mult))
            p.op("vector", "VE", lambda e: e.tensor_tensor(rt8[:, 1, :], hd8[:, 0, :], cs8[:, 1, :], ALU.mult))
            p.op("vector", "VE", lambda e, dst=dst: e.tensor_tensor(dst[:, 1, :], rt8[:, 0, :], rt8[:, 1, :], ALU.add))
        rope_done = p.cnt["VE"]
        p.wait("tensor", "VE", rope_done)

        for pr in range(CT):
            p.mm(PA[0:64, 0:64], [(qkk_b[:, k, 128 * pr:128 * pr + 64],
                                   qq_b[:, k, 128 * pr:128 * pr + 64]) for k in range(2)])
            tv = p.mm(PA[64:128, 64:128], [(qkk_b[:, k, 128 * pr + 64:128 * (pr + 1)],
                                            qq_b[:, k, 128 * pr + 64:128 * (pr + 1)]) for k in range(2)],
                      tile_position=(0, 64))
            p.wait("vector", "TE", tv)
            sl = pr
            p.op("vector", "VE", lambda e, sl=sl: e.scalar_tensor_tensor(
                attf[0:64, :], PA[0:64, 0:64], 1.0 / G, S["sb_pos_bias"][0:64, :], ALU.mult, ALU.add))
            vv = p.op("vector", "VE", lambda e, sl=sl: e.scalar_tensor_tensor(
                attf[64:128, :], PA[64:128, 64:128], 1.0 / G, S["sb_pos_bias"][64:128, :], ALU.mult, ALU.add))
            p.wait("scalar", "VE", vv)
            p.op("scalar", "SE", lambda e: e.activation(attf[:, :], attf[:, :], AF.Relu))
            sv = p.op("scalar", "SE", lambda e, sl=sl: e.activation(attf[:, :], attf[:, :], AF.Square))
            p.wait("vector", "SE", sv)
            vv = p.op("vector", "VE", lambda e, sl=sl: e.tensor_copy(att_sb[0:64, sl, 0:64], attf[0:64, :]))
            vv = p.op("vector", "VE", lambda e, sl=sl: e.tensor_copy(att_sb[64:128, sl, 64:128], attf[64:128, :]))
            p.wait("tensor", "VE", vv)
        p.wait("tensor", "L8", ld_cs)
        p.wait("tensor", "TL4", tld_c2)
        u_evac = {}
        for m in range(16):
            bank = m % 2
            gbank = 2 + (m % 2)
            if m >= 2:
                p.wait("tensor", "VE", u_evac[m - 2])
            for pr in range(CT):
                p.mm(PS[:, bank, 128 * pr:128 * (pr + 1)],
                     [(v_tc[:, pr, 128 * m:128 * (m + 1)], att_sb[:, pr, :])] +
                     [(linkv_b[:, k, 128 * m:128 * (m + 1)], lq_b[:, k, 128 * pr:128 * (pr + 1)])
                      for k in range(2)])
            tvqo = p.cnt["TE"]
            tvg = p.mm(PS[:, gbank, :],
                       [(wgT1_s[:, k, 128 * m:128 * (m + 1)], xn1fc[:, k, :]) for k in range(8)])
            p.wait("scalar", "TE", tvg)
            if m >= 1:
                p.wait("scalar", "VE", u_evac[m - 1])
            sv = p.op("scalar", "SE", lambda e, gbank=gbank, m=m: e.activation(
                gate_b[:, :], PS[:, gbank, :], AF.Silu, bias=bg_c(m)))
            p.wait("vector", "SE", sv)
            p.wait("vector", "TE", tvg)
            vv = p.op("vector", "VE", lambda e, m=m, bank=bank: e.tensor_tensor(
                u_sb[:, m, :], gate_b[:, :], PS[:, bank, :], ALU.mult))
            u_evac[m] = vv
        p.wait("tensor", "VE", u_evac[15])
        p.wait("tensor", "TL4", tld_c2)
        for tt in range(CT):
            it = c * CT + tt
            rows = slice(128 * it, 128 * (it + 1))
            if it >= 2:
                p.wait("sync", "VE", x1_used[it - 2])
            ldx1 = p.dma("sync", "LX1C%d" % (tt % 2), x1c[:, tt % 2, :], sp_x1[rows, :])
            tv = None
            for nh in range(2):
                tv = p.mm(PS[:, 4 + nh, :],
                          [(u_sb[:, k, 128 * tt:128 * (tt + 1)],
                            woT_s[:, k, 512 * nh:512 * (nh + 1)]) for k in range(16)])
            p.wait("vector", "TE", tv)
            p.op("vector", "VE", lambda e: e.scalar_tensor_tensor(
                o1[:, 0:512], PS[:, 4, :], 1.0, bo_s[:, 0:512], ALU.mult, ALU.add))
            vv = p.op("vector", "VE", lambda e: e.scalar_tensor_tensor(
                o1[:, 512:1024], PS[:, 5, :], 1.0, bo_s[:, 512:1024], ALU.mult, ALU.add))
            p.wait("tensor", "VE", vv)
            if sto_all:
                p.wait("scalar", "STO", sto_all[-1])
                p.wait("vector", "STO", sto_all[-1])
            emit_ln(o1[:, :], g1[:, :])
            p.op("vector", "VE", lambda e: e.tensor_tensor(g1[:, :], g1[:, :], ln2w_s[:, :], ALU.mult))
            p.op("vector", "VE", lambda e: e.tensor_tensor(s_c[:, :], g1[:, :], ln2b_s[:, :], ALU.add))
            tv = None
            for nh in range(2):
                tv = p.mm(PS[:, 0 + nh, :],
                          [(xnfc[:, k, 128 * tt:128 * (tt + 1)],
                            wtern[:, k, 512 * nh:512 * (nh + 1)]) for k in range(8)])
            p.wait("vector", "TE", tv)
            if sto_all:
                p.wait("vector", "STO", sto_all[-1])
            p.op("vector", "VE", lambda e: e.tensor_scalar(g1[:, 0:512], PS[:, 0, :], alpha_c, None, ALU.mult))
            p.op("vector", "VE", lambda e: e.tensor_scalar(g1[:, 512:1024], PS[:, 1, :], alpha_c, None, ALU.mult))
            vv = p.op("vector", "VE", lambda e: e.tensor_tensor(g1[:, :], g1[:, :], l1b_s[:, :], ALU.add))
            p.wait("tensor", "VE", vv)
            p.wait("scalar", "VE", vv)
            sv = p.op("scalar", "SE", lambda e: e.activation(g1[:, :], g1[:, :], AF.Silu))
            p.wait("vector", "SE", sv)
            p.op("vector", "VE", lambda e: e.tensor_tensor(g1[:, :], g1[:, :], s_c[:, :], ALU.mult))
            p.wait("vector", "LX1C%d" % (tt % 2), ldx1)
            vv = p.op("vector", "VE", lambda e, tt=tt: e.tensor_tensor(g1[:, :], g1[:, :], x1c[:, tt % 2, :], ALU.add))
            x1_used[it] = vv
            p.wait("sync", "VE", vv)
            sto_all.append(p.dma("sync", "STO", out_h[rows, :], g1[:, :]))
        te_chunk_end = p.cnt["TE"]
        ve_comb_end = p.cnt["VE"]
    p.wait("sync", "STO", sto_all[-1])

    emit(nc, p)
    return nc, io


def emit(nc, p):
    with nc.Block() as blk:
        @blk.sync
        def _(e):
            for fn in p.ops["sync"]:
                fn(e)

        @blk.scalar
        def _(e):
            for fn in p.ops["scalar"]:
                fn(e)

        @blk.vector
        def _(e):
            for fn in p.ops["vector"]:
                fn(e)

        @blk.tensor
        def _(e):
            for fn in p.ops["tensor"]:
                fn(e)

        @blk.gpsimd
        def _(e):
            for fn in p.ops["gpsimd"]:
                fn(e)


def host_inputs(inputs, core):
    i = inputs
    d = {}
    d["x"] = i["L0_attn_x"][core]
    d["w_inT"] = i["L0_attn_W_in"].T
    d["w_xdtT"] = np.concatenate([i["L0_attn_W_x"].T, i["L0_attn_W_dt"].T], axis=1)
    d["d_mat"] = i["L0_attn_D_mat"]
    d["w_outT"] = i["L0_attn_W_out"].T
    d["gate_wT"] = i["L0_ffn_gate_w"].T
    d["wgT_all"] = i["L0_ffn_w_gate"].transpose(0, 2, 1)
    d["wuT_all"] = i["L0_ffn_w_up"].transpose(0, 2, 1)
    d["wdT_all"] = i["L0_ffn_w_down"].transpose(0, 2, 1)
    d["wqkT"] = i["L1_attn_Wqk"].T
    d["wvT"] = i["L1_attn_Wv"].T
    d["wgT1"] = i["L1_attn_Wg"].T
    d["woT"] = i["L1_attn_Wo"].T
    d["l1wT"] = i["L1_ffn_weight"].T

    cs = np.zeros((128, 64), np.float32)
    cs[:, 0:16] = i["L0_attn_b_in"].reshape(16, 128).T
    cs[:, 16:32] = i["L0_attn_conv_b"].reshape(16, 128).T
    cs[:, 32:34] = i["L1_attn_bqk"].reshape(2, 128).T
    cs[:, 34:50] = i["L1_attn_bg"].reshape(16, 128).T
    cs[:, 50:58] = i["L1_attn_gamma"].reshape(4, 2, 128).transpose(2, 0, 1).reshape(128, 8)
    d["cst_small"] = cs
    cs2 = np.zeros((128, 64), np.float32)
    cs2[:, 0:8] = i["L1_attn_beta"].reshape(4, 2, 128).transpose(2, 0, 1).reshape(128, 8)
    cs2[:, 8:9] = np.float32(np.asarray(i["L1_ffn_alpha"]).reshape(1)[0])
    cs2[:, 9:10] = 1.0
    cs2[:, 10:42] = np.arange(T, dtype=np.float32).reshape(NT, 128).T
    cs2[:, 42] = np.arange(128, dtype=np.float32) * T
    cs2[:, 43:51] = (np.arange(NE, dtype=np.float32) * T + KSEL - 1)[None, :]
    d["cst_small2"] = cs2
    d["ones128b"] = np.ones((128, 128), np.float32).astype(BF)
    d["ltonesb"] = np.triu(np.ones((128, 128), np.float32), 1).astype(BF)
    d["convw_fm"] = i["L0_attn_conv_w"][:, 0, :].reshape(16, 128, 4).transpose(1, 0, 2).reshape(128, 64)
    c64 = np.zeros((64, 8), np.float32)
    c64[:, 0] = i["L0_attn_b_x"]
    c64[:, 1] = i["L0_attn_b_dt"]
    c64[:, 2] = i["L0_attn_A_log"]
    d["cst64"] = c64
    d["bc_l0"] = np.tile(i["L0_attn_b_out"][None, :], (128, 1))
    d["bc_ln1w"] = np.tile(i["L1_attn_ln1_w"][None, :], (128, 1))
    d["bc_ln1b"] = np.tile(i["L1_attn_ln1_b"][None, :], (128, 1))
    d["bc_ln2w"] = np.tile(i["L1_attn_ln2_w"][None, :], (128, 1))
    d["bc_ln2b"] = np.tile(i["L1_attn_ln2_b"][None, :], (128, 1))
    d["bc_bo"] = np.tile(i["L1_attn_bo"][None, :], (128, 1))
    d["bc_l1b"] = np.tile(i["L1_ffn_bias"][None, :], (128, 1))
    d["bc_bv"] = np.tile(i["L1_attn_bv"][None, :], (128, 1))
    idx = np.arange(G)
    dist = np.clip(np.abs(idx[:, None] - idx[None, :]), 0, 32)
    pb = i["L1_attn_pos_emb"][dist].astype(np.float32)
    d["pos_bias"] = np.concatenate([pb, pb], axis=0)
    d["iden_b"] = np.eye(128, dtype=np.float32).astype(BF)
    half = 128
    inv_freq = 10000.0 ** (-np.arange(half, dtype=np.float32) / half)
    sinus = np.arange(T, dtype=np.float32)[None, :] * inv_freq[:, None]
    d["cos_fm"] = np.cos(sinus)
    d["sin_fm"] = np.sin(sinus)
    for k in list(d):
        a = np.asarray(d[k])
        if a.dtype == np.float64:
            a = a.astype(np.float32)
        d[k] = np.ascontiguousarray(a)
    return d


def kernel(**inputs):
    nc, io = build(debug=False)
    in_maps = [host_inputs(inputs, c) for c in range(8)]
    res = run_bass_kernel_spmd(nc, in_maps, core_ids=list(range(8)))
    return np.concatenate([r["out"][None] for r in res.results], axis=0)
